# version 11
# baseline (speedup 1.0000x reference)
import contextlib
import numpy as np
import ml_dtypes
import concourse.bass as bass
import concourse.mybir as mybir
from concourse.bass_utils import run_bass_kernel_spmd

F32 = mybir.dt.float32
BF16 = mybir.dt.bfloat16
F16 = mybir.dt.float16
U8 = mybir.dt.uint8
I32 = mybir.dt.int32
AF = mybir.ActivationFunctionType
ALU = mybir.AluOpType

H = 4096
DSSM = 8192
NH = 64
CONV = 8448
PROJ = 16704
KTAPS = 4
EPS = 1e-5
L = 128
NDT = H // 128
NZT = DSSM // 128
NCT = CONV // 128


class Sem:
    def __init__(self, h):
        self.h = h
        self.n = 0
    def inc(self, instr, k):
        instr.then_inc(self.h, k)
        self.n += k
        return self.n


def ts(i, n=128):
    return slice(i * n, (i + 1) * n)


def build_program(T_MAIN=2048, debug_outs=(), n_cc=8):
    NCH = T_MAIN // L
    NTT = T_MAIN // 128
    TP = T_MAIN + 3
    TPAD = ((TP + 15) // 16) * 16
    NGRP = 8
    GH = NH // NGRP

    CHUNKS = []
    off = 0
    while TP - off > 3:
        sz = min(512, TP - 3 - off)
        CHUNKS.append((off, sz))
        off += sz
    CHUNKS.append((off, 3))
    NCHK = len(CHUNKS)

    nc = bass.Bass("TRN2", target_bir_lowering=False, debug=False)

    x_main = nc.dram_tensor("x_main", [T_MAIN, H], F32, kind="ExternalInput").ap()
    x_pre = nc.dram_tensor("x_pre", [3, H], F32, kind="ExternalInput").ap()
    W_in = nc.dram_tensor("W_in", [H, PROJ], F32, kind="ExternalInput").ap()
    W_out = nc.dram_tensor("W_out", [DSSM, H], F32, kind="ExternalInput").ap()
    packedT = nc.dram_tensor("packedT", [2048, H], U8, kind="ExternalInput").ap()
    scalesE = nc.dram_tensor("scalesE", [2048, H], F16, kind="ExternalInput").ap()
    conv_win = nc.dram_tensor("conv_win", [CONV, KTAPS], F32, kind="ExternalInput").ap()
    hvecs = nc.dram_tensor("hvecs", [NH, 4], F32, kind="ExternalInput").ap()
    nw_in = nc.dram_tensor("nw_in", [DSSM, 1], F32, kind="ExternalInput").ap()
    flags_in = nc.dram_tensor("flags_in", [1, 2], F32, kind="ExternalInput").ap()
    triu_in = nc.dram_tensor("triu_in", [128, 128], BF16, kind="ExternalInput").ap()
    ident_in = nc.dram_tensor("ident_in", [128, 128], BF16, kind="ExternalInput").ap()
    identf_in = nc.dram_tensor("identf_in", [128, 128], F32, kind="ExternalInput").ap()
    out_t = nc.dram_tensor("out", [T_MAIN, H], F32, kind="ExternalOutput").ap()

    wdeq_s = nc.dram_tensor("wdeq_s", [H, H], BF16).ap()
    ln1_s = nc.dram_tensor("ln1_s", [TPAD, H], BF16).ap()
    x1_s = nc.dram_tensor("x1_s", [TP, H], F32).ap()
    ln2_s = nc.dram_tensor("ln2_s", [TPAD, H], BF16).ap()
    z_s = nc.dram_tensor("z_s", [DSSM, T_MAIN], BF16).ap()
    xf_s = nc.dram_tensor("xf_s", [DSSM, T_MAIN], BF16).ap()
    yraw_s = nc.dram_tensor("yraw_s", [DSSM, T_MAIN], BF16).ap()
    yg_s = nc.dram_tensor("yg_s", [DSSM, T_MAIN], BF16).ap()
    x2_s = nc.dram_tensor("x2_s", [T_MAIN, H], F32).ap()
    TPAD3 = ((T_MAIN + 15) // 16) * 16
    ln3_s = nc.dram_tensor("ln3_s", [TPAD3, H], BF16).ap()
    wout_s = nc.dram_tensor("wout_s", [DSSM, H], BF16).ap()
    cc_in = nc.dram_tensor("cc_in", [128, DSSM], F32).ap()
    cc_out = nc.dram_tensor("cc_out", [128, DSSM], F32).ap()
    bncA = nc.dram_tensor("bncA", [NH, T_MAIN], BF16).ap()
    bncB = nc.dram_tensor("bncB", [NH, T_MAIN], BF16).ap()
    bncC = nc.dram_tensor("bncC", [NH, NCH], F32).ap()

    dbg_map = {"wdeq": wdeq_s, "ln1": ln1_s, "x1": x1_s, "ln2": ln2_s, "z": z_s,
               "xf": xf_s, "yraw": yraw_s, "yg": yg_s, "x2": x2_s, "ln3": ln3_s}
    dbg = {}
    for name in debug_outs:
        src = dbg_map[name]
        dbg[name] = nc.dram_tensor("dbg_" + name, list(src.tensor.shape), src.dtype,
                                   kind="ExternalOutput").ap()

    es = contextlib.ExitStack()
    sbuf = lambda n, shp, dt: es.enter_context(nc.sbuf_tensor(n, shp, dt))
    SEM = lambda n: Sem(es.enter_context(nc.semaphore(n)))

    triu = sbuf("triu", [128, 128], BF16)
    ident = sbuf("ident", [128, 128], BF16)
    identf = sbuf("identf", [128, 128], F32)
    onescol = sbuf("onescol", [1, 128], BF16)
    ones33 = sbuf("ones33", [33, 128], BF16)
    onescolf = sbuf("onescolf", [1, 128], F32)
    onesfull = sbuf("onesfull", [128, 128], BF16)
    fl = sbuf("fl", [128, 2], F32)
    hv = sbuf("hv", [NH, 4], F32)
    cw2 = sbuf("cw2", [128, NCT * KTAPS], F32)
    nw2 = sbuf("nw2", [128, NZT], F32)
    epscol = sbuf("epscol", [128, 1], F32)
    onecolf2 = sbuf("onecolf2", [128, 1], F32)
    rstd3p = sbuf("rstd3p", [128, NCH], F32)

    s_ld = SEM("s_ld")
    s_ldg = SEM("s_ldg")
    s_tr = SEM("s_tr")
    s_pe = SEM("s_pe")
    s_dve = SEM("s_dve")
    s_act = SEM("s_act")
    s_gp = SEM("s_gp")
    s_cc = SEM("s_cc")

    blk = es.enter_context(nc.Block())

    def phase_barrier():
        c = dict(ld=s_ld.n, ldg=s_ldg.n, tr=s_tr.n, pe=s_pe.n, dve=s_dve.n,
                 act=s_act.n, gp=s_gp.n)
        for sec in (blk.sync, blk.gpsimd, blk.vector, blk.scalar, blk.tensor):
            @sec
            def _(e):
                e.wait_ge(s_ld.h, c["ld"]); e.wait_ge(s_ldg.h, c["ldg"])
                e.wait_ge(s_tr.h, c["tr"]); e.wait_ge(s_pe.h, c["pe"])
                e.wait_ge(s_dve.h, c["dve"]); e.wait_ge(s_act.h, c["act"])
                e.wait_ge(s_gp.h, c["gp"])

    @blk.sync
    def _(e):
        s_ld.inc(e.dma_start(out=triu[:], in_=triu_in), 16)
        s_ld.inc(e.dma_start(out=ident[:], in_=ident_in), 16)
        s_ld.inc(e.dma_start(out=identf[:], in_=identf_in), 16)
        s_ld.inc(e.dma_start(out=hv[:], in_=hvecs), 16)
        s_ld.inc(e.dma_start(out=fl[:], in_=flags_in.squeeze(0).partition_broadcast(128)), 16)
        s_ld.inc(e.dma_start(out=cw2[:].rearrange("p (a k) -> p a k", k=KTAPS), in_=conv_win.rearrange("(a p) k -> p a k", p=128)), 16)
        s_ld.inc(e.dma_start(out=nw2[:].unsqueeze(2), in_=nw_in.rearrange("(a p) b -> p a b", p=128), allow_slow_non_contiguous=True), 16)

    @blk.vector
    def _(e):
        e.wait_ge(s_ld.h, s_ld.n)
        e.memset(onescol[:], 1.0)
        e.memset(ones33[:], 1.0)
        e.memset(onescolf[:], 1.0)
        e.memset(onesfull[:], 1.0)
        e.memset(epscol[:], float(EPS))
        e.memset(onecolf2[:], 1.0)
        e.drain()
        s_dve.inc(e.nop(), 1)
    phase_barrier()

    with contextlib.ExitStack() as ph:
        psb = lambda n, shp, dt: ph.enter_context(nc.sbuf_tensor(n, shp, dt))
        pk = [psb(f"pk{i}", [128, H], U8) for i in range(2)]
        sc = [psb(f"sc{i}", [128, H], F16) for i in range(2)]
        lohi = psb("lohi", [128, 2 * H], U8)
        tg0 = psb("tg0", [128, 2 * H], BF16)
        wdq = [psb(f"wdq{i}", [128, 2 * H], BF16) for i in range(2)]
        wofp = [psb(f"wofp{i}", [128, 2 * H], F32) for i in range(2)]
        wosp = [psb(f"wosp{i}", [128, 2 * H], BF16) for i in range(2)]
        p0ld = [SEM("p0lda"), SEM("p0ldb")]
        p0st = [SEM("p0sta"), SEM("p0stb")]
        wls = [SEM("p0wla"), SEM("p0wlb")]
        wss = [SEM("p0wsa"), SEM("p0wsb")]
        gp2 = SEM("p0gp2")
        dv2 = SEM("p0dv2")
        dve0, gp0 = s_dve.n, s_gp.n

        @blk.sync
        def _(e):
            for i in range(16):
                if i >= 2:
                    e.wait_ge(s_gp.h, gp0 + (i - 1))
                p0ld[i % 2].inc(e.dma_start(out=pk[i % 2][:], in_=packedT[ts(i), :]), 16)
                p0ld[i % 2].inc(e.dma_start(out=sc[i % 2][:], in_=scalesE[ts(i), :]), 16)

        @blk.vector
        def _(e):
            for i in range(16):
                e.wait_ge(p0ld[i % 2].h, 32 * (i // 2 + 1))
                if i >= 2:
                    e.wait_ge(p0st[i % 2].h, 32 * (i // 2))
                p, w, sci = pk[i % 2], wdq[i % 2], sc[i % 2]
                e.tensor_scalar(lohi[:, 0:H], p[:], 15, None, ALU.bitwise_and, ALU.bypass)
                e.tensor_scalar(lohi[:, H:2 * H].bitcast(I32), p[:].bitcast(I32), 4, 0x0F0F0F0F,
                                ALU.logical_shift_right, ALU.bitwise_and)
                e.drain()
                e.tensor_scalar(tg0[:, :], lohi[:, :], -8.0, None, ALU.add, ALU.bypass)
                e.drain()
                e.tensor_tensor(w[:, 0:H], tg0[:, 0:H], sci[:], ALU.mult)
                e.tensor_tensor(w[:, H:2 * H], tg0[:, H:2 * H], sci[:], ALU.mult)
                e.drain()
                s_gp.inc(e.nop(), 1)

        def presc(e, i, sem):
            e.wait_ge(wls[i % 2].h, 16 * (i // 2 + 1))
            if i >= 2:
                e.wait_ge(wss[i % 2].h, 16 * (i // 2))
            nwrep = nw2[:, i * 2:(i + 1) * 2].unsqueeze(2).broadcast_to([128, 2, H])
            e.tensor_tensor(wosp[i % 2][:, :].rearrange("p (a n) -> p a n", a=2),
                            wofp[i % 2][:, :].rearrange("p (a n) -> p a n", a=2),
                            nwrep, ALU.mult)
            e.drain()
            sem.inc(e.nop(), 1)

        @blk.vector
        def _(e):
            for i in range(0, 32, 2):
                presc(e, i, dv2)

        @blk.gpsimd
        def _(e):
            def load(i):
                wls[i % 2].inc(e.dma_start(
                    out=wofp[i % 2][:, :].rearrange("p (a n) -> p a n", a=2),
                    in_=W_out[i * 256:(i + 1) * 256, :].rearrange("(a p) n -> p a n", p=128)), 16)
            load(0)
            load(1)
            for i in range(32):
                if i % 2 == 1:
                    presc(e, i, gp2)
                else:
                    e.wait_ge(dv2.h, i // 2 + 1)
                if i + 2 < 32:
                    load(i + 2)

        @blk.scalar
        def _(e):
            for i in range(16):
                e.wait_ge(s_gp.h, gp0 + (i + 1))
                p0st[i % 2].inc(e.dma_start(out=wdeq_s[ts(i), :], in_=wdq[i % 2][:, 0:H]), 16)
                p0st[i % 2].inc(e.dma_start(out=wdeq_s[ts(16 + i), :], in_=wdq[i % 2][:, H:2 * H]), 16)

        @blk.sync
        def _(e):
            for i in range(32):
                if i % 2 == 0:
                    e.wait_ge(dv2.h, i // 2 + 1)
                else:
                    e.wait_ge(gp2.h, i // 2 + 1)
                wss[i % 2].inc(e.dma_start(
                    out=wout_s[i * 256:(i + 1) * 256, :].rearrange("(a p) n -> p a n", p=128),
                    in_=wosp[i % 2][:, :].rearrange("p (a n) -> p a n", a=2)), 16)

        @blk.scalar
        def _(e):
            e.wait_ge(p0st[0].h, p0st[0].n)
            e.wait_ge(p0st[1].h, p0st[1].n)
            s_act.inc(e.nop(), 1)

        @blk.sync
        def _(e):
            e.wait_ge(wss[0].h, wss[0].n)
            e.wait_ge(wss[1].h, wss[1].n)
            s_ld.inc(e.nop(), 16)
    phase_barrier()

    def resid_load2(e, sem, buf, r0, r1, cols, pre_ap, main_ap):
        w = cols.stop - cols.start
        if pre_ap is not None:
            if r0 < 3:
                sem.inc(e.dma_start(out=buf[0:3 - r0, 0:w], in_=pre_ap[r0:3, cols]), 16)
            if r1 > 3:
                a = max(r0, 3)
                sem.inc(e.dma_start(out=buf[a - r0:r1 - r0, 0:w], in_=main_ap[a - 3:r1 - 3, cols]), 16)
        else:
            sem.inc(e.dma_start(out=buf[0:r1 - r0, 0:w], in_=main_ap[r0:r1, cols]), 16)

    def resid_load(e, buf, r0, r1, cols, pre_ap, main_ap):
        w = cols.stop - cols.start
        if pre_ap is not None:
            if r0 < 3:
                s_ld.inc(e.dma_start(out=buf[0:3 - r0, 0:w], in_=pre_ap[r0:3, cols]), 16)
            if r1 > 3:
                a = max(r0, 3)
                s_ld.inc(e.dma_start(out=buf[a - r0:r1 - r0, 0:w], in_=main_ap[a - 3:r1 - 3, cols]), 16)
        else:
            s_ld.inc(e.dma_start(out=buf[0:r1 - r0, 0:w], in_=main_ap[r0:r1, cols]), 16)

    def resid_nld(r0, r1, has_pre):
        return ((1 if r0 < 3 else 0) + (1 if r1 > 3 else 0)) if has_pre else 1

    def emit_ln(tag, src_pre, src_main, nrows, dst):
        ntile = (nrows + 127) // 128
        hp = src_pre is not None
        with contextlib.ExitStack() as ph:
            psb = lambda n, shp, dt: ph.enter_context(nc.sbuf_tensor(n, shp, dt))
            xs = [psb(f"{tag}x{i}", [128, H], F32) for i in range(2)]
            disc = psb(f"{tag}d", [128, H], BF16)
            disc2 = psb(f"{tag}d2", [128, H], BF16)
            zpad = psb(f"{tag}z", [16, H], BF16)
            lnt = [psb(f"{tag}l{i}", [128, H], BF16) for i in range(2)]
            cols = psb(f"{tag}c", [128, 8 * ntile], F32)
            lda = [SEM(tag + "la"), SEM(tag + "lb")]
            ldb = [SEM(tag + "ma"), SEM(tag + "mb")]
            stp = [SEM(tag + "sa"), SEM(tag + "sb")]
            dve0, act0 = s_dve.n, s_act.n

            npt = [resid_nld(i * 128, min((i + 1) * 128, nrows), hp) for i in range(ntile)]

            def cum_par(npt_list, i):
                return 16 * sum(npt_list[j] for j in range(i % 2, i + 1, 2))

            @blk.sync
            def _(e):
                for i in range(ntile):
                    if i >= 2:
                        e.wait_ge(s_act.h, act0 + (i - 1))
                    resid_load2(e, lda[i % 2], xs[i % 2], i * 128,
                                min((i + 1) * 128, nrows), slice(0, H), src_pre, src_main)

            @blk.vector
            def _(e):
                e.memset(cols[:], 0.0)
                e.memset(zpad[:], 0.0)
                e.drain()
                for i in range(ntile):
                    r = min((i + 1) * 128, nrows) - i * 128
                    e.wait_ge(lda[i % 2].h, cum_par(npt, i))
                    if i >= 2:
                        e.wait_ge(s_act.h, act0 + i - 1)
                    e.tensor_scalar(disc[0:r, :], xs[i % 2][0:r, :], 1.0, None, ALU.mult, ALU.add,
                                    accum_out=cols[0:r, i:i + 1])
                    e.drain()
                    s_dve.inc(e.nop(), 1)

            @blk.scalar
            def _(e):
                for i in range(ntile):
                    r = min((i + 1) * 128, nrows) - i * 128
                    e.wait_ge(s_dve.h, dve0 + i + 1)
                    e.activation(disc2[0:r, :], xs[i % 2][0:r, :], AF.Square,
                                 accum_out=cols[0:r, ntile + i:ntile + i + 1])
                    e.drain()
                    s_act.inc(e.nop(), 1)
                e.activation(cols[:, 2 * ntile:3 * ntile], cols[:, 0:ntile], AF.Copy, scale=1.0 / H)
                e.drain()
                e.activation(cols[:, 3 * ntile:4 * ntile], cols[:, 2 * ntile:3 * ntile], AF.Square)
                e.drain()
                s_act.inc(e.nop(), 1)

            @blk.vector
            def _(e):
                e.wait_ge(s_act.h, act0 + ntile + 1)
                e.tensor_scalar(cols[:, 4 * ntile:5 * ntile], cols[:, ntile:2 * ntile], 1.0 / H,
                                None, ALU.mult, ALU.bypass)
                e.drain()
                e.tensor_tensor(cols[:, 5 * ntile:6 * ntile], cols[:, 4 * ntile:5 * ntile],
                                cols[:, 3 * ntile:4 * ntile], ALU.subtract)
                e.drain()
                s_dve.inc(e.nop(), 1)

            @blk.scalar
            def _(e):
                e.wait_ge(s_dve.h, dve0 + ntile + 1)
                e.activation(cols[:, 6 * ntile:7 * ntile], cols[:, 5 * ntile:6 * ntile],
                             AF.Ln, bias=epscol[:, 0:1])
                e.drain()
                e.activation(cols[:, 7 * ntile:8 * ntile], cols[:, 6 * ntile:7 * ntile],
                             AF.Exp, scale=-0.5)
                e.drain()
                s_act.inc(e.nop(), 1)

            dve1 = s_dve.n
            @blk.sync
            def _(e):
                for i in range(ntile):
                    if i >= 2:
                        e.wait_ge(s_dve.h, dve1 + (i - 1))
                    else:
                        e.wait_ge(s_act.h, act0 + ntile)
                    resid_load2(e, ldb[i % 2], xs[i % 2], i * 128,
                                min((i + 1) * 128, nrows), slice(0, H), src_pre, src_main)

            @blk.vector
            def _(e):
                e.wait_ge(s_act.h, act0 + ntile + 2)
                for i in range(ntile):
                    r = min((i + 1) * 128, nrows) - i * 128
                    e.wait_ge(ldb[i % 2].h, cum_par(npt, i))
                    if i >= 2:
                        e.wait_ge(stp[i % 2].h, 16 * (i // 2))
                    e.tensor_scalar(lnt[i % 2][0:r, :].rearrange("p (eo a) -> p eo a", eo=2),
                                    xs[i % 2][0:r, :].rearrange("p (a eo) -> p eo a", eo=2),
                                    cols[0:r, 2 * ntile + i:2 * ntile + i + 1],
                                    cols[0:r, 7 * ntile + i:7 * ntile + i + 1],
                                    ALU.subtract, ALU.mult)
                    e.drain()
                    s_dve.inc(e.nop(), 1)

            @blk.scalar
            def _(e):
                for i in range(ntile):
                    r0, r1 = i * 128, min((i + 1) * 128, nrows)
                    e.wait_ge(s_dve.h, dve1 + i + 1)
                    stp[i % 2].inc(e.dma_start(out=dst[r0:r1, :], in_=lnt[i % 2][0:r1 - r0, :]), 16)
                dstpad = dst.tensor.shape[0]
                if dstpad > nrows:
                    stp[0].inc(e.dma_start(out=dst[nrows:dstpad, :],
                                           in_=zpad[0:dstpad - nrows, :]), 16)

            @blk.scalar
            def _(e):
                e.wait_ge(stp[0].h, stp[0].n)
                e.wait_ge(stp[1].h, stp[1].n)
                s_act.inc(e.nop(), 1)

    def emit_ffn(tag, ln_src, ln_rows, resid_pre, resid_main, dst):
        ntile = (ln_rows + 127) // 128
        hp = resid_pre is not None
        qgrp = [list(range(g, min(g + 3, ntile))) for g in range(0, ntile, 3)]
        NQ = 4
        with contextlib.ExitStack() as ph:
            psb = lambda n, shp, dt: ph.enter_context(nc.sbuf_tensor(n, shp, dt))
            wq = [psb(f"{tag}wq{i}", [128, NDT * 1024], BF16) for i in range(2)]
            lnq = [psb(f"{tag}lq{i}", [128, NDT * 400], BF16) for i in range(2)]
            xres = [psb(f"{tag}xr{i}", [128, 1024], F32) for i in range(2)]
            ot = [psb(f"{tag}ot{i}", [128, 1024], F32) for i in range(2)]
            pss = [ph.enter_context(nc.psum_tensor(f"{tag}p{i}", [128, 1024], F32)) for i in range(2)]
            ldx = [SEM(tag + "xa"), SEM(tag + "xb")]
            trq = [SEM(tag + "ta"), SEM(tag + "tb")]
            wqs = [SEM(tag + "wqa"), SEM(tag + "wqb")]
            stp = [SEM(tag + "sa"), SEM(tag + "sb")]
            pe0, dve0 = s_pe.n, s_dve.n

            tl = [(nq, qi, t) for nq in range(NQ) for qi, q in enumerate(qgrp) for t in q]
            npt = [resid_nld(t * 128, min((t + 1) * 128, ln_rows), hp) for (_, _, t) in tl]

            def cum_par(i):
                return 16 * sum(npt[j] for j in range(i % 2, i + 1, 2))

            grps = [(nq, qi) for nq in range(NQ) for qi in range(len(qgrp))]
            gidx = {g: i for i, g in enumerate(grps)}
            tiles_thru = {}
            c = 0
            for g in grps:
                c += len(qgrp[g[1]])
                tiles_thru[g] = c

            def tr_par(gi):
                return 16 * NDT * (gi // 2 + 1)

            @blk.gpsimd
            def _(e):
                kglob = 0
                for nq in range(NQ):
                    if nq >= 2:
                        e.wait_ge(s_pe.h, pe0 + (nq - 1) * ntile)
                    for piece in range(4):
                        rows = slice(piece * 1024, (piece + 1) * 1024)
                        wqs[nq % 2].inc(e.dma_start(
                            out=wq[nq % 2][:, piece * 8 * 1024:(piece + 1) * 8 * 1024].rearrange(
                                "p (a n) -> p a n", a=8),
                            in_=wdeq_s[rows, ts(nq, 1024)].rearrange("(a p) n -> p a n", p=128)), 16)
                    for _i in range(ntile):
                        k = kglob
                        t = tl[k][2]
                        r0, r1 = t * 128, min((t + 1) * 128, ln_rows)
                        e.wait_ge(s_dve.h, dve0 + k + 1)
                        stp[k % 2].inc(e.dma_start(out=dst[r0:r1, ts(nq, 1024)],
                                                   in_=ot[k % 2][0:r1 - r0, :]), 16)
                        kglob += 1

            @blk.scalar
            def _(e):
                for gi, g in enumerate(grps):
                    if gi >= 2:
                        e.wait_ge(s_pe.h, pe0 + tiles_thru[grps[gi - 2]])
                    q = qgrp[g[1]]
                    r0 = q[0] * 128
                    rq = ((min((q[-1] + 1) * 128, ln_rows) - r0 + 15) // 16) * 16
                    for d in range(NDT):
                        trq[gi % 2].inc(e.dma_start_transpose(
                            lnq[gi % 2][:, d * 400:d * 400 + rq], ln_src[r0:r0 + rq, ts(d)]), 16)

            @blk.sync
            def _(e):
                for k, (nq, qi, t) in enumerate(tl):
                    if k >= 2:
                        e.wait_ge(s_dve.h, dve0 + k - 1)
                    resid_load2(e, ldx[k % 2], xres[k % 2], t * 128,
                                min((t + 1) * 128, ln_rows), ts(nq, 1024), resid_pre, resid_main)

            @blk.tensor
            def _(e):
                for k, (nq, qi, t) in enumerate(tl):
                    gi = gidx[(nq, qi)]
                    e.wait_ge(trq[gi % 2].h, tr_par(gi))
                    e.wait_ge(wqs[nq % 2].h, 16 * 4 * (nq // 2 + 1))
                    if k >= 2:
                        e.wait_ge(s_dve.h, dve0 + k - 1)
                    r0 = t * 128
                    m = min(128, ln_rows - r0)
                    toff = r0 - qgrp[qi][0] * 128
                    for d in range(NDT):
                        for c2 in range(2):
                            _mi = e.matmul(pss[k % 2][0:m, ts(c2, 512)],
                                           lnq[gi % 2][:, d * 400 + toff:d * 400 + toff + m],
                                           wq[nq % 2][:, d * 1024 + c2 * 512:d * 1024 + (c2 + 1) * 512],
                                           start=(d == 0), stop=(d == NDT - 1))
                    s_pe.inc(_mi, 1)

            @blk.vector
            def _(e):
                for k, (nq, qi, t) in enumerate(tl):
                    m = min(128, ln_rows - t * 128)
                    e.wait_ge(s_pe.h, pe0 + k + 1)
                    e.wait_ge(ldx[k % 2].h, cum_par(k))
                    if k >= 2:
                        e.wait_ge(stp[k % 2].h, 16 * (k // 2))
                    e.scalar_tensor_tensor(ot[k % 2][0:m, :], pss[k % 2][0:m, :], 0.5,
                                           xres[k % 2][0:m, :], ALU.mult, ALU.add)
                    e.drain()
                    s_dve.inc(e.nop(), 1)

            @blk.gpsimd
            def _(e):
                e.wait_ge(stp[0].h, stp[0].n)
                e.wait_ge(stp[1].h, stp[1].n)
                s_gp.inc(e.nop(), 1)

    emit_ln("l1", x_pre, x_main, TP, ln1_s)
    phase_barrier()
    emit_ffn("f1", ln1_s, TP, x_pre, x_main, x1_s)
    phase_barrier()
    emit_ln("l2", x1_s[0:3, :], x1_s[3:TP, :], TP, ln2_s)
    phase_barrier()

    W_in_p = W_in.rearrange("(blk p eo) n -> eo p blk n", blk=16, p=128, eo=2)
    jorder = [130] + list(range(64)) + list(range(64, 130))
    conv_chunks = [(o, min(512, T_MAIN - o)) for o in range(0, T_MAIN, 512)]
    NCV = len(conv_chunks)

    def pe_before_main(k):
        return 5 * 0 + NCHK * k + NCV * min(max(k - 66, 0), 65) + (NCV if k > 130 else 0)

    def pe_before_conv(b):
        kb = 66 + b if b <= 64 else 130
        return NCHK * (kb + 1) + NCV * b

    def act_before_main(k):
        return NCHK * k + NCV * min(max(k - 66, 0), 65)

    def act_before_conv(b):
        kb = 66 + b if b <= 64 else 130
        return NCHK * (kb + 1) + NCV * b

    mid_es = contextlib.ExitStack()
    Bt_r = mid_es.enter_context(nc.sbuf_tensor("Bt_r", [128, T_MAIN], BF16))
    Ct_r = mid_es.enter_context(nc.sbuf_tensor("Ct_r", [128, T_MAIN], BF16))
    dtraw = mid_es.enter_context(nc.sbuf_tensor("dtraw", [NH, TP], F32))
    with contextlib.ExitStack() as ph:
        psb = lambda n, shp, dt: ph.enter_context(nc.sbuf_tensor(n, shp, dt))
        ln2T = psb("ln2T", [128, NDT * TPAD], BF16)
        wi = [psb(f"wi{i}", [128, NDT * 128], BF16) for i in range(2)]
        stripes = [psb(f"strp{i}", [128, TP], BF16) for i in range(2)]
        zstr = [psb(f"zs{i}", [128, TP], BF16) for i in range(2)]
        xfb = [psb(f"xfb{i}", [128, 512], BF16) for i in range(2)]
        diag4 = [psb(f"dg{i}", [128, KTAPS * 128], BF16) for i in range(2)]
        pMain = ph.enter_context(nc.psum_tensor("pMain", [128, 2048], F32))
        pRag = ph.enter_context(nc.psum_tensor("pRag", [128, 512], F32))
        pCv = [ph.enter_context(nc.psum_tensor(f"pCv{i}", [128, 512], F32)) for i in range(2)]
        wis = [SEM("p2wia"), SEM("p2wib")]
        zst = [SEM("p2za"), SEM("p2zb")]
        xfst = [SEM("p2xa"), SEM("p2xb")]
        ld0, ldg0, tr0, pe0, dve0, act0 = s_ld.n, s_ldg.n, s_tr.n, s_pe.n, s_dve.n, s_act.n

        @blk.scalar
        def _(e):
            for d in range(NDT):
                s_tr.inc(e.dma_start_transpose(ln2T[:, d * TPAD:d * TPAD + TPAD],
                                               ln2_s[:, ts(d)]), 16)
        TR_LN2T = s_tr.n

        @blk.gpsimd
        def _(e):
            for k, j in enumerate(jorder):
                if k >= 2:
                    e.wait_ge(s_pe.h, pe0 + pe_before_main(k - 2) + NCHK)
                jw = 64 if j == 130 else 128
                for half in range(2):
                    a0 = half * 16
                    wis[k % 2].inc(e.dma_start(
                        out=wi[k % 2][:, a0 * 128:(a0 + 16) * 128].rearrange(
                            "p (a n) -> p a n", n=128)[:, :, 0:jw],
                        in_=W_in_p[half, :, :, j * 128:j * 128 + jw]), 16)

        @blk.vector
        def _(e):
            for b in range(NCT):
                if b >= 2:
                    e.wait_ge(s_pe.h, pe0 + pe_before_conv(b - 2) + NCV)
                for tap in range(KTAPS):
                    e.tensor_scalar(diag4[b % 2][:, ts(tap)], ident[:],
                                    cw2[:, b * KTAPS + tap:b * KTAPS + tap + 1], None,
                                    ALU.mult, ALU.bypass)
                e.drain()
                s_dve.inc(e.nop(), 1)

        @blk.tensor
        def _(e):
            def emit_conv(b):
                e.wait_ge(s_act.h, act0 + act_before_conv(b))
                e.wait_ge(s_dve.h, dve0 + b + 1)
                st_buf = stripes[(65 + b) % 2]
                for oi, (o, osz) in enumerate(conv_chunks):
                    gci = b * NCV + oi
                    if gci >= 2:
                        bb, oo = divmod(gci - 2, NCV)
                        e.wait_ge(s_act.h, act0 + act_before_conv(bb) + oo + 1)
                    for tap in range(KTAPS):
                        _mi = e.matmul(pCv[gci % 2][:, 0:osz], diag4[b % 2][:, ts(tap)],
                                       st_buf[:, o + tap:o + tap + osz],
                                       start=(tap == 0), stop=(tap == KTAPS - 1))
                    s_pe.inc(_mi, 1)

            for k, j in enumerate(jorder):
                jw = 64 if j == 130 else 128
                e.wait_ge(s_tr.h, TR_LN2T)
                e.wait_ge(wis[k % 2].h, 32 * (k // 2 + 1))
                for ci, (off, sz) in enumerate(CHUNKS):
                    if k >= 1:
                        e.wait_ge(s_act.h, act0 + act_before_main(k - 1) + ci + 1)
                    tgt = pRag[0:jw, 0:3] if sz == 3 else pMain[0:jw, off:off + sz]
                    for d in range(NDT):
                        _mi = e.matmul(tgt, wi[k % 2][:, d * 128:d * 128 + jw],
                                       ln2T[:, d * TPAD + off:d * TPAD + off + sz],
                                       start=(d == 0), stop=(d == NDT - 1))
                    s_pe.inc(_mi, 1)
                if 66 <= k <= 130:
                    emit_conv(k - 66)
            emit_conv(65)

        def Z_ST(k):
            return 16 * ((k - 1) // 2 + 1)

        def XF_ST(gci):
            return 16 * (gci // 2 + 1)

        @blk.scalar
        def _(e):
            def silu_evacs(b):
                ct = b
                for oi, (o, osz) in enumerate(conv_chunks):
                    gci = b * NCV + oi
                    e.wait_ge(s_pe.h, pe0 + pe_before_conv(b) + oi + 1)
                    if ct == 64:
                        dst_ap = Bt_r[:, o:o + osz]
                    elif ct == 65:
                        dst_ap = Ct_r[:, o:o + osz]
                    else:
                        dst_ap = xfb[gci % 2][:, 0:osz]
                        if gci >= 2:
                            e.wait_ge(xfst[gci % 2].h, XF_ST(gci - 2))
                    e.activation(dst_ap, pCv[gci % 2][:, 0:osz], AF.Silu)
                    e.drain()
                    s_act.inc(e.nop(), 1)

            for k, j in enumerate(jorder):
                jw = 64 if j == 130 else 128
                for ci, (off, sz) in enumerate(CHUNKS):
                    e.wait_ge(s_pe.h, pe0 + pe_before_main(k) + ci + 1)
                    src = pRag[0:jw, 0:3] if sz == 3 else pMain[0:jw, off:off + sz]
                    if j == 130:
                        e.activation(dtraw[:, off:off + sz], src, AF.Copy)
                    elif j < 64:
                        if ci == 0 and k >= 3:
                            e.wait_ge(zst[k % 2].h, Z_ST(k - 2))
                        e.activation(zstr[k % 2][:, off:off + sz], src, AF.Copy)
                    else:
                        if ci == 0 and k >= 67:
                            e.wait_ge(s_pe.h, pe0 + pe_before_conv(j - 64 - 2) + NCV)
                        e.activation(stripes[k % 2][:, off:off + sz], src, AF.Copy)
                    e.drain()
                    s_act.inc(e.nop(), 1)
                if 66 <= k <= 130:
                    silu_evacs(k - 66)
            silu_evacs(65)

        @blk.sync
        def _(e):
            for k, j in enumerate(jorder):
                if j < 64:
                    e.wait_ge(s_act.h, act0 + act_before_main(k) + NCHK)
                    zst[k % 2].inc(e.dma_start(out=z_s[ts(j), :], in_=zstr[k % 2][:, 3:TP]), 16)
            for b in range(64):
                for oi, (o, osz) in enumerate(conv_chunks):
                    gci = b * NCV + oi
                    e.wait_ge(s_act.h, act0 + act_before_conv(b) + oi + 1)
                    xfst[gci % 2].inc(e.dma_start(out=xf_s[ts(b), o:o + osz],
                                                  in_=xfb[gci % 2][:, 0:osz]), 16)
        @blk.sync
        def _(e):
            for p in range(2):
                e.wait_ge(zst[p].h, zst[p].n)
                e.wait_ge(xfst[p].h, xfst[p].n)
            s_ld.inc(e.nop(), 16)
    phase_barrier()


    ssd_es = contextlib.ExitStack()
    ssb = lambda n, shp, dt: ssd_es.enter_context(nc.sbuf_tensor(n, shp, dt))
    seed = ssb("seed", [128, DSSM], BF16)
    ssd2_es = contextlib.ExitStack()
    ssb2 = lambda n, shp, dt: ssd2_es.enter_context(nc.sbuf_tensor(n, shp, dt))
    lamb = ssb2("lamb", [128, NH * NCH], F32)
    wTp = ssb2("wTp", [128, NCH * 64], F32)
    colterm = ssb2("colterm", [128, NCH * 64], F32)
    logdAT = ssb2("logdAT", [128, NCH * 64], BF16)
    hslab = ssb2("hslab", [128, DSSM], F32)
    hbf = ssb2("hbf", [128, DSSM], BF16)
    DIall = ssb2("DIall", [128, NH * 128], BF16)

    with contextlib.ExitStack() as ph:
        psb = lambda n, shp, dt: ph.enter_context(nc.sbuf_tensor(n, shp, dt))
        dtv = psb("dtv", [NH, T_MAIN], F32)
        logdt = psb("logdt", [NH, T_MAIN], F32)
        logdA_b = psb("logdA_b", [NH, T_MAIN], BF16)
        cumhs = psb("cumhs", [NH, T_MAIN], F32)
        cumabs = psb("cumabs", [NH, T_MAIN], F32)
        w_hs = psb("w_hs", [NH, T_MAIN], F32)
        ta = psb("ta", [NH, T_MAIN], F32)
        tb = psb("tb", [NH, T_MAIN], F32)
        tc = psb("tcx", [NH, T_MAIN], F32)
        td = psb("td", [NH, T_MAIN], F32)
        tbf = psb("tbf", [NH, T_MAIN], BF16)
        ones64 = psb("ones64", [NH, L], F32)
        negA = psb("negA", [NH, 1], F32)
        lam64 = psb("lam64", [NH, NCH], F32)
        lamflat = psb("lamflat", [1, NH * NCH], F32)
        ldtT = psb("ldtT", [128, 64], F32)
        Drow = psb("Drow", [1, 64], F32)
        Dbc = psb("Dbc", [128, 64], F32)
        pTb = ph.enter_context(nc.psum_tensor("pTb", [128, 64], BF16))
        pTf = ph.enter_context(nc.psum_tensor("pTf", [128, 64], F32))
        pTf2 = ph.enter_context(nc.psum_tensor("pTf2", [128, 64], F32))
        pTlam = ph.enter_context(nc.psum_tensor("pTlam", [128, 512], F32))
        dve0, act0, gp0, pe0, ldg0 = s_dve.n, s_act.n, s_gp.n, s_pe.n, s_ldg.n

        @blk.vector
        def _(e):
            e.memset(ones64[:], 1.0)
            e.memset(hslab[:], 0.0)
            e.tensor_scalar(ta[:], dtraw[:, 3:TP], hv[:, 0:1], None, ALU.add, ALU.bypass)
            e.drain()
            e.tensor_scalar(tb[:], ta[:], 0.0, None, ALU.max, ALU.bypass)
            e.tensor_scalar(tc[:], ta[:], -1.0, None, ALU.mult, ALU.bypass)
            e.drain()
            e.tensor_tensor(tc[:], ta[:], tc[:], ALU.max)
            e.drain()
            s_dve.inc(e.nop(), 1)

        @blk.scalar
        def _(e):
            e.wait_ge(s_dve.h, dve0 + 1)
            e.activation(td[:], tc[:], AF.Exp, scale=-1.0)
            e.drain()
            e.activation(tc[:], td[:], AF.Ln, bias=onecolf2[0:NH, 0:1])
            e.drain()
            e.activation(negA[:], hv[:, 1:2], AF.Exp)
            e.drain()
            s_act.inc(e.nop(), 1)

        @blk.vector
        def _(e):
            e.wait_ge(s_act.h, act0 + 1)
            e.tensor_tensor(dtv[:], tb[:], tc[:], ALU.add)
            e.drain()
            e.tensor_scalar(ta[:], dtv[:], 1e-19, None, ALU.max, ALU.bypass)
            e.tensor_scalar(negA[:], negA[:], -1.0, None, ALU.mult, ALU.bypass)
            e.drain()
            e.tensor_scalar(tb[:], dtv[:], negA[:, 0:1], None, ALU.mult, ALU.bypass)
            e.drain()
            e.tensor_copy(logdA_b[:], tb[:])
            for k in range(NCH):
                e.tensor_tensor_scan(cumhs[:, ts(k, L)], ones64[:], tb[:, ts(k, L)],
                                     0.0, ALU.mult, ALU.add)
            e.drain()
            for k in range(NCH):
                init = 0.0 if k == 0 else cumabs[:, k * L - 1:k * L]
                e.tensor_tensor_scan(cumabs[:, ts(k, L)], ones64[:], tb[:, ts(k, L)],
                                     init, ALU.mult, ALU.add)
                e.drain()
            s_dve.inc(e.nop(), 1)

        @blk.scalar
        def _(e):
            e.wait_ge(s_dve.h, dve0 + 2)
            e.activation(logdt[:], ta[:], AF.Ln)
            e.drain()
            e.activation(tbf[:], cumhs[:], AF.Exp)
            e.drain()
            s_act.inc(e.nop(), 1)
            for k in range(NCH):
                e.activation(ta[:, ts(k, L)], cumhs[:, ts(k, L)], AF.Exp, scale=-1.0,
                             bias=cumhs[:, k * L + L - 1:k * L + L])
            e.drain()
            e.activation(lam64[:], cumhs[:, L - 1::L], AF.Exp)
            e.drain()
            s_act.inc(e.nop(), 1)

        @blk.gpsimd
        def _(e):
            e.wait_ge(s_act.h, act0 + 2)
            s_ldg.inc(e.dma_start(out=bncA, in_=tbf[:]), 16)
            e.wait_ge(s_ldg.h, ldg0 + 16)

        @blk.vector
        def _(e):
            e.wait_ge(s_act.h, act0 + 3)
            e.wait_ge(s_ldg.h, ldg0 + 16)
            e.tensor_tensor(w_hs[:], ta[:], dtv[:], ALU.mult)
            e.drain()
            s_dve.inc(e.nop(), 1)

        @blk.scalar
        def _(e):
            e.wait_ge(s_dve.h, dve0 + 3)
            e.activation(tbf[:], cumabs[:], AF.Exp)
            e.drain()
            s_act.inc(e.nop(), 1)

        @blk.gpsimd
        def _(e):
            e.wait_ge(s_act.h, act0 + 4)
            s_ldg.inc(e.dma_start(out=bncB, in_=tbf[:]), 16)
            s_ldg.inc(e.dma_start(out=bncC, in_=lam64[:]), 16)
            e.wait_ge(s_ldg.h, ldg0 + 48)
            s_ldg.inc(e.dma_start(out=lamflat[:],
                                  in_=bncC.rearrange("a b -> (a b)").unsqueeze(0)), 16)
            e.wait_ge(s_ldg.h, ldg0 + 64)
            s_gp.inc(e.nop(), 1)

        NLAMMM = (NH * NCH + 511) // 512
        @blk.tensor
        def _(e):
            e.wait_ge(s_gp.h, gp0 + 1)
            for i in range(NLAMMM):
                if i >= 1:
                    e.wait_ge(s_act.h, act0 + 4 + i)
                w = min(512, NH * NCH - i * 512)
                _mi = e.matmul(pTlam[:, 0:w], onescolf[:], lamflat[0:1, i * 512:i * 512 + w],
                               start=True, stop=True)
                s_pe.inc(_mi, 1)

        @blk.scalar
        def _(e):
            for i in range(NLAMMM):
                e.wait_ge(s_pe.h, pe0 + i + 1)
                w = min(512, NH * NCH - i * 512)
                e.activation(lamb[:, i * 512:i * 512 + w], pTlam[:, 0:w], AF.Copy)
                e.drain()
                s_act.inc(e.nop(), 1)

        pe1, dve1, act1 = pe0 + NLAMMM, dve0 + 3, act0 + 4 + NLAMMM
        @blk.tensor
        def _(e):
            e.wait_ge(s_dve.h, dve1)
            e.wait_ge(s_act.h, act1)
            for k in range(NCH):
                if k > 0:
                    e.wait_ge(s_dve.h, dve1 + 2 * k)
                    e.wait_ge(s_act.h, act1 + 2 * k)
                e.transpose(pTb[:, :], logdA_b[:, ts(k, L)], ident[0:64, 0:64])
                _mi = e.transpose(pTf[:, :], logdt[:, ts(k, L)], identf[0:64, 0:64])
                s_pe.inc(_mi, 1)
                e.wait_ge(s_dve.h, dve1 + 2 * k + 1)
                e.wait_ge(s_act.h, act1 + 2 * k + 1)
                e.transpose(pTf2[:, :], w_hs[:, ts(k, L)], identf[0:64, 0:64])
                _mi = e.matmul(pTf[:, :], triu[:], logdAT[:, k * 64:(k + 1) * 64],
                               start=True, stop=True)
                s_pe.inc(_mi, 1)

        @blk.vector
        def _(e):
            for k in range(NCH):
                e.wait_ge(s_pe.h, pe1 + 2 * k + 1)
                e.tensor_copy(logdAT[:, k * 64:(k + 1) * 64], pTb[:, :])
                e.drain()
                s_dve.inc(e.nop(), 1)
                e.wait_ge(s_pe.h, pe1 + 2 * k + 2)
                e.wait_ge(s_act.h, act1 + 2 * k + 1)
                e.tensor_tensor(colterm[:, k * 64:(k + 1) * 64], ldtT[:],
                                pTf[:, :], ALU.subtract)
                e.drain()
                s_dve.inc(e.nop(), 1)

        @blk.scalar
        def _(e):
            for k in range(NCH):
                e.wait_ge(s_pe.h, pe1 + 2 * k + 1)
                e.activation(ldtT[:], pTf[:, :], AF.Copy)
                e.drain()
                s_act.inc(e.nop(), 1)
                e.wait_ge(s_pe.h, pe1 + 2 * k + 2)
                e.wait_ge(s_dve.h, dve1 + 2 * k + 2)
                e.activation(wTp[:, k * 64:(k + 1) * 64], pTf2[:, :], AF.Copy)
                e.drain()
                s_act.inc(e.nop(), 1)

        pe2, act2, dve2 = pe1 + 2 * NCH, act1 + 2 * NCH, dve1 + 2 * NCH
        @blk.tensor
        def _(e):
            e.wait_ge(s_act.h, act2)
            _mi = e.transpose(pTf[0:1, 0:64], hv[:, 2:3], identf[0:64, 0:64])
            s_pe.inc(_mi, 1)
            e.wait_ge(s_act.h, act2 + 1)
            _mi = e.matmul(pTf2[:, 0:64], onescolf[:], Drow[:], start=True, stop=True)
            s_pe.inc(_mi, 1)

        @blk.scalar
        def _(e):
            e.wait_ge(s_pe.h, pe2 + 1)
            e.activation(Drow[:], pTf[0:1, 0:64], AF.Copy)
            e.drain()
            s_act.inc(e.nop(), 1)
            e.wait_ge(s_pe.h, pe2 + 2)
            e.activation(Dbc[:], pTf2[:, 0:64], AF.Copy)
            e.drain()
            s_act.inc(e.nop(), 1)

        @blk.vector
        def _(e):
            e.wait_ge(s_act.h, act2 + 2)
            for h in range(NH):
                e.tensor_scalar(DIall[:, ts(h)], ident[:], Dbc[:, h:h + 1], None,
                                ALU.mult, ALU.bypass)
            e.drain()
            s_dve.inc(e.nop(), 1)
    phase_barrier()

    with contextlib.ExitStack() as ph:
        psb = lambda n, shp, dt: ph.enter_context(nc.sbuf_tensor(n, shp, dt))
        xq = [psb(f"xq{i}", [128, DSSM], BF16) for i in range(2)]
        smb = psb("smb", [128, L], BF16)
        btT = psb("btT", [128, 128], BF16)
        striB = [psb(f"stri{i}", [128, 1024], BF16) for i in range(2)]
        xwB = [psb(f"xw{i}", [128, 1024], BF16) for i in range(2)]
        EclB = [psb(f"ecl{i}", [128, 1024], BF16) for i in range(2)]
        EexpB = [psb(f"eexp{i}", [128, 1024], BF16) for i in range(2)]
        mppB = [psb(f"mpp{i}", [128, 1024], BF16) for i in range(2)]
        cdecB = [psb(f"cdec{i}", [128, 1024], BF16) for i in range(2)]
        ystB = [psb(f"yst{i}", [128, 1024], BF16) for i in range(2)]
        expfk = psb("expfk", [1, NH * L], BF16)
        pE = ph.enter_context(nc.psum_tensor("pE", [128, 1024], F32))
        pY = ph.enter_context(nc.psum_tensor("pY", [128, 1024], F32))
        pBS = ph.enter_context(nc.psum_tensor("pBS", [128, 1024], F32))
        pSC = ph.enter_context(nc.psum_tensor("pSC", [128, L], F32))
        pBT = ph.enter_context(nc.psum_tensor("pBT", [128, 128], BF16))
        xqs = [SEM("p3xa"), SEM("p3xb")]
        yst = [SEM("p3ya"), SEM("p3yb")]
        efs = SEM("p3ef")
        pe0, dve0, act0, gp0, tr0, ld0 = s_pe.n, s_dve.n, s_act.n, s_gp.n, s_tr.n, s_ld.n

        def pe_at(k, idx):
            return pe0 + 42 * k + idx

        def dve_at(k, idx):
            return dve0 + 42 * k + idx

        def act_at(k, idx):
            return act0 + 17 * k + idx

        def gp_at(k, idx):
            return gp0 + 8 * k + idx

        def prev2(k, g):
            gg = g - 2
            if gg < 0:
                return k - 1, gg + 8
            return k, gg

        @blk.tensor
        def _(e):
            for k in range(NCH):
                if k >= 1:
                    e.wait_ge(s_dve.h, dve_at(k - 1, 1))
                _mi = e.matmul(pSC[:, :], Bt_r[:, ts(k, L)], Ct_r[:, ts(k, L)], start=True, stop=True)
                s_pe.inc(_mi, 1)
                if k >= 1:
                    e.wait_ge(s_dve.h, dve_at(k - 1, 2))
                _mi = e.transpose(pBT[:, :], Bt_r[:, ts(k, L)], ident[:])
                s_pe.inc(_mi, 1)
                for g in range(8):
                    e.wait_ge(s_dve.h, dve_at(k, 3 + 5 * g))
                    kk, gg = (k, g - 1) if g >= 1 else (k - 1, 7)
                    if kk >= 0:
                        e.wait_ge(s_dve.h, dve_at(kk, 5 + 5 * gg))
                    for c2 in range(2):
                        _mi = e.matmul(pE[:, ts(c2, 512)], onesfull[:],
                                       striB[g % 2][:, ts(c2, 512)], start=True, stop=True)
                    s_pe.inc(_mi, 1)
                    kk, gg = (k, g - 1) if g >= 1 else (k - 1, 7)
                    if kk >= 0:
                        e.wait_ge(s_dve.h, dve_at(kk, 7 + 5 * gg))
                    e.wait_ge(efs.h, 16 * (k + 1))
                    for hh in range(8):
                        h = g * 8 + hh
                        _mi = e.matmul(pBS[:, ts(hh)], onescol[:],
                                       expfk[0:1, ts(h)],
                                       start=(hh % 4 == 0), stop=(hh % 4 == 3),
                                       skip_group_check=True)
                    s_pe.inc(_mi, 1)
                    e.wait_ge(s_gp.h, gp_at(k, 1 + g))
                    kk, gg = (k, g - 1) if g >= 1 else (k - 1, 7)
                    if kk >= 0:
                        e.wait_ge(s_act.h, act_at(kk, 3 + 2 * gg))
                    for hh in range(8):
                        h = g * 8 + hh
                        _mi = e.matmul(pY[:, ts(hh)], xq[k % 2][:, ts(h)], mppB[g % 2][:, ts(hh)],
                                       start=(hh % 4 == 0), stop=False, skip_group_check=True)
                    s_pe.inc(_mi, 1)
                    e.wait_ge(s_dve.h, dve_at(k, 6 + 5 * g))
                    e.wait_ge(s_act.h, act_at(k, 1))
                    for hh in range(8):
                        h = g * 8 + hh
                        _mi = e.matmul(pY[:, ts(hh)], hbf[:, ts(h)], cdecB[g % 2][:, ts(hh)],
                                       start=False, stop=(hh % 4 == 3), skip_group_check=True)
                    s_pe.inc(_mi, 1)
                    e.wait_ge(s_dve.h, dve_at(k, 4 + 5 * g))
                    for c2 in range(2):
                        _mi = e.matmul(pBS[:, ts(c2, 512)], btT[:], xwB[g % 2][:, ts(c2, 512)],
                                       start=True, stop=True, skip_group_check=True)
                    s_pe.inc(_mi, 1)

        @blk.vector
        def _(e):
            for k in range(NCH):
                e.wait_ge(s_pe.h, pe_at(k, 1))
                if k >= 1:
                    e.wait_ge(s_gp.h, gp_at(k - 1, 8))
                e.tensor_tensor(smb[:], pSC[:, :], triu[:], ALU.mult)
                e.drain()
                s_dve.inc(e.nop(), 1)
                e.wait_ge(s_pe.h, pe_at(k, 2))
                if k >= 1:
                    e.wait_ge(s_pe.h, pe_at(k - 1, 42))
                e.tensor_copy(btT[:], pBT[:, :])
                e.drain()
                s_dve.inc(e.nop(), 1)
                for g in range(8):
                    kk, gg = prev2(k, g)
                    if kk >= 0:
                        e.wait_ge(s_pe.h, pe_at(kk, 3 + 5 * gg))
                    tri_rep = triu[:, :].unsqueeze(1).broadcast_to([128, 8, 128])
                    ldA_rep = logdAT[:, k * 64 + g * 8:k * 64 + (g + 1) * 8].unsqueeze(2).broadcast_to([128, 8, 128])
                    e.tensor_tensor(striB[g % 2][:, :].rearrange("p (a b) -> p a b", a=8),
                                    tri_rep, ldA_rep, ALU.mult)
                    e.drain()
                    s_dve.inc(e.nop(), 1)
                    if g == 0:
                        e.wait_ge(xqs[k % 2].h, 16 * (k // 2 + 1))
                    kk, gg = prev2(k, g)
                    if kk >= 0:
                        e.wait_ge(s_pe.h, pe_at(kk, 7 + 5 * gg))
                    w_rep = wTp[:, k * 64 + g * 8:k * 64 + (g + 1) * 8].unsqueeze(2).broadcast_to([128, 8, 128])
                    e.tensor_tensor(xwB[g % 2][:, :].rearrange("p (a b) -> p a b", a=8),
                                    xq[k % 2][:, ts(g, 1024)].rearrange("p (a b) -> p a b", a=8),
                                    w_rep, ALU.mult)
                    e.drain()
                    s_dve.inc(e.nop(), 1)
                    e.wait_ge(s_pe.h, pe_at(k, 3 + 5 * g))
                    kk, gg = prev2(k, g)
                    if kk >= 0:
                        e.wait_ge(s_act.h, act_at(kk, 2 + 2 * gg))
                    ct_rep2 = colterm[:, k * 64 + g * 8:k * 64 + (g + 1) * 8].unsqueeze(2).broadcast_to([128, 8, 128])
                    e.tensor_tensor(EclB[g % 2][:, :].rearrange("p (a b) -> p a b", a=8),
                                    pE[:, :].rearrange("p (a b) -> p a b", a=8),
                                    ct_rep2, ALU.add)
                    e.drain()
                    e.tensor_scalar(EclB[g % 2][:, :], EclB[g % 2][:, :], 60.0, None,
                                    ALU.min, ALU.bypass)
                    e.drain()
                    s_dve.inc(e.nop(), 1)
                    e.wait_ge(s_pe.h, pe_at(k, 4 + 5 * g))
                    kk, gg = prev2(k, g)
                    if kk >= 0:
                        e.wait_ge(s_pe.h, pe_at(kk, 6 + 5 * gg))
                    ct_rep = Ct_r[:, ts(k, L)].unsqueeze(1).broadcast_to([128, 8, 128])
                    e.tensor_tensor(cdecB[g % 2][:, :].rearrange("p (a b) -> p a b", a=8),
                                    pBS[:, :].rearrange("p (a b) -> p a b", a=8),
                                    ct_rep, ALU.mult)
                    e.drain()
                    s_dve.inc(e.nop(), 1)
                    e.wait_ge(s_pe.h, pe_at(k, 7 + 5 * g))
                    e.wait_ge(s_act.h, act_at(k, 1))
                    lam_rep = lamb[:, k::NCH][:, g * 8:(g + 1) * 8].unsqueeze(2).broadcast_to([128, 8, 128])
                    e.tensor_tensor(hslab[:, ts(g, 1024)].rearrange("p (a b) -> p a b", a=8),
                                    hslab[:, ts(g, 1024)].rearrange("p (a b) -> p a b", a=8),
                                    lam_rep, ALU.mult)
                    e.drain()
                    e.tensor_tensor(hslab[:, ts(g, 1024)], hslab[:, ts(g, 1024)],
                                    pBS[:, :], ALU.add)
                    e.drain()
                    s_dve.inc(e.nop(), 1)

        @blk.scalar
        def _(e):
            for k in range(2):
                if k < NCH:
                    xqs[k % 2].inc(e.dma_start_transpose(xq[k % 2][:, :], xf_s[:, ts(k, L)]), 16)
            for k in range(NCH):
                if k >= 1:
                    e.wait_ge(s_dve.h, dve_at(k - 1, 42))
                e.activation(hbf[:], hslab[:], AF.Copy)
                e.drain()
                s_act.inc(e.nop(), 1)
                for g in range(8):
                    e.wait_ge(s_dve.h, dve_at(k, 5 + 5 * g))
                    kk, gg = prev2(k, g)
                    if kk >= 0:
                        e.wait_ge(s_gp.h, gp_at(kk, 1 + gg))
                    e.activation(EexpB[g % 2][:, :], EclB[g % 2][:, :], AF.Exp)
                    e.drain()
                    s_act.inc(e.nop(), 1)
                    e.wait_ge(s_pe.h, pe_at(k, 6 + 5 * g))
                    kk, gg = prev2(k, g)
                    if kk >= 0:
                        e.wait_ge(yst[gg % 2].h, 16 * ((8 * kk + gg) // 2 + 1))
                    e.activation(ystB[g % 2][:, :], pY[:, :], AF.Copy)
                    e.drain()
                    s_act.inc(e.nop(), 1)
                if k + 2 < NCH:
                    e.wait_ge(s_dve.h, dve_at(k, 39))
                    xqs[k % 2].inc(e.dma_start_transpose(xq[k % 2][:, :], xf_s[:, ts(k + 2, L)]), 16)

        @blk.gpsimd
        def _(e):
            for k in range(NCH):
                for g in range(8):
                    e.wait_ge(s_act.h, act_at(k, 2 + 2 * g))
                    e.wait_ge(s_dve.h, dve_at(k, 1))
                    kk, gg = prev2(k, g)
                    if kk >= 0:
                        e.wait_ge(s_pe.h, pe_at(kk, 5 + 5 * gg))
                    sm_rep = smb[:, :].unsqueeze(1).broadcast_to([128, 8, 128])
                    e.tensor_tensor(mppB[g % 2][:, :].rearrange("p (a b) -> p a b", a=8),
                                    EexpB[g % 2][:, :].rearrange("p (a b) -> p a b", a=8),
                                    sm_rep, ALU.mult)
                    e.drain()
                    e.tensor_tensor(mppB[g % 2][:, :], mppB[g % 2][:, :],
                                    DIall[:, ts(g, 1024)], ALU.add)
                    e.drain()
                    s_gp.inc(e.nop(), 1)

        @blk.sync
        def _(e):
            for k in range(NCH):
                if k >= 1:
                    e.wait_ge(s_pe.h, pe_at(k - 1, 39))
                efs.inc(e.dma_start(
                    out=expfk[0:1, :].rearrange("p (h t) -> p h t", h=NH),
                    in_=bncA[:, ts(k, L)].unsqueeze(0)), 16)
                for g in range(8):
                    e.wait_ge(s_act.h, act_at(k, 3 + 2 * g))
                    yst[g % 2].inc(e.dma_start(
                        out=yraw_s[g * 1024:(g + 1) * 1024, ts(k, L)].rearrange(
                            "(h p) t -> p h t", p=128),
                        in_=ystB[g % 2][:, :].rearrange("p (h t) -> p h t", h=8)), 16)

        @blk.sync
        def _(e):
            e.wait_ge(yst[0].h, yst[0].n)
            e.wait_ge(yst[1].h, yst[1].n)
            e.wait_ge(efs.h, efs.n)
            s_ld.inc(e.nop(), 16)
    phase_barrier()

    with contextlib.ExitStack() as ph:
        htmp = ph.enter_context(nc.sbuf_tensor("htmp", [128, DSSM], F32))
        dve0, gp0, cc0, ldg0 = s_dve.n, s_gp.n, s_cc.n, s_ldg.n

        @blk.vector
        def _(e):
            e.tensor_scalar(htmp[:], hslab[:], fl[:, 0:1], None, ALU.mult, ALU.bypass)
            e.drain()
            s_dve.inc(e.nop(), 1)

        @blk.gpsimd
        def _(e):
            e.wait_ge(s_dve.h, dve0 + 1)
            s_ldg.inc(e.dma_start(out=cc_in, in_=htmp[:]), 16)
            e.wait_ge(s_ldg.h, ldg0 + 16)
            cci = e.collective_compute(
                "AllReduce", ALU.add,
                replica_groups=[[2 * i, 2 * i + 1] for i in range(n_cc // 2)],
                ins=[cc_in.opt()], outs=[cc_out.opt()],
            )
            cci.then_inc(s_cc.h, 1)
            s_cc.n += 1
            e.wait_ge(s_cc.h, cc0 + 1)
            s_ldg.inc(e.dma_start(out=htmp[:], in_=cc_out), 16)
            e.wait_ge(s_ldg.h, ldg0 + 32)
            s_gp.inc(e.nop(), 1)

        @blk.vector
        def _(e):
            e.wait_ge(s_gp.h, gp0 + 1)
            e.tensor_scalar(seed[:], htmp[:], fl[:, 1:2], None, ALU.mult, ALU.bypass)
            e.drain()
            s_dve.inc(e.nop(), 1)
    phase_barrier()
    ssd2_es.close()

    assert T_MAIN % 512 == 0
    NTQ = NCH // 4
    host = {k: [] for k in range(NCH)}
    tail34 = []
    for tq in range(NTQ):
        items = [("g", tq, 0)] + [("u", tq, o8) for o8 in range(8)]
        if 4 * tq + 7 < NCH:
            host[4 * tq + 4] += items[0:3]
            host[4 * tq + 5] += items[3:5]
            host[4 * tq + 6] += items[5:7]
            host[4 * tq + 7] += items[7:9]
        else:
            tail34 += items
    ordered34 = [it for k in range(NCH) for it in host[k]] + tail34

    pe4_g, pe4_u, dve4_g, dve4_e, act4_g = {}, {}, {}, {}, {}
    _pe4 = _dve4 = _act4 = 0
    for it in ordered34:
        if it[0] == "g":
            _pe4 += 1; pe4_g[it[1]] = _pe4
            _dve4 += 1; dve4_g[it[1]] = _dve4
            _act4 += 1; act4_g[it[1]] = _act4
        else:
            ui = 8 * it[1] + it[2]
            for p in range(8):
                _pe4 += 1; pe4_u[(ui, p)] = _pe4
            for tt in range(4):
                _dve4 += 1; dve4_e[(ui, tt)] = _dve4

    with contextlib.ExitStack() as ph:
        psb = lambda n, shp, dt: ph.enter_context(nc.sbuf_tensor(n, shp, dt))
        yslab = psb("yslab", [128, DSSM], BF16)
        zslab = psb("zslab", [128, DSSM], BF16)
        silz = psb("silz", [128, DSSM], BF16)
        yg2 = [psb(f"yg2{i}", [128, 1024], BF16) for i in range(2)]
        ycB = [psb(f"yc{i}", [128, 1024], F32) for i in range(2)]
        cdaB = [psb(f"cda{i}", [128, 1024], BF16) for i in range(2)]
        expabsk = psb("expabsk", [33, 4096], BF16)
        ygr = psb("ygr", [128, NZT * 512], BF16)
        wosB = [psb(f"wos{i}", [128, 8 * 512], BF16) for i in range(2)]
        x1b = [psb(f"x1b{i}", [128, 512], F32) for i in range(4)]
        x2b = [psb(f"x2b{i}", [128, 512], F32) for i in range(4)]
        rsq = psb("rsq", [128, 4], F32)
        rln = psb("rln", [128, 4], F32)
        pB2 = ph.enter_context(nc.psum_tensor("pB2", [128, 1024], F32))
        pD = ph.enter_context(nc.psum_tensor("pD", [128, 1024], F32))
        pQ = [ph.enter_context(nc.psum_tensor(f"pQ{i}", [128, 512], F32)) for i in range(4)]
        eas = SEM("pbea")
        yls = SEM("pbyl")
        zls = SEM("pbzl")
        ygst = [SEM("pbga"), SEM("pbgb")]
        s_pe4 = SEM("spe4")
        s_dve4 = SEM("sdv4")
        s_act4 = SEM("sac4")
        p4yg = SEM("p4yg")
        p4w = [SEM("p4wa"), SEM("p4wb")]
        p4x1 = [SEM(f"p4x{i}") for i in range(4)]
        p4st = [SEM(f"p4s{i}") for i in range(4)]
        pe0, dve0, act0 = s_pe.n, s_dve.n, s_act.n

        def pe3(k, i):
            return pe0 + 16 * k + i

        def dve3(k, i):
            return dve0 + 24 * k + i

        def act3(k):
            return act0 + k + 1

        def prev2b(k, g):
            gg = g - 2
            if gg < 0:
                return k - 1, gg + 8
            return k, gg

        def ygr_t(c, tt):
            return ygr[:, (c * 4 + tt) * 128:(c * 4 + tt) * 128 + 128]

        def x2_stores(e, it):
            tq, o8 = it[1], it[2]
            ui = 8 * tq + o8
            for tt in range(4):
                m = 4 * ui + tt
                tg = tq * 4 + tt
                e.wait_ge(s_dve4.h, dve4_e[(ui, tt)])
                p4st[m % 4].inc(e.dma_start(
                    out=x2_s[tg * 128:tg * 128 + 128, ts(o8, 512)],
                    in_=x2b[m % 4][:, :]), 16)

        @blk.sync
        def _(e):
            for k in range(NCH):
                if k >= 1:
                    e.wait_ge(s_pe.h, pe3(k - 1, 15))
                eas.inc(e.dma_start(
                    out=expabsk[0:1, :].rearrange("p (h t) -> p h t", h=32),
                    in_=bncB[0:32, ts(k, L)].unsqueeze(0)), 16)
                eas.inc(e.dma_start(
                    out=expabsk[32:33, :].rearrange("p (h t) -> p h t", h=32),
                    in_=bncB[32:64, ts(k, L)].unsqueeze(0)), 16)
                if k >= 1:
                    e.wait_ge(s_dve.h, dve3(k - 1, 24))
                for g in range(8):
                    yls.inc(e.dma_start(
                        out=yslab[:, ts(g, 1024)].rearrange("p (h t) -> p h t", h=8),
                        in_=yraw_s[g * 1024:(g + 1) * 1024, ts(k, L)].rearrange(
                            "(h p) t -> p h t", p=128)), 16)
                if k >= 1:
                    e.wait_ge(s_act.h, act3(k - 1))
                for g in range(8):
                    zls.inc(e.dma_start(
                        out=zslab[:, ts(g, 1024)].rearrange("p (h t) -> p h t", h=8),
                        in_=z_s[g * 1024:(g + 1) * 1024, ts(k, L)].rearrange(
                            "(h p) t -> p h t", p=128)), 16)
                for g in range(8):
                    e.wait_ge(s_dve.h, dve3(k, 3 + 3 * g))
                    ygst[g % 2].inc(e.dma_start(
                        out=yg_s[g * 1024:(g + 1) * 1024, ts(k, L)].rearrange(
                            "(h p) t -> p h t", p=128),
                        in_=yg2[g % 2][:, :].rearrange("p (h t) -> p h t", h=8)), 16)
                if k % 4 == 3 and (k // 4) < NTQ:
                    tq = k // 4
                    e.wait_ge(ygst[0].h, 16 * 4 * (k + 1))
                    e.wait_ge(ygst[1].h, 16 * 4 * (k + 1))
                    if tq >= 1:
                        e.wait_ge(s_pe4.h, pe4_u[(8 * tq - 1, 7)])
                    for g in range(8):
                        p4yg.inc(e.dma_start(
                            out=ygr[:, g * 8 * 4 * 128:(g + 1) * 8 * 4 * 128].rearrange(
                                "p (a t) -> p a t", a=8),
                            in_=yg_s[g * 1024:(g + 1) * 1024,
                                     tq * 512:(tq + 1) * 512].rearrange(
                                "(a p) t -> p a t", p=128)), 16)
                for it in host[k]:
                    if it[0] == "u":
                        x2_stores(e, it)
            for it in tail34:
                if it[0] == "u":
                    x2_stores(e, it)

        @blk.sync
        def _(e):
            e.wait_ge(ygst[0].h, ygst[0].n)
            e.wait_ge(ygst[1].h, ygst[1].n)
            e.wait_ge(eas.h, eas.n)
            for i in range(4):
                e.wait_ge(p4st[i].h, p4st[i].n)
            s_ld.inc(e.nop(), 16)

        @blk.gpsimd
        def _(e):
            for it in ordered34:
                if it[0] != "u":
                    continue
                tq, o8 = it[1], it[2]
                ui = 8 * tq + o8
                for p in range(8):
                    q = 8 * ui + p
                    if q >= 2:
                        qq = q - 2
                        e.wait_ge(s_pe4.h, pe4_u[(qq // 8, qq % 8)])
                    p4w[q % 2].inc(e.dma_start(
                        out=wosB[q % 2][:, :].rearrange("p (a n) -> p a n", a=8),
                        in_=wout_s[p * 1024:(p + 1) * 1024, ts(o8, 512)].rearrange(
                            "(a p) n -> p a n", p=128)), 16)

        def p4_pe_pieces(it):
            if it[0] == "g":
                tq = it[1]

                def mkg(tt):
                    def f(e):
                        if tt == 0:
                            e.wait_ge(p4yg.h, 16 * 8 * (tq + 1))
                            if tq >= 1:
                                e.wait_ge(s_dve4.h, dve4_e[(8 * tq - 1, 3)])
                        for c in range(64):
                            _mi = e.matmul(pQ[tt][:, 0:128], ygr_t(c, tt), ygr_t(c, tt),
                                           start=(c == 0), stop=(c == 63),
                                           skip_group_check=True)
                        if tt == 3:
                            s_pe4.inc(_mi, 1)
                    return f
                return [mkg(tt) for tt in range(4)]
            tq, o8 = it[1], it[2]
            ui = 8 * tq + o8

            def mkp(p):
                def f(e):
                    if p == 0:
                        if o8 == 0:
                            e.wait_ge(s_dve4.h, dve4_g[tq])
                        else:
                            e.wait_ge(s_dve4.h, dve4_e[(ui - 1, 3)])
                    q = 8 * ui + p
                    e.wait_ge(p4w[q % 2].h, 16 * (q // 2 + 1))
                    for cc in range(8):
                        c = 8 * p + cc
                        for tt in range(4):
                            _mi = e.matmul(pQ[tt][:, :], ygr_t(c, tt),
                                           wosB[q % 2][:, cc * 512:(cc + 1) * 512],
                                           start=(c == 0), stop=(c == 63),
                                           skip_group_check=True)
                    s_pe4.inc(_mi, 1)
                return f
            return [mkp(p) for p in range(8)]

        @blk.tensor
        def _(e):
            for k in range(NCH):
                early, late = [], []
                for idx, it in enumerate(host[k]):
                    if it[0] == "g" or idx == 0:
                        early += p4_pe_pieces(it)
                    else:
                        late.append(it)
                ei = 0
                for g in range(8):
                    if ei < len(early):
                        early[ei](e)
                        ei += 1
                    kk, gg = (k, g - 1) if g >= 1 else (k - 1, 7)
                    if kk >= 0:
                        e.wait_ge(s_dve.h, dve3(kk, 1 + 3 * gg))
                    e.wait_ge(eas.h, 32 * (k + 1))
                    for hh in range(8):
                        h = g * 8 + hh
                        r = 32 * (h // 32)
                        _mi = e.matmul(pB2[:, ts(hh)], ones33[r:r + 1, :],
                                       expabsk[r:r + 1, (h % 32) * 128:(h % 32) * 128 + 128],
                                       start=(hh % 4 == 0), stop=(hh % 4 == 3),
                                       skip_group_check=True)
                    s_pe.inc(_mi, 1)
                    if ei < len(early):
                        early[ei](e)
                        ei += 1
                    e.wait_ge(s_dve.h, dve3(k, 1 + 3 * g))
                    kk, gg = (k, g - 1) if g >= 1 else (k - 1, 7)
                    if kk >= 0:
                        e.wait_ge(s_dve.h, dve3(kk, 2 + 3 * gg))
                    for hh in range(8):
                        h = g * 8 + hh
                        _mi = e.matmul(pD[:, ts(hh)], seed[:, ts(h)], cdaB[g % 2][:, ts(hh)],
                                       start=(hh % 4 == 0), stop=(hh % 4 == 3),
                                       skip_group_check=True)
                    s_pe.inc(_mi, 1)
                while ei < len(early):
                    early[ei](e)
                    ei += 1
                for it in late:
                    for f in p4_pe_pieces(it):
                        f(e)
            for it in tail34:
                for f in p4_pe_pieces(it):
                    f(e)

        def p4_dve(e, it):
            if it[0] == "g":
                tq = it[1]
                e.wait_ge(s_pe4.h, pe4_g[tq])
                if tq >= 1:
                    e.wait_ge(s_act4.h, act4_g[tq - 1])
                e.memset(rsq[:, :], 0.0)
                e.drain()
                for tt in range(4):
                    e.tensor_tensor(ycB[0][:, 0:128], pQ[tt][:, 0:128], identf[:], ALU.mult)
                    e.drain()
                    e.tensor_scalar(ycB[0][:, 128:256], ycB[0][:, 0:128], 1.0, None,
                                    ALU.mult, ALU.add, accum_out=rsq[:, tt:tt + 1])
                    e.drain()
                s_dve4.inc(e.nop(), 1)
            else:
                tq, o8 = it[1], it[2]
                ui = 8 * tq + o8
                e.wait_ge(s_pe4.h, pe4_u[(ui, 7)])
                e.wait_ge(s_act4.h, act4_g[tq])
                for tt in range(4):
                    m = 4 * ui + tt
                    tg = tq * 4 + tt
                    e.wait_ge(p4x1[m % 4].h, 16 * (m // 4 + 1))
                    if m >= 4:
                        e.wait_ge(p4st[m % 4].h, 16 * (m // 4))
                    e.scalar_tensor_tensor(x2b[m % 4][:, :], pQ[tt][:, :],
                                           rstd3p[:, tg:tg + 1], x1b[m % 4][:, :],
                                           ALU.mult, ALU.add)
                    e.drain()
                    s_dve4.inc(e.nop(), 1)

        @blk.vector
        def _(e):
            for k in range(NCH):
                for g in range(8):
                    e.wait_ge(s_pe.h, pe3(k, 2 * g + 1))
                    kk, gg = prev2b(k, g)
                    if kk >= 0:
                        e.wait_ge(s_pe.h, pe3(kk, 2 * gg + 2))
                    ct_rep = Ct_r[:, ts(k, L)].unsqueeze(1).broadcast_to([128, 8, 128])
                    e.tensor_tensor(cdaB[g % 2][:, :].rearrange("p (a b) -> p a b", a=8),
                                    pB2[:, :].rearrange("p (a b) -> p a b", a=8),
                                    ct_rep, ALU.mult)
                    e.drain()
                    s_dve.inc(e.nop(), 1)
                    e.wait_ge(s_pe.h, pe3(k, 2 * g + 2))
                    e.wait_ge(yls.h, 16 * 8 * (k + 1))
                    e.tensor_tensor(ycB[g % 2][:, :], yslab[:, ts(g, 1024)], pD[:, :], ALU.add)
                    e.drain()
                    s_dve.inc(e.nop(), 1)
                    e.wait_ge(s_act.h, act3(k))
                    v = 16 * (4 * k + g // 2)
                    if v > 0:
                        e.wait_ge(ygst[g % 2].h, v)
                    e.tensor_tensor(yg2[g % 2][:, :], ycB[g % 2][:, :],
                                    silz[:, ts(g, 1024)], ALU.mult)
                    e.drain()
                    s_dve.inc(e.nop(), 1)
                for it in host[k]:
                    p4_dve(e, it)
            for it in tail34:
                p4_dve(e, it)

        def p4_act(e, it):
            if it[0] == "g":
                tq = it[1]
                e.wait_ge(s_dve4.h, dve4_g[tq])
                e.activation(rln[:, 0:4], rsq[:, 0:4], AF.Ln, scale=1.0 / DSSM,
                             bias=epscol[:, 0:1])
                e.drain()
                e.activation(rstd3p[:, 4 * tq:4 * tq + 4], rln[:, 0:4], AF.Exp, scale=-0.5)
                e.drain()
                s_act4.inc(e.nop(), 1)
            else:
                tq, o8 = it[1], it[2]
                ui = 8 * tq + o8
                for tt in range(4):
                    m = 4 * ui + tt
                    tg = tq * 4 + tt
                    if m >= 4:
                        e.wait_ge(s_dve4.h, dve4_e[((m - 4) // 4, (m - 4) % 4)])
                    p4x1[m % 4].inc(e.dma_start(
                        out=x1b[m % 4][:, :],
                        in_=x1_s[3 + tg * 128:3 + tg * 128 + 128, ts(o8, 512)]), 16)

        @blk.scalar
        def _(e):
            for k in range(NCH):
                e.wait_ge(zls.h, 16 * 8 * (k + 1))
                if k >= 1:
                    e.wait_ge(s_dve.h, dve3(k - 1, 24))
                e.activation(silz[:], zslab[:], AF.Silu)
                e.drain()
                s_act.inc(e.nop(), 1)
                for it in host[k]:
                    p4_act(e, it)
            for it in tail34:
                p4_act(e, it)
    phase_barrier()
    ssd_es.close()
    mid_es.close()


    emit_ln("l3", None, x2_s, T_MAIN, ln3_s)
    phase_barrier()
    emit_ffn("f2", ln3_s, T_MAIN, None, x2_s, out_t)
    phase_barrier()

    @blk.sync
    def _(e):
        for name, dap in dbg.items():
            s_ld.inc(e.dma_start(out=dap, in_=dbg_map[name]), 16)
        e.wait_ge(s_ld.h, s_ld.n)
        e.wait_ge(s_ldg.h, s_ldg.n)
        e.wait_ge(s_tr.h, s_tr.n)

    es.close()
    return nc, dbg_map


def _aux_inputs():
    triu_np = np.triu(np.ones((128, 128), np.float32)).astype(ml_dtypes.bfloat16)
    ident_np = np.eye(128, dtype=np.float32)
    return triu_np, ident_np.astype(ml_dtypes.bfloat16), ident_np


def make_in_maps(inputs, T_MAIN=2048):
    x = np.ascontiguousarray(np.asarray(inputs["L0_attn_hidden_states"], np.float32))
    W_in = np.ascontiguousarray(np.asarray(inputs["L0_attn_W_in"], np.float32))
    W_out = np.ascontiguousarray(np.asarray(inputs["L0_attn_W_out"], np.float32))
    conv_w = np.ascontiguousarray(np.asarray(inputs["L0_attn_conv_w"], np.float32))
    dt_bias = np.asarray(inputs["L0_attn_dt_bias"], np.float32)
    A_log = np.asarray(inputs["L0_attn_A_log"], np.float32)
    D_param = np.asarray(inputs["L0_attn_D_param"], np.float32)
    nw = np.asarray(inputs["L0_attn_norm_weight"], np.float32).reshape(DSSM, 1)
    wp = np.asarray(inputs["L0_ffn_weight_packed"], np.int32)
    sce = np.asarray(inputs["L0_ffn_scales"])
    pT = np.ascontiguousarray(wp.view(np.uint8)[:, 0::4].T)
    scT = np.ascontiguousarray(np.repeat(
        np.asarray(sce).view(np.uint16).T, 16, axis=0)).view(np.float16)
    hvecs = np.stack([dt_bias, A_log, D_param, np.zeros_like(D_param)], axis=1)
    hvecs = np.ascontiguousarray(hvecs.astype(np.float32))
    triu_np, ident_b, ident_f = _aux_inputs()

    Bsz, S, _ = x.shape
    nhalf = S // T_MAIN
    in_maps = []
    for core in range(8):
        b, half = core // nhalf if nhalf > 1 else core // 2, core % nhalf if nhalf > 1 else core % 2
        b, half = divmod(core, nhalf) if nhalf > 1 else (core // 2, core % 2)
        xm = np.ascontiguousarray(x[b, half * T_MAIN:(half + 1) * T_MAIN])
        if half == 0:
            xp = np.zeros((3, H), np.float32)
        else:
            xp = np.ascontiguousarray(x[b, half * T_MAIN - 3:half * T_MAIN])
        flags = np.array([[1.0 if half == 0 else 0.0,
                           0.0 if half == 0 else 1.0]], np.float32)
        in_maps.append({
            "x_main": xm, "x_pre": xp, "W_in": W_in, "W_out": W_out,
            "packedT": pT, "scalesE": scT, "conv_win": conv_w, "hvecs": hvecs,
            "nw_in": np.ascontiguousarray(nw), "flags_in": flags,
            "triu_in": triu_np, "ident_in": ident_b, "identf_in": ident_f,
        })
    return in_maps


_BUILT = {}


def kernel(**inputs):
    T_MAIN = 2048
    key = T_MAIN
    if key not in _BUILT:
        _BUILT[key] = build_program(T_MAIN)
    nc, _ = _BUILT[key]
    in_maps = make_in_maps(inputs, T_MAIN)
    res = run_bass_kernel_spmd(nc, in_maps, core_ids=list(range(8)))
    x = np.asarray(inputs["L0_attn_hidden_states"])
    Bsz, S, _ = x.shape
    out = np.empty((Bsz, S, H), np.float32)
    for core in range(8):
        b, half = divmod(core, S // T_MAIN)
        out[b, half * T_MAIN:(half + 1) * T_MAIN] = res.results[core]["out"]
    return out


# version 12
# speedup vs baseline: 1.0487x; 1.0487x over previous
import contextlib
import numpy as np
import ml_dtypes
import concourse.bass as bass
import concourse.mybir as mybir
from concourse.bass_utils import run_bass_kernel_spmd

F32 = mybir.dt.float32
BF16 = mybir.dt.bfloat16
F16 = mybir.dt.float16
U8 = mybir.dt.uint8
I32 = mybir.dt.int32
AF = mybir.ActivationFunctionType
ALU = mybir.AluOpType

H = 4096
DSSM = 8192
NH = 64
CONV = 8448
PROJ = 16704
KTAPS = 4
EPS = 1e-5
L = 128
NDT = H // 128
NZT = DSSM // 128
NCT = CONV // 128


class Sem:
    def __init__(self, h):
        self.h = h
        self.n = 0
    def inc(self, instr, k):
        instr.then_inc(self.h, k)
        self.n += k
        return self.n


def ts(i, n=128):
    return slice(i * n, (i + 1) * n)


def build_program(T_MAIN=2048, debug_outs=(), n_cc=8):
    NCH = T_MAIN // L
    NTT = T_MAIN // 128
    TP = T_MAIN + 3
    TPAD = ((TP + 15) // 16) * 16
    NGRP = 8
    GH = NH // NGRP

    CHUNKS = []
    off = 0
    while TP - off > 3:
        sz = min(512, TP - 3 - off)
        CHUNKS.append((off, sz))
        off += sz
    CHUNKS.append((off, 3))
    NCHK = len(CHUNKS)

    nc = bass.Bass("TRN2", target_bir_lowering=False, debug=False)

    x_main = nc.dram_tensor("x_main", [T_MAIN, H], F32, kind="ExternalInput").ap()
    x_pre = nc.dram_tensor("x_pre", [3, H], F32, kind="ExternalInput").ap()
    W_in = nc.dram_tensor("W_in", [H, PROJ], F32, kind="ExternalInput").ap()
    W_out = nc.dram_tensor("W_out", [DSSM, H], F32, kind="ExternalInput").ap()
    packedT = nc.dram_tensor("packedT", [2048, H], U8, kind="ExternalInput").ap()
    scalesE = nc.dram_tensor("scalesE", [2048, H], F16, kind="ExternalInput").ap()
    conv_win = nc.dram_tensor("conv_win", [CONV, KTAPS], F32, kind="ExternalInput").ap()
    hvecs = nc.dram_tensor("hvecs", [NH, 4], F32, kind="ExternalInput").ap()
    nw_in = nc.dram_tensor("nw_in", [DSSM, 1], F32, kind="ExternalInput").ap()
    flags_in = nc.dram_tensor("flags_in", [1, 2], F32, kind="ExternalInput").ap()
    triu_in = nc.dram_tensor("triu_in", [128, 128], BF16, kind="ExternalInput").ap()
    ident_in = nc.dram_tensor("ident_in", [128, 128], BF16, kind="ExternalInput").ap()
    identf_in = nc.dram_tensor("identf_in", [128, 128], F32, kind="ExternalInput").ap()
    out_t = nc.dram_tensor("out", [T_MAIN, H], F32, kind="ExternalOutput").ap()

    wdeq_s = nc.dram_tensor("wdeq_s", [H, H], BF16).ap()
    ln1_s = nc.dram_tensor("ln1_s", [TPAD, H], BF16).ap()
    x1_s = nc.dram_tensor("x1_s", [TP, H], F32).ap()
    ln2_s = nc.dram_tensor("ln2_s", [TPAD, H], BF16).ap()
    z_s = nc.dram_tensor("z_s", [DSSM, T_MAIN], BF16).ap()
    xf_s = nc.dram_tensor("xf_s", [DSSM, T_MAIN], BF16).ap()
    yraw_s = nc.dram_tensor("yraw_s", [DSSM, T_MAIN], BF16).ap()
    yg_s = nc.dram_tensor("yg_s", [DSSM, T_MAIN], BF16).ap()
    x2_s = nc.dram_tensor("x2_s", [T_MAIN, H], F32).ap()
    TPAD3 = ((T_MAIN + 15) // 16) * 16
    ln3_s = nc.dram_tensor("ln3_s", [TPAD3, H], BF16).ap()
    wout_s = nc.dram_tensor("wout_s", [DSSM, H], BF16).ap()
    cc_in = nc.dram_tensor("cc_in", [128, DSSM], F32).ap()
    cc_out = nc.dram_tensor("cc_out", [128, DSSM], F32).ap()
    bncA = nc.dram_tensor("bncA", [NH, T_MAIN], BF16).ap()
    bncB = nc.dram_tensor("bncB", [NH, T_MAIN], BF16).ap()
    bncC = nc.dram_tensor("bncC", [NH, NCH], F32).ap()

    dbg_map = {"wdeq": wdeq_s, "ln1": ln1_s, "x1": x1_s, "ln2": ln2_s, "z": z_s,
               "xf": xf_s, "yraw": yraw_s, "yg": yg_s, "x2": x2_s, "ln3": ln3_s}
    dbg = {}
    for name in debug_outs:
        src = dbg_map[name]
        dbg[name] = nc.dram_tensor("dbg_" + name, list(src.tensor.shape), src.dtype,
                                   kind="ExternalOutput").ap()

    es = contextlib.ExitStack()
    sbuf = lambda n, shp, dt: es.enter_context(nc.sbuf_tensor(n, shp, dt))
    SEM = lambda n: Sem(es.enter_context(nc.semaphore(n)))

    triu = sbuf("triu", [128, 128], BF16)
    ident = sbuf("ident", [128, 128], BF16)
    identf = sbuf("identf", [128, 128], F32)
    onescol = sbuf("onescol", [1, 128], BF16)
    ones33 = sbuf("ones33", [33, 128], BF16)
    onescolf = sbuf("onescolf", [1, 128], F32)
    onesfull = sbuf("onesfull", [128, 128], BF16)
    fl = sbuf("fl", [128, 2], F32)
    hv = sbuf("hv", [NH, 4], F32)
    cw2 = sbuf("cw2", [128, NCT * KTAPS], F32)
    nw2 = sbuf("nw2", [128, NZT], F32)
    epscol = sbuf("epscol", [128, 1], F32)
    onecolf2 = sbuf("onecolf2", [128, 1], F32)
    rstd3p = sbuf("rstd3p", [128, NCH], F32)

    s_ld = SEM("s_ld")
    s_ldg = SEM("s_ldg")
    s_tr = SEM("s_tr")
    s_pe = SEM("s_pe")
    s_dve = SEM("s_dve")
    s_act = SEM("s_act")
    s_gp = SEM("s_gp")
    s_cc = SEM("s_cc")

    blk = es.enter_context(nc.Block())

    def phase_barrier():
        c = dict(ld=s_ld.n, ldg=s_ldg.n, tr=s_tr.n, pe=s_pe.n, dve=s_dve.n,
                 act=s_act.n, gp=s_gp.n)
        for sec in (blk.sync, blk.gpsimd, blk.vector, blk.scalar, blk.tensor):
            @sec
            def _(e):
                e.wait_ge(s_ld.h, c["ld"]); e.wait_ge(s_ldg.h, c["ldg"])
                e.wait_ge(s_tr.h, c["tr"]); e.wait_ge(s_pe.h, c["pe"])
                e.wait_ge(s_dve.h, c["dve"]); e.wait_ge(s_act.h, c["act"])
                e.wait_ge(s_gp.h, c["gp"])

    @blk.sync
    def _(e):
        s_ld.inc(e.dma_start(out=triu[:], in_=triu_in), 16)
        s_ld.inc(e.dma_start(out=ident[:], in_=ident_in), 16)
        s_ld.inc(e.dma_start(out=identf[:], in_=identf_in), 16)
        s_ld.inc(e.dma_start(out=hv[:], in_=hvecs), 16)
        s_ld.inc(e.dma_start(out=fl[:], in_=flags_in.squeeze(0).partition_broadcast(128)), 16)
        s_ld.inc(e.dma_start(out=cw2[:].rearrange("p (a k) -> p a k", k=KTAPS), in_=conv_win.rearrange("(a p) k -> p a k", p=128)), 16)
        s_ld.inc(e.dma_start(out=nw2[:].unsqueeze(2), in_=nw_in.rearrange("(a p) b -> p a b", p=128), allow_slow_non_contiguous=True), 16)

    @blk.vector
    def _(e):
        e.wait_ge(s_ld.h, s_ld.n)
        e.memset(onescol[:], 1.0)
        e.memset(ones33[:], 1.0)
        e.memset(onescolf[:], 1.0)
        e.memset(onesfull[:], 1.0)
        e.memset(epscol[:], float(EPS))
        e.memset(onecolf2[:], 1.0)
        e.drain()
        s_dve.inc(e.nop(), 1)
    phase_barrier()

    with contextlib.ExitStack() as ph:
        psb = lambda n, shp, dt: ph.enter_context(nc.sbuf_tensor(n, shp, dt))
        pk = [psb(f"pk{i}", [128, H], U8) for i in range(2)]
        sc = [psb(f"sc{i}", [128, H], F16) for i in range(2)]
        lohi = psb("lohi", [128, 2 * H], U8)
        tg0 = psb("tg0", [128, 2 * H], BF16)
        wdq = [psb(f"wdq{i}", [128, 2 * H], BF16) for i in range(2)]
        wofp = [psb(f"wofp{i}", [128, 2 * H], F32) for i in range(2)]
        wosp = [psb(f"wosp{i}", [128, 2 * H], BF16) for i in range(2)]
        p0ld = [SEM("p0lda"), SEM("p0ldb")]
        p0st = [SEM("p0sta"), SEM("p0stb")]
        wls = [SEM("p0wla"), SEM("p0wlb")]
        wss = [SEM("p0wsa"), SEM("p0wsb")]
        gp2 = SEM("p0gp2")
        dv2 = SEM("p0dv2")
        dve0, gp0 = s_dve.n, s_gp.n

        @blk.sync
        def _(e):
            for i in range(16):
                if i >= 2:
                    e.wait_ge(s_gp.h, gp0 + (i - 1))
                p0ld[i % 2].inc(e.dma_start(out=pk[i % 2][:], in_=packedT[ts(i), :]), 16)
                p0ld[i % 2].inc(e.dma_start(out=sc[i % 2][:], in_=scalesE[ts(i), :]), 16)

        @blk.vector
        def _(e):
            for i in range(16):
                e.wait_ge(p0ld[i % 2].h, 32 * (i // 2 + 1))
                if i >= 2:
                    e.wait_ge(p0st[i % 2].h, 32 * (i // 2))
                p, w, sci = pk[i % 2], wdq[i % 2], sc[i % 2]
                e.tensor_scalar(lohi[:, 0:H], p[:], 15, None, ALU.bitwise_and, ALU.bypass)
                e.tensor_scalar(lohi[:, H:2 * H].bitcast(I32), p[:].bitcast(I32), 4, 0x0F0F0F0F,
                                ALU.logical_shift_right, ALU.bitwise_and)
                e.drain()
                e.tensor_scalar(tg0[:, :], lohi[:, :], -8.0, None, ALU.add, ALU.bypass)
                e.drain()
                e.tensor_tensor(w[:, 0:H], tg0[:, 0:H], sci[:], ALU.mult)
                e.tensor_tensor(w[:, H:2 * H], tg0[:, H:2 * H], sci[:], ALU.mult)
                e.drain()
                s_gp.inc(e.nop(), 1)

        def presc(e, i, sem):
            e.wait_ge(wls[i % 2].h, 16 * (i // 2 + 1))
            if i >= 2:
                e.wait_ge(wss[i % 2].h, 16 * (i // 2))
            nwrep = nw2[:, i * 2:(i + 1) * 2].unsqueeze(2).broadcast_to([128, 2, H])
            e.tensor_tensor(wosp[i % 2][:, :].rearrange("p (a n) -> p a n", a=2),
                            wofp[i % 2][:, :].rearrange("p (a n) -> p a n", a=2),
                            nwrep, ALU.mult)
            e.drain()
            sem.inc(e.nop(), 1)

        @blk.vector
        def _(e):
            for i in range(0, 32, 2):
                presc(e, i, dv2)

        @blk.gpsimd
        def _(e):
            def load(i):
                wls[i % 2].inc(e.dma_start(
                    out=wofp[i % 2][:, :].rearrange("p (a n) -> p a n", a=2),
                    in_=W_out[i * 256:(i + 1) * 256, :].rearrange("(a p) n -> p a n", p=128)), 16)
            load(0)
            load(1)
            for i in range(32):
                if i % 2 == 1:
                    presc(e, i, gp2)
                else:
                    e.wait_ge(dv2.h, i // 2 + 1)
                if i + 2 < 32:
                    load(i + 2)

        @blk.scalar
        def _(e):
            for i in range(16):
                e.wait_ge(s_gp.h, gp0 + (i + 1))
                p0st[i % 2].inc(e.dma_start(out=wdeq_s[ts(i), :], in_=wdq[i % 2][:, 0:H]), 16)
                p0st[i % 2].inc(e.dma_start(out=wdeq_s[ts(16 + i), :], in_=wdq[i % 2][:, H:2 * H]), 16)

        @blk.sync
        def _(e):
            for i in range(32):
                if i % 2 == 0:
                    e.wait_ge(dv2.h, i // 2 + 1)
                else:
                    e.wait_ge(gp2.h, i // 2 + 1)
                wss[i % 2].inc(e.dma_start(
                    out=wout_s[i * 256:(i + 1) * 256, :].rearrange("(a p) n -> p a n", p=128),
                    in_=wosp[i % 2][:, :].rearrange("p (a n) -> p a n", a=2)), 16)

        @blk.scalar
        def _(e):
            e.wait_ge(p0st[0].h, p0st[0].n)
            e.wait_ge(p0st[1].h, p0st[1].n)
            s_act.inc(e.nop(), 1)

        @blk.sync
        def _(e):
            e.wait_ge(wss[0].h, wss[0].n)
            e.wait_ge(wss[1].h, wss[1].n)
            s_ld.inc(e.nop(), 16)
    phase_barrier()

    def resid_load2(e, sem, buf, r0, r1, cols, pre_ap, main_ap):
        w = cols.stop - cols.start
        if pre_ap is not None:
            if r0 < 3:
                sem.inc(e.dma_start(out=buf[0:3 - r0, 0:w], in_=pre_ap[r0:3, cols]), 16)
            if r1 > 3:
                a = max(r0, 3)
                sem.inc(e.dma_start(out=buf[a - r0:r1 - r0, 0:w], in_=main_ap[a - 3:r1 - 3, cols]), 16)
        else:
            sem.inc(e.dma_start(out=buf[0:r1 - r0, 0:w], in_=main_ap[r0:r1, cols]), 16)

    def resid_load(e, buf, r0, r1, cols, pre_ap, main_ap):
        w = cols.stop - cols.start
        if pre_ap is not None:
            if r0 < 3:
                s_ld.inc(e.dma_start(out=buf[0:3 - r0, 0:w], in_=pre_ap[r0:3, cols]), 16)
            if r1 > 3:
                a = max(r0, 3)
                s_ld.inc(e.dma_start(out=buf[a - r0:r1 - r0, 0:w], in_=main_ap[a - 3:r1 - 3, cols]), 16)
        else:
            s_ld.inc(e.dma_start(out=buf[0:r1 - r0, 0:w], in_=main_ap[r0:r1, cols]), 16)

    def resid_nld(r0, r1, has_pre):
        return ((1 if r0 < 3 else 0) + (1 if r1 > 3 else 0)) if has_pre else 1

    def emit_ln(tag, src_pre, src_main, nrows, dst):
        ntile = (nrows + 127) // 128
        hp = src_pre is not None
        with contextlib.ExitStack() as ph:
            psb = lambda n, shp, dt: ph.enter_context(nc.sbuf_tensor(n, shp, dt))
            xs = [psb(f"{tag}x{i}", [128, H], F32) for i in range(2)]
            disc = psb(f"{tag}d", [128, H], BF16)
            disc2 = psb(f"{tag}d2", [128, H], BF16)
            zpad = psb(f"{tag}z", [16, H], BF16)
            lnt = [psb(f"{tag}l{i}", [128, H], BF16) for i in range(2)]
            cols = psb(f"{tag}c", [128, 8 * ntile], F32)
            lda = [SEM(tag + "la"), SEM(tag + "lb")]
            ldb = [SEM(tag + "ma"), SEM(tag + "mb")]
            stp = [SEM(tag + "sa"), SEM(tag + "sb")]
            dve0, act0 = s_dve.n, s_act.n

            npt = [resid_nld(i * 128, min((i + 1) * 128, nrows), hp) for i in range(ntile)]

            def cum_par(npt_list, i):
                return 16 * sum(npt_list[j] for j in range(i % 2, i + 1, 2))

            @blk.sync
            def _(e):
                for i in range(ntile):
                    if i >= 2:
                        e.wait_ge(s_act.h, act0 + (i - 1))
                    resid_load2(e, lda[i % 2], xs[i % 2], i * 128,
                                min((i + 1) * 128, nrows), slice(0, H), src_pre, src_main)

            @blk.vector
            def _(e):
                e.memset(cols[:], 0.0)
                e.memset(zpad[:], 0.0)
                e.drain()
                for i in range(ntile):
                    r = min((i + 1) * 128, nrows) - i * 128
                    e.wait_ge(lda[i % 2].h, cum_par(npt, i))
                    if i >= 2:
                        e.wait_ge(s_act.h, act0 + i - 1)
                    e.tensor_scalar(disc[0:r, :], xs[i % 2][0:r, :], 1.0, None, ALU.mult, ALU.add,
                                    accum_out=cols[0:r, i:i + 1])
                    e.drain()
                    s_dve.inc(e.nop(), 1)

            @blk.scalar
            def _(e):
                for i in range(ntile):
                    r = min((i + 1) * 128, nrows) - i * 128
                    e.wait_ge(s_dve.h, dve0 + i + 1)
                    e.activation(disc2[0:r, :], xs[i % 2][0:r, :], AF.Square,
                                 accum_out=cols[0:r, ntile + i:ntile + i + 1])
                    e.drain()
                    s_act.inc(e.nop(), 1)
                e.activation(cols[:, 2 * ntile:3 * ntile], cols[:, 0:ntile], AF.Copy, scale=1.0 / H)
                e.drain()
                e.activation(cols[:, 3 * ntile:4 * ntile], cols[:, 2 * ntile:3 * ntile], AF.Square)
                e.drain()
                s_act.inc(e.nop(), 1)

            @blk.vector
            def _(e):
                e.wait_ge(s_act.h, act0 + ntile + 1)
                e.tensor_scalar(cols[:, 4 * ntile:5 * ntile], cols[:, ntile:2 * ntile], 1.0 / H,
                                None, ALU.mult, ALU.bypass)
                e.drain()
                e.tensor_tensor(cols[:, 5 * ntile:6 * ntile], cols[:, 4 * ntile:5 * ntile],
                                cols[:, 3 * ntile:4 * ntile], ALU.subtract)
                e.drain()
                s_dve.inc(e.nop(), 1)

            @blk.scalar
            def _(e):
                e.wait_ge(s_dve.h, dve0 + ntile + 1)
                e.activation(cols[:, 6 * ntile:7 * ntile], cols[:, 5 * ntile:6 * ntile],
                             AF.Ln, bias=epscol[:, 0:1])
                e.drain()
                e.activation(cols[:, 7 * ntile:8 * ntile], cols[:, 6 * ntile:7 * ntile],
                             AF.Exp, scale=-0.5)
                e.drain()
                s_act.inc(e.nop(), 1)

            dve1 = s_dve.n
            @blk.sync
            def _(e):
                for i in range(ntile):
                    if i >= 2:
                        e.wait_ge(s_dve.h, dve1 + (i - 1))
                    else:
                        e.wait_ge(s_act.h, act0 + ntile)
                    resid_load2(e, ldb[i % 2], xs[i % 2], i * 128,
                                min((i + 1) * 128, nrows), slice(0, H), src_pre, src_main)

            @blk.vector
            def _(e):
                e.wait_ge(s_act.h, act0 + ntile + 2)
                for i in range(ntile):
                    r = min((i + 1) * 128, nrows) - i * 128
                    e.wait_ge(ldb[i % 2].h, cum_par(npt, i))
                    if i >= 2:
                        e.wait_ge(stp[i % 2].h, 16 * (i // 2))
                    e.tensor_scalar(lnt[i % 2][0:r, :].rearrange("p (eo a) -> p eo a", eo=2),
                                    xs[i % 2][0:r, :].rearrange("p (a eo) -> p eo a", eo=2),
                                    cols[0:r, 2 * ntile + i:2 * ntile + i + 1],
                                    cols[0:r, 7 * ntile + i:7 * ntile + i + 1],
                                    ALU.subtract, ALU.mult)
                    e.drain()
                    s_dve.inc(e.nop(), 1)

            @blk.scalar
            def _(e):
                for i in range(ntile):
                    r0, r1 = i * 128, min((i + 1) * 128, nrows)
                    e.wait_ge(s_dve.h, dve1 + i + 1)
                    stp[i % 2].inc(e.dma_start(out=dst[r0:r1, :], in_=lnt[i % 2][0:r1 - r0, :]), 16)
                dstpad = dst.tensor.shape[0]
                if dstpad > nrows:
                    stp[0].inc(e.dma_start(out=dst[nrows:dstpad, :],
                                           in_=zpad[0:dstpad - nrows, :]), 16)

            @blk.scalar
            def _(e):
                e.wait_ge(stp[0].h, stp[0].n)
                e.wait_ge(stp[1].h, stp[1].n)
                s_act.inc(e.nop(), 1)

    def emit_ffn(tag, ln_src, ln_rows, resid_pre, resid_main, dst):
        ntile = (ln_rows + 127) // 128
        hp = resid_pre is not None
        qgrp = [list(range(g, min(g + 2, ntile))) for g in range(0, ntile, 2)]
        NPAIR = 2
        with contextlib.ExitStack() as ph:
            psb = lambda n, shp, dt: ph.enter_context(nc.sbuf_tensor(n, shp, dt))
            wq = [psb(f"{tag}wq{i}", [128, NDT * 1024], BF16) for i in range(2)]
            lnq = [psb(f"{tag}lq{i}", [128, NDT * 272], BF16) for i in range(2)]
            xres = [psb(f"{tag}xr{i}", [128, 2048], F32) for i in range(2)]
            ot = [psb(f"{tag}ot{i}", [128, 2048], F32) for i in range(2)]
            pss = [ph.enter_context(nc.psum_tensor(f"{tag}p{i}", [128, 2048], F32)) for i in range(2)]
            ldx = [SEM(tag + "xa"), SEM(tag + "xb")]
            trq = [SEM(tag + "ta"), SEM(tag + "tb")]
            wqs = [SEM(tag + "wqa"), SEM(tag + "wqb")]
            stp = [SEM(tag + "sa"), SEM(tag + "sb")]
            pe0, dve0 = s_pe.n, s_dve.n

            tl = [(p, qi, t) for p in range(NPAIR) for qi, q in enumerate(qgrp) for t in q]
            npt = [resid_nld(t * 128, min((t + 1) * 128, ln_rows), hp) for (_, _, t) in tl]

            def cum_par(i):
                return 16 * sum(npt[j] for j in range(i % 2, i + 1, 2))

            grps = [(p, qi) for p in range(NPAIR) for qi in range(len(qgrp))]
            gidx = {g: i for i, g in enumerate(grps)}
            tiles_thru = {}
            c = 0
            for g in grps:
                c += len(qgrp[g[1]])
                tiles_thru[g] = c

            def tr_par(gi):
                return 16 * NDT * (gi // 2 + 1)

            @blk.gpsimd
            def _(e):
                kglob = 0
                for p in range(NPAIR):
                    if p >= 1:
                        e.wait_ge(s_pe.h, pe0 + p * ntile)
                    for q2 in range(2):
                        nq = 2 * p + q2
                        for piece in range(4):
                            rows = slice(piece * 1024, (piece + 1) * 1024)
                            wqs[q2].inc(e.dma_start(
                                out=wq[q2][:, piece * 8 * 1024:(piece + 1) * 8 * 1024].rearrange(
                                    "p (a n) -> p a n", a=8),
                                in_=wdeq_s[rows, ts(nq, 1024)].rearrange("(a p) n -> p a n", p=128)), 16)
                    for _i in range(ntile):
                        k = kglob
                        t = tl[k][2]
                        r0, r1 = t * 128, min((t + 1) * 128, ln_rows)
                        e.wait_ge(s_dve.h, dve0 + k + 1)
                        stp[k % 2].inc(e.dma_start(out=dst[r0:r1, p * 2048:(p + 1) * 2048],
                                                   in_=ot[k % 2][0:r1 - r0, :]), 16)
                        kglob += 1

            @blk.scalar
            def _(e):
                for gi, g in enumerate(grps):
                    if gi >= 2:
                        e.wait_ge(s_pe.h, pe0 + tiles_thru[grps[gi - 2]])
                    q = qgrp[g[1]]
                    r0 = q[0] * 128
                    rq = ((min((q[-1] + 1) * 128, ln_rows) - r0 + 15) // 16) * 16
                    for d in range(NDT):
                        trq[gi % 2].inc(e.dma_start_transpose(
                            lnq[gi % 2][:, d * 272:d * 272 + rq], ln_src[r0:r0 + rq, ts(d)]), 16)

            @blk.sync
            def _(e):
                for k, (p, qi, t) in enumerate(tl):
                    if k >= 2:
                        e.wait_ge(s_dve.h, dve0 + k - 1)
                    resid_load2(e, ldx[k % 2], xres[k % 2], t * 128,
                                min((t + 1) * 128, ln_rows), slice(p * 2048, (p + 1) * 2048),
                                resid_pre, resid_main)

            @blk.tensor
            def _(e):
                for k, (p, qi, t) in enumerate(tl):
                    gi = gidx[(p, qi)]
                    e.wait_ge(trq[gi % 2].h, tr_par(gi))
                    e.wait_ge(wqs[0].h, 16 * 4 * (p + 1))
                    e.wait_ge(wqs[1].h, 16 * 4 * (p + 1))
                    if k >= 2:
                        e.wait_ge(s_dve.h, dve0 + k - 1)
                    r0 = t * 128
                    m = min(128, ln_rows - r0)
                    toff = r0 - qgrp[qi][0] * 128
                    for d in range(NDT):
                        for q2 in range(2):
                            for c2 in range(2):
                                _mi = e.matmul(pss[k % 2][0:m, q2 * 1024 + c2 * 512:q2 * 1024 + (c2 + 1) * 512],
                                               lnq[gi % 2][:, d * 272 + toff:d * 272 + toff + m],
                                               wq[q2][:, d * 1024 + c2 * 512:d * 1024 + (c2 + 1) * 512],
                                               start=(d == 0), stop=(d == NDT - 1))
                    s_pe.inc(_mi, 1)

            @blk.vector
            def _(e):
                for k, (p, qi, t) in enumerate(tl):
                    m = min(128, ln_rows - t * 128)
                    e.wait_ge(s_pe.h, pe0 + k + 1)
                    e.wait_ge(ldx[k % 2].h, cum_par(k))
                    if k >= 2:
                        e.wait_ge(stp[k % 2].h, 16 * (k // 2))
                    e.scalar_tensor_tensor(ot[k % 2][0:m, :], pss[k % 2][0:m, :], 0.5,
                                           xres[k % 2][0:m, :], ALU.mult, ALU.add)
                    e.drain()
                    s_dve.inc(e.nop(), 1)

            @blk.gpsimd
            def _(e):
                e.wait_ge(stp[0].h, stp[0].n)
                e.wait_ge(stp[1].h, stp[1].n)
                s_gp.inc(e.nop(), 1)


    emit_ln("l1", x_pre, x_main, TP, ln1_s)
    phase_barrier()
    emit_ffn("f1", ln1_s, TP, x_pre, x_main, x1_s)
    phase_barrier()
    emit_ln("l2", x1_s[0:3, :], x1_s[3:TP, :], TP, ln2_s)
    phase_barrier()

    W_in_p = W_in.rearrange("(blk p eo) n -> eo p blk n", blk=16, p=128, eo=2)
    jorder = [130] + list(range(64)) + list(range(64, 130))
    conv_chunks = [(o, min(512, T_MAIN - o)) for o in range(0, T_MAIN, 512)]
    NCV = len(conv_chunks)

    def pe_before_main(k):
        return 5 * 0 + NCHK * k + NCV * min(max(k - 66, 0), 65) + (NCV if k > 130 else 0)

    def pe_before_conv(b):
        kb = 66 + b if b <= 64 else 130
        return NCHK * (kb + 1) + NCV * b

    def act_before_main(k):
        return NCHK * k + NCV * min(max(k - 66, 0), 65)

    def act_before_conv(b):
        kb = 66 + b if b <= 64 else 130
        return NCHK * (kb + 1) + NCV * b

    mid_es = contextlib.ExitStack()
    Bt_r = mid_es.enter_context(nc.sbuf_tensor("Bt_r", [128, T_MAIN], BF16))
    Ct_r = mid_es.enter_context(nc.sbuf_tensor("Ct_r", [128, T_MAIN], BF16))
    dtraw = mid_es.enter_context(nc.sbuf_tensor("dtraw", [NH, TP], F32))
    with contextlib.ExitStack() as ph:
        psb = lambda n, shp, dt: ph.enter_context(nc.sbuf_tensor(n, shp, dt))
        ln2T = psb("ln2T", [128, NDT * TPAD], BF16)
        wi = [psb(f"wi{i}", [128, NDT * 128], BF16) for i in range(2)]
        stripes = [psb(f"strp{i}", [128, TP], BF16) for i in range(2)]
        zstr = [psb(f"zs{i}", [128, TP], BF16) for i in range(2)]
        xfb = [psb(f"xfb{i}", [128, 512], BF16) for i in range(2)]
        diag4 = [psb(f"dg{i}", [128, KTAPS * 128], BF16) for i in range(2)]
        pMain = ph.enter_context(nc.psum_tensor("pMain", [128, 2048], F32))
        pRag = ph.enter_context(nc.psum_tensor("pRag", [128, 512], F32))
        pCv = [ph.enter_context(nc.psum_tensor(f"pCv{i}", [128, 512], F32)) for i in range(2)]
        wis = [SEM("p2wia"), SEM("p2wib")]
        zst = [SEM("p2za"), SEM("p2zb")]
        xfst = [SEM("p2xa"), SEM("p2xb")]
        ld0, ldg0, tr0, pe0, dve0, act0 = s_ld.n, s_ldg.n, s_tr.n, s_pe.n, s_dve.n, s_act.n

        @blk.scalar
        def _(e):
            for d in range(NDT):
                s_tr.inc(e.dma_start_transpose(ln2T[:, d * TPAD:d * TPAD + TPAD],
                                               ln2_s[:, ts(d)]), 16)
        TR_LN2T = s_tr.n

        @blk.gpsimd
        def _(e):
            for k, j in enumerate(jorder):
                if k >= 2:
                    e.wait_ge(s_pe.h, pe0 + pe_before_main(k - 2) + NCHK)
                jw = 64 if j == 130 else 128
                for half in range(2):
                    a0 = half * 16
                    wis[k % 2].inc(e.dma_start(
                        out=wi[k % 2][:, a0 * 128:(a0 + 16) * 128].rearrange(
                            "p (a n) -> p a n", n=128)[:, :, 0:jw],
                        in_=W_in_p[half, :, :, j * 128:j * 128 + jw]), 16)

        @blk.vector
        def _(e):
            for b in range(NCT):
                if b >= 2:
                    e.wait_ge(s_pe.h, pe0 + pe_before_conv(b - 2) + NCV)
                for tap in range(KTAPS):
                    e.tensor_scalar(diag4[b % 2][:, ts(tap)], ident[:],
                                    cw2[:, b * KTAPS + tap:b * KTAPS + tap + 1], None,
                                    ALU.mult, ALU.bypass)
                e.drain()
                s_dve.inc(e.nop(), 1)

        @blk.tensor
        def _(e):
            def emit_conv(b):
                e.wait_ge(s_act.h, act0 + act_before_conv(b))
                e.wait_ge(s_dve.h, dve0 + b + 1)
                st_buf = stripes[(65 + b) % 2]
                for oi, (o, osz) in enumerate(conv_chunks):
                    gci = b * NCV + oi
                    if gci >= 2:
                        bb, oo = divmod(gci - 2, NCV)
                        e.wait_ge(s_act.h, act0 + act_before_conv(bb) + oo + 1)
                    for tap in range(KTAPS):
                        _mi = e.matmul(pCv[gci % 2][:, 0:osz], diag4[b % 2][:, ts(tap)],
                                       st_buf[:, o + tap:o + tap + osz],
                                       start=(tap == 0), stop=(tap == KTAPS - 1))
                    s_pe.inc(_mi, 1)

            for k, j in enumerate(jorder):
                jw = 64 if j == 130 else 128
                e.wait_ge(s_tr.h, TR_LN2T)
                e.wait_ge(wis[k % 2].h, 32 * (k // 2 + 1))
                for ci, (off, sz) in enumerate(CHUNKS):
                    if k >= 1:
                        e.wait_ge(s_act.h, act0 + act_before_main(k - 1) + ci + 1)
                    tgt = pRag[0:jw, 0:3] if sz == 3 else pMain[0:jw, off:off + sz]
                    for d in range(NDT):
                        _mi = e.matmul(tgt, wi[k % 2][:, d * 128:d * 128 + jw],
                                       ln2T[:, d * TPAD + off:d * TPAD + off + sz],
                                       start=(d == 0), stop=(d == NDT - 1))
                    s_pe.inc(_mi, 1)
                if 66 <= k <= 130:
                    emit_conv(k - 66)
            emit_conv(65)

        def Z_ST(k):
            return 16 * ((k - 1) // 2 + 1)

        def XF_ST(gci):
            return 16 * (gci // 2 + 1)

        @blk.scalar
        def _(e):
            def silu_evacs(b):
                ct = b
                for oi, (o, osz) in enumerate(conv_chunks):
                    gci = b * NCV + oi
                    e.wait_ge(s_pe.h, pe0 + pe_before_conv(b) + oi + 1)
                    if ct == 64:
                        dst_ap = Bt_r[:, o:o + osz]
                    elif ct == 65:
                        dst_ap = Ct_r[:, o:o + osz]
                    else:
                        dst_ap = xfb[gci % 2][:, 0:osz]
                        if gci >= 2:
                            e.wait_ge(xfst[gci % 2].h, XF_ST(gci - 2))
                    e.activation(dst_ap, pCv[gci % 2][:, 0:osz], AF.Silu)
                    e.drain()
                    s_act.inc(e.nop(), 1)

            for k, j in enumerate(jorder):
                jw = 64 if j == 130 else 128
                for ci, (off, sz) in enumerate(CHUNKS):
                    e.wait_ge(s_pe.h, pe0 + pe_before_main(k) + ci + 1)
                    src = pRag[0:jw, 0:3] if sz == 3 else pMain[0:jw, off:off + sz]
                    if j == 130:
                        e.activation(dtraw[:, off:off + sz], src, AF.Copy)
                    elif j < 64:
                        if ci == 0 and k >= 3:
                            e.wait_ge(zst[k % 2].h, Z_ST(k - 2))
                        e.activation(zstr[k % 2][:, off:off + sz], src, AF.Copy)
                    else:
                        if ci == 0 and k >= 67:
                            e.wait_ge(s_pe.h, pe0 + pe_before_conv(j - 64 - 2) + NCV)
                        e.activation(stripes[k % 2][:, off:off + sz], src, AF.Copy)
                    e.drain()
                    s_act.inc(e.nop(), 1)
                if 66 <= k <= 130:
                    silu_evacs(k - 66)
            silu_evacs(65)

        @blk.sync
        def _(e):
            for k, j in enumerate(jorder):
                if j < 64:
                    e.wait_ge(s_act.h, act0 + act_before_main(k) + NCHK)
                    zst[k % 2].inc(e.dma_start(out=z_s[ts(j), :], in_=zstr[k % 2][:, 3:TP]), 16)
            for b in range(64):
                for oi, (o, osz) in enumerate(conv_chunks):
                    gci = b * NCV + oi
                    e.wait_ge(s_act.h, act0 + act_before_conv(b) + oi + 1)
                    xfst[gci % 2].inc(e.dma_start(out=xf_s[ts(b), o:o + osz],
                                                  in_=xfb[gci % 2][:, 0:osz]), 16)
        @blk.sync
        def _(e):
            for p in range(2):
                e.wait_ge(zst[p].h, zst[p].n)
                e.wait_ge(xfst[p].h, xfst[p].n)
            s_ld.inc(e.nop(), 16)
    phase_barrier()


    ssd_es = contextlib.ExitStack()
    ssb = lambda n, shp, dt: ssd_es.enter_context(nc.sbuf_tensor(n, shp, dt))
    seed = ssb("seed", [128, DSSM], BF16)
    ssd2_es = contextlib.ExitStack()
    ssb2 = lambda n, shp, dt: ssd2_es.enter_context(nc.sbuf_tensor(n, shp, dt))
    lamb = ssb2("lamb", [128, NH * NCH], F32)
    wTp = ssb2("wTp", [128, NCH * 64], F32)
    colterm = ssb2("colterm", [128, NCH * 64], F32)
    logdAT = ssb2("logdAT", [128, NCH * 64], BF16)
    hslab = ssb2("hslab", [128, DSSM], F32)
    hbf = ssb2("hbf", [128, DSSM], BF16)
    DIall = ssb2("DIall", [128, NH * 128], BF16)

    with contextlib.ExitStack() as ph:
        psb = lambda n, shp, dt: ph.enter_context(nc.sbuf_tensor(n, shp, dt))
        dtv = psb("dtv", [NH, T_MAIN], F32)
        logdt = psb("logdt", [NH, T_MAIN], F32)
        logdA_b = psb("logdA_b", [NH, T_MAIN], BF16)
        cumhs = psb("cumhs", [NH, T_MAIN], F32)
        cumabs = psb("cumabs", [NH, T_MAIN], F32)
        w_hs = psb("w_hs", [NH, T_MAIN], F32)
        ta = psb("ta", [NH, T_MAIN], F32)
        tb = psb("tb", [NH, T_MAIN], F32)
        tc = psb("tcx", [NH, T_MAIN], F32)
        td = psb("td", [NH, T_MAIN], F32)
        tbf = psb("tbf", [NH, T_MAIN], BF16)
        ones64 = psb("ones64", [NH, L], F32)
        negA = psb("negA", [NH, 1], F32)
        lam64 = psb("lam64", [NH, NCH], F32)
        lamflat = psb("lamflat", [1, NH * NCH], F32)
        ldtT = psb("ldtT", [128, 64], F32)
        Drow = psb("Drow", [1, 64], F32)
        Dbc = psb("Dbc", [128, 64], F32)
        pTb = ph.enter_context(nc.psum_tensor("pTb", [128, 64], BF16))
        pTf = ph.enter_context(nc.psum_tensor("pTf", [128, 64], F32))
        pTf2 = ph.enter_context(nc.psum_tensor("pTf2", [128, 64], F32))
        pTlam = ph.enter_context(nc.psum_tensor("pTlam", [128, 512], F32))
        dve0, act0, gp0, pe0, ldg0 = s_dve.n, s_act.n, s_gp.n, s_pe.n, s_ldg.n

        @blk.vector
        def _(e):
            e.memset(ones64[:], 1.0)
            e.memset(hslab[:], 0.0)
            e.tensor_scalar(ta[:], dtraw[:, 3:TP], hv[:, 0:1], None, ALU.add, ALU.bypass)
            e.drain()
            e.tensor_scalar(tb[:], ta[:], 0.0, None, ALU.max, ALU.bypass)
            e.tensor_scalar(tc[:], ta[:], -1.0, None, ALU.mult, ALU.bypass)
            e.drain()
            e.tensor_tensor(tc[:], ta[:], tc[:], ALU.max)
            e.drain()
            s_dve.inc(e.nop(), 1)

        @blk.scalar
        def _(e):
            e.wait_ge(s_dve.h, dve0 + 1)
            e.activation(td[:], tc[:], AF.Exp, scale=-1.0)
            e.drain()
            e.activation(tc[:], td[:], AF.Ln, bias=onecolf2[0:NH, 0:1])
            e.drain()
            e.activation(negA[:], hv[:, 1:2], AF.Exp)
            e.drain()
            s_act.inc(e.nop(), 1)

        @blk.vector
        def _(e):
            e.wait_ge(s_act.h, act0 + 1)
            e.tensor_tensor(dtv[:], tb[:], tc[:], ALU.add)
            e.drain()
            e.tensor_scalar(ta[:], dtv[:], 1e-19, None, ALU.max, ALU.bypass)
            e.tensor_scalar(negA[:], negA[:], -1.0, None, ALU.mult, ALU.bypass)
            e.drain()
            e.tensor_scalar(tb[:], dtv[:], negA[:, 0:1], None, ALU.mult, ALU.bypass)
            e.drain()
            e.tensor_copy(logdA_b[:], tb[:])
            for k in range(NCH):
                e.tensor_tensor_scan(cumhs[:, ts(k, L)], ones64[:], tb[:, ts(k, L)],
                                     0.0, ALU.mult, ALU.add)
            e.drain()
            for k in range(NCH):
                init = 0.0 if k == 0 else cumabs[:, k * L - 1:k * L]
                e.tensor_tensor_scan(cumabs[:, ts(k, L)], ones64[:], tb[:, ts(k, L)],
                                     init, ALU.mult, ALU.add)
                e.drain()
            s_dve.inc(e.nop(), 1)

        @blk.scalar
        def _(e):
            e.wait_ge(s_dve.h, dve0 + 2)
            e.activation(logdt[:], ta[:], AF.Ln)
            e.drain()
            e.activation(tbf[:], cumhs[:], AF.Exp)
            e.drain()
            s_act.inc(e.nop(), 1)
            for k in range(NCH):
                e.activation(ta[:, ts(k, L)], cumhs[:, ts(k, L)], AF.Exp, scale=-1.0,
                             bias=cumhs[:, k * L + L - 1:k * L + L])
            e.drain()
            e.activation(lam64[:], cumhs[:, L - 1::L], AF.Exp)
            e.drain()
            s_act.inc(e.nop(), 1)

        @blk.gpsimd
        def _(e):
            e.wait_ge(s_act.h, act0 + 2)
            s_ldg.inc(e.dma_start(out=bncA, in_=tbf[:]), 16)
            e.wait_ge(s_ldg.h, ldg0 + 16)

        @blk.vector
        def _(e):
            e.wait_ge(s_act.h, act0 + 3)
            e.wait_ge(s_ldg.h, ldg0 + 16)
            e.tensor_tensor(w_hs[:], ta[:], dtv[:], ALU.mult)
            e.drain()
            s_dve.inc(e.nop(), 1)

        @blk.scalar
        def _(e):
            e.wait_ge(s_dve.h, dve0 + 3)
            e.activation(tbf[:], cumabs[:], AF.Exp)
            e.drain()
            s_act.inc(e.nop(), 1)

        @blk.gpsimd
        def _(e):
            e.wait_ge(s_act.h, act0 + 4)
            s_ldg.inc(e.dma_start(out=bncB, in_=tbf[:]), 16)
            s_ldg.inc(e.dma_start(out=bncC, in_=lam64[:]), 16)
            e.wait_ge(s_ldg.h, ldg0 + 48)
            s_ldg.inc(e.dma_start(out=lamflat[:],
                                  in_=bncC.rearrange("a b -> (a b)").unsqueeze(0)), 16)
            e.wait_ge(s_ldg.h, ldg0 + 64)
            s_gp.inc(e.nop(), 1)

        NLAMMM = (NH * NCH + 511) // 512
        @blk.tensor
        def _(e):
            e.wait_ge(s_gp.h, gp0 + 1)
            for i in range(NLAMMM):
                if i >= 1:
                    e.wait_ge(s_act.h, act0 + 4 + i)
                w = min(512, NH * NCH - i * 512)
                _mi = e.matmul(pTlam[:, 0:w], onescolf[:], lamflat[0:1, i * 512:i * 512 + w],
                               start=True, stop=True)
                s_pe.inc(_mi, 1)

        @blk.scalar
        def _(e):
            for i in range(NLAMMM):
                e.wait_ge(s_pe.h, pe0 + i + 1)
                w = min(512, NH * NCH - i * 512)
                e.activation(lamb[:, i * 512:i * 512 + w], pTlam[:, 0:w], AF.Copy)
                e.drain()
                s_act.inc(e.nop(), 1)

        pe1, dve1, act1 = pe0 + NLAMMM, dve0 + 3, act0 + 4 + NLAMMM
        @blk.tensor
        def _(e):
            e.wait_ge(s_dve.h, dve1)
            e.wait_ge(s_act.h, act1)
            for k in range(NCH):
                if k > 0:
                    e.wait_ge(s_dve.h, dve1 + 2 * k)
                    e.wait_ge(s_act.h, act1 + 2 * k)
                e.transpose(pTb[:, :], logdA_b[:, ts(k, L)], ident[0:64, 0:64])
                _mi = e.transpose(pTf[:, :], logdt[:, ts(k, L)], identf[0:64, 0:64])
                s_pe.inc(_mi, 1)
                e.wait_ge(s_dve.h, dve1 + 2 * k + 1)
                e.wait_ge(s_act.h, act1 + 2 * k + 1)
                e.transpose(pTf2[:, :], w_hs[:, ts(k, L)], identf[0:64, 0:64])
                _mi = e.matmul(pTf[:, :], triu[:], logdAT[:, k * 64:(k + 1) * 64],
                               start=True, stop=True)
                s_pe.inc(_mi, 1)

        @blk.vector
        def _(e):
            for k in range(NCH):
                e.wait_ge(s_pe.h, pe1 + 2 * k + 1)
                e.tensor_copy(logdAT[:, k * 64:(k + 1) * 64], pTb[:, :])
                e.drain()
                s_dve.inc(e.nop(), 1)
                e.wait_ge(s_pe.h, pe1 + 2 * k + 2)
                e.wait_ge(s_act.h, act1 + 2 * k + 1)
                e.tensor_tensor(colterm[:, k * 64:(k + 1) * 64], ldtT[:],
                                pTf[:, :], ALU.subtract)
                e.drain()
                s_dve.inc(e.nop(), 1)

        @blk.scalar
        def _(e):
            for k in range(NCH):
                e.wait_ge(s_pe.h, pe1 + 2 * k + 1)
                e.activation(ldtT[:], pTf[:, :], AF.Copy)
                e.drain()
                s_act.inc(e.nop(), 1)
                e.wait_ge(s_pe.h, pe1 + 2 * k + 2)
                e.wait_ge(s_dve.h, dve1 + 2 * k + 2)
                e.activation(wTp[:, k * 64:(k + 1) * 64], pTf2[:, :], AF.Copy)
                e.drain()
                s_act.inc(e.nop(), 1)

        pe2, act2, dve2 = pe1 + 2 * NCH, act1 + 2 * NCH, dve1 + 2 * NCH
        @blk.tensor
        def _(e):
            e.wait_ge(s_act.h, act2)
            _mi = e.transpose(pTf[0:1, 0:64], hv[:, 2:3], identf[0:64, 0:64])
            s_pe.inc(_mi, 1)
            e.wait_ge(s_act.h, act2 + 1)
            _mi = e.matmul(pTf2[:, 0:64], onescolf[:], Drow[:], start=True, stop=True)
            s_pe.inc(_mi, 1)

        @blk.scalar
        def _(e):
            e.wait_ge(s_pe.h, pe2 + 1)
            e.activation(Drow[:], pTf[0:1, 0:64], AF.Copy)
            e.drain()
            s_act.inc(e.nop(), 1)
            e.wait_ge(s_pe.h, pe2 + 2)
            e.activation(Dbc[:], pTf2[:, 0:64], AF.Copy)
            e.drain()
            s_act.inc(e.nop(), 1)

        @blk.vector
        def _(e):
            e.wait_ge(s_act.h, act2 + 2)
            for h in range(NH):
                e.tensor_scalar(DIall[:, ts(h)], ident[:], Dbc[:, h:h + 1], None,
                                ALU.mult, ALU.bypass)
            e.drain()
            s_dve.inc(e.nop(), 1)
    phase_barrier()

    with contextlib.ExitStack() as ph:
        psb = lambda n, shp, dt: ph.enter_context(nc.sbuf_tensor(n, shp, dt))
        xq = [psb(f"xq{i}", [128, DSSM], BF16) for i in range(2)]
        smb = psb("smb", [128, L], BF16)
        btT = psb("btT", [128, 128], BF16)
        striB = [psb(f"stri{i}", [128, 1024], BF16) for i in range(2)]
        xwB = [psb(f"xw{i}", [128, 1024], BF16) for i in range(2)]
        EclB = [psb(f"ecl{i}", [128, 1024], BF16) for i in range(2)]
        EexpB = [psb(f"eexp{i}", [128, 1024], BF16) for i in range(2)]
        mppB = [psb(f"mpp{i}", [128, 1024], BF16) for i in range(2)]
        cdecB = [psb(f"cdec{i}", [128, 1024], BF16) for i in range(2)]
        ystB = [psb(f"yst{i}", [128, 1024], BF16) for i in range(2)]
        expfk = psb("expfk", [1, NH * L], BF16)
        pE = ph.enter_context(nc.psum_tensor("pE", [128, 1024], F32))
        pY = ph.enter_context(nc.psum_tensor("pY", [128, 1024], F32))
        pBS = ph.enter_context(nc.psum_tensor("pBS", [128, 1024], F32))
        pSC = ph.enter_context(nc.psum_tensor("pSC", [128, L], F32))
        pBT = ph.enter_context(nc.psum_tensor("pBT", [128, 128], BF16))
        xqs = [SEM("p3xa"), SEM("p3xb")]
        yst = [SEM("p3ya"), SEM("p3yb")]
        efs = SEM("p3ef")
        pe0, dve0, act0, gp0, tr0, ld0 = s_pe.n, s_dve.n, s_act.n, s_gp.n, s_tr.n, s_ld.n

        def pe_at(k, idx):
            return pe0 + 42 * k + idx

        def dve_at(k, idx):
            return dve0 + 42 * k + idx

        def act_at(k, idx):
            return act0 + 17 * k + idx

        def gp_at(k, idx):
            return gp0 + 8 * k + idx

        def prev2(k, g):
            gg = g - 2
            if gg < 0:
                return k - 1, gg + 8
            return k, gg

        @blk.tensor
        def _(e):
            for k in range(NCH):
                if k >= 1:
                    e.wait_ge(s_dve.h, dve_at(k - 1, 1))
                _mi = e.matmul(pSC[:, :], Bt_r[:, ts(k, L)], Ct_r[:, ts(k, L)], start=True, stop=True)
                s_pe.inc(_mi, 1)
                if k >= 1:
                    e.wait_ge(s_dve.h, dve_at(k - 1, 2))
                _mi = e.transpose(pBT[:, :], Bt_r[:, ts(k, L)], ident[:])
                s_pe.inc(_mi, 1)
                for g in range(8):
                    e.wait_ge(s_dve.h, dve_at(k, 3 + 5 * g))
                    kk, gg = (k, g - 1) if g >= 1 else (k - 1, 7)
                    if kk >= 0:
                        e.wait_ge(s_dve.h, dve_at(kk, 5 + 5 * gg))
                    for c2 in range(2):
                        _mi = e.matmul(pE[:, ts(c2, 512)], onesfull[:],
                                       striB[g % 2][:, ts(c2, 512)], start=True, stop=True)
                    s_pe.inc(_mi, 1)
                    kk, gg = (k, g - 1) if g >= 1 else (k - 1, 7)
                    if kk >= 0:
                        e.wait_ge(s_dve.h, dve_at(kk, 7 + 5 * gg))
                    e.wait_ge(efs.h, 16 * (k + 1))
                    for hh in range(8):
                        h = g * 8 + hh
                        _mi = e.matmul(pBS[:, ts(hh)], onescol[:],
                                       expfk[0:1, ts(h)],
                                       start=(hh % 4 == 0), stop=(hh % 4 == 3),
                                       skip_group_check=True)
                    s_pe.inc(_mi, 1)
                    e.wait_ge(s_gp.h, gp_at(k, 1 + g))
                    kk, gg = (k, g - 1) if g >= 1 else (k - 1, 7)
                    if kk >= 0:
                        e.wait_ge(s_act.h, act_at(kk, 3 + 2 * gg))
                    for hh in range(8):
                        h = g * 8 + hh
                        _mi = e.matmul(pY[:, ts(hh)], xq[k % 2][:, ts(h)], mppB[g % 2][:, ts(hh)],
                                       start=(hh % 4 == 0), stop=False, skip_group_check=True)
                    s_pe.inc(_mi, 1)
                    e.wait_ge(s_dve.h, dve_at(k, 6 + 5 * g))
                    e.wait_ge(s_act.h, act_at(k, 1))
                    for hh in range(8):
                        h = g * 8 + hh
                        _mi = e.matmul(pY[:, ts(hh)], hbf[:, ts(h)], cdecB[g % 2][:, ts(hh)],
                                       start=False, stop=(hh % 4 == 3), skip_group_check=True)
                    s_pe.inc(_mi, 1)
                    e.wait_ge(s_dve.h, dve_at(k, 4 + 5 * g))
                    for c2 in range(2):
                        _mi = e.matmul(pBS[:, ts(c2, 512)], btT[:], xwB[g % 2][:, ts(c2, 512)],
                                       start=True, stop=True, skip_group_check=True)
                    s_pe.inc(_mi, 1)

        @blk.vector
        def _(e):
            for k in range(NCH):
                e.wait_ge(s_pe.h, pe_at(k, 1))
                if k >= 1:
                    e.wait_ge(s_gp.h, gp_at(k - 1, 8))
                e.tensor_tensor(smb[:], pSC[:, :], triu[:], ALU.mult)
                e.drain()
                s_dve.inc(e.nop(), 1)
                e.wait_ge(s_pe.h, pe_at(k, 2))
                if k >= 1:
                    e.wait_ge(s_pe.h, pe_at(k - 1, 42))
                e.tensor_copy(btT[:], pBT[:, :])
                e.drain()
                s_dve.inc(e.nop(), 1)
                for g in range(8):
                    kk, gg = prev2(k, g)
                    if kk >= 0:
                        e.wait_ge(s_pe.h, pe_at(kk, 3 + 5 * gg))
                    tri_rep = triu[:, :].unsqueeze(1).broadcast_to([128, 8, 128])
                    ldA_rep = logdAT[:, k * 64 + g * 8:k * 64 + (g + 1) * 8].unsqueeze(2).broadcast_to([128, 8, 128])
                    e.tensor_tensor(striB[g % 2][:, :].rearrange("p (a b) -> p a b", a=8),
                                    tri_rep, ldA_rep, ALU.mult)
                    e.drain()
                    s_dve.inc(e.nop(), 1)
                    if g == 0:
                        e.wait_ge(xqs[k % 2].h, 16 * (k // 2 + 1))
                    kk, gg = prev2(k, g)
                    if kk >= 0:
                        e.wait_ge(s_pe.h, pe_at(kk, 7 + 5 * gg))
                    w_rep = wTp[:, k * 64 + g * 8:k * 64 + (g + 1) * 8].unsqueeze(2).broadcast_to([128, 8, 128])
                    e.tensor_tensor(xwB[g % 2][:, :].rearrange("p (a b) -> p a b", a=8),
                                    xq[k % 2][:, ts(g, 1024)].rearrange("p (a b) -> p a b", a=8),
                                    w_rep, ALU.mult)
                    e.drain()
                    s_dve.inc(e.nop(), 1)
                    e.wait_ge(s_pe.h, pe_at(k, 3 + 5 * g))
                    kk, gg = prev2(k, g)
                    if kk >= 0:
                        e.wait_ge(s_act.h, act_at(kk, 2 + 2 * gg))
                    ct_rep2 = colterm[:, k * 64 + g * 8:k * 64 + (g + 1) * 8].unsqueeze(2).broadcast_to([128, 8, 128])
                    e.tensor_tensor(EclB[g % 2][:, :].rearrange("p (a b) -> p a b", a=8),
                                    pE[:, :].rearrange("p (a b) -> p a b", a=8),
                                    ct_rep2, ALU.add)
                    e.drain()
                    e.tensor_scalar(EclB[g % 2][:, :], EclB[g % 2][:, :], 60.0, None,
                                    ALU.min, ALU.bypass)
                    e.drain()
                    s_dve.inc(e.nop(), 1)
                    e.wait_ge(s_pe.h, pe_at(k, 4 + 5 * g))
                    kk, gg = prev2(k, g)
                    if kk >= 0:
                        e.wait_ge(s_pe.h, pe_at(kk, 6 + 5 * gg))
                    ct_rep = Ct_r[:, ts(k, L)].unsqueeze(1).broadcast_to([128, 8, 128])
                    e.tensor_tensor(cdecB[g % 2][:, :].rearrange("p (a b) -> p a b", a=8),
                                    pBS[:, :].rearrange("p (a b) -> p a b", a=8),
                                    ct_rep, ALU.mult)
                    e.drain()
                    s_dve.inc(e.nop(), 1)
                    e.wait_ge(s_pe.h, pe_at(k, 7 + 5 * g))
                    e.wait_ge(s_act.h, act_at(k, 1))
                    lam_rep = lamb[:, k::NCH][:, g * 8:(g + 1) * 8].unsqueeze(2).broadcast_to([128, 8, 128])
                    e.tensor_tensor(hslab[:, ts(g, 1024)].rearrange("p (a b) -> p a b", a=8),
                                    hslab[:, ts(g, 1024)].rearrange("p (a b) -> p a b", a=8),
                                    lam_rep, ALU.mult)
                    e.drain()
                    e.tensor_tensor(hslab[:, ts(g, 1024)], hslab[:, ts(g, 1024)],
                                    pBS[:, :], ALU.add)
                    e.drain()
                    s_dve.inc(e.nop(), 1)

        @blk.scalar
        def _(e):
            for k in range(2):
                if k < NCH:
                    xqs[k % 2].inc(e.dma_start_transpose(xq[k % 2][:, :], xf_s[:, ts(k, L)]), 16)
            for k in range(NCH):
                if k >= 1:
                    e.wait_ge(s_dve.h, dve_at(k - 1, 42))
                e.activation(hbf[:], hslab[:], AF.Copy)
                e.drain()
                s_act.inc(e.nop(), 1)
                for g in range(8):
                    e.wait_ge(s_dve.h, dve_at(k, 5 + 5 * g))
                    kk, gg = prev2(k, g)
                    if kk >= 0:
                        e.wait_ge(s_gp.h, gp_at(kk, 1 + gg))
                    e.activation(EexpB[g % 2][:, :], EclB[g % 2][:, :], AF.Exp)
                    e.drain()
                    s_act.inc(e.nop(), 1)
                    e.wait_ge(s_pe.h, pe_at(k, 6 + 5 * g))
                    kk, gg = prev2(k, g)
                    if kk >= 0:
                        e.wait_ge(yst[gg % 2].h, 16 * ((8 * kk + gg) // 2 + 1))
                    e.activation(ystB[g % 2][:, :], pY[:, :], AF.Copy)
                    e.drain()
                    s_act.inc(e.nop(), 1)
                if k + 2 < NCH:
                    e.wait_ge(s_dve.h, dve_at(k, 39))
                    xqs[k % 2].inc(e.dma_start_transpose(xq[k % 2][:, :], xf_s[:, ts(k + 2, L)]), 16)

        @blk.gpsimd
        def _(e):
            for k in range(NCH):
                for g in range(8):
                    e.wait_ge(s_act.h, act_at(k, 2 + 2 * g))
                    e.wait_ge(s_dve.h, dve_at(k, 1))
                    kk, gg = prev2(k, g)
                    if kk >= 0:
                        e.wait_ge(s_pe.h, pe_at(kk, 5 + 5 * gg))
                    sm_rep = smb[:, :].unsqueeze(1).broadcast_to([128, 8, 128])
                    e.tensor_tensor(mppB[g % 2][:, :].rearrange("p (a b) -> p a b", a=8),
                                    EexpB[g % 2][:, :].rearrange("p (a b) -> p a b", a=8),
                                    sm_rep, ALU.mult)
                    e.drain()
                    e.tensor_tensor(mppB[g % 2][:, :], mppB[g % 2][:, :],
                                    DIall[:, ts(g, 1024)], ALU.add)
                    e.drain()
                    s_gp.inc(e.nop(), 1)

        @blk.sync
        def _(e):
            for k in range(NCH):
                if k >= 1:
                    e.wait_ge(s_pe.h, pe_at(k - 1, 39))
                efs.inc(e.dma_start(
                    out=expfk[0:1, :].rearrange("p (h t) -> p h t", h=NH),
                    in_=bncA[:, ts(k, L)].unsqueeze(0)), 16)
                for g in range(8):
                    e.wait_ge(s_act.h, act_at(k, 3 + 2 * g))
                    yst[g % 2].inc(e.dma_start(
                        out=yraw_s[g * 1024:(g + 1) * 1024, ts(k, L)].rearrange(
                            "(h p) t -> p h t", p=128),
                        in_=ystB[g % 2][:, :].rearrange("p (h t) -> p h t", h=8)), 16)

        @blk.sync
        def _(e):
            e.wait_ge(yst[0].h, yst[0].n)
            e.wait_ge(yst[1].h, yst[1].n)
            e.wait_ge(efs.h, efs.n)
            s_ld.inc(e.nop(), 16)
    phase_barrier()

    with contextlib.ExitStack() as ph:
        htmp = ph.enter_context(nc.sbuf_tensor("htmp", [128, DSSM], F32))
        dve0, gp0, cc0, ldg0 = s_dve.n, s_gp.n, s_cc.n, s_ldg.n

        @blk.vector
        def _(e):
            e.tensor_scalar(htmp[:], hslab[:], fl[:, 0:1], None, ALU.mult, ALU.bypass)
            e.drain()
            s_dve.inc(e.nop(), 1)

        @blk.gpsimd
        def _(e):
            e.wait_ge(s_dve.h, dve0 + 1)
            s_ldg.inc(e.dma_start(out=cc_in, in_=htmp[:]), 16)
            e.wait_ge(s_ldg.h, ldg0 + 16)
            cci = e.collective_compute(
                "AllReduce", ALU.add,
                replica_groups=[[2 * i, 2 * i + 1] for i in range(n_cc // 2)],
                ins=[cc_in.opt()], outs=[cc_out.opt()],
            )
            cci.then_inc(s_cc.h, 1)
            s_cc.n += 1
            e.wait_ge(s_cc.h, cc0 + 1)
            s_ldg.inc(e.dma_start(out=htmp[:], in_=cc_out), 16)
            e.wait_ge(s_ldg.h, ldg0 + 32)
            s_gp.inc(e.nop(), 1)

        @blk.vector
        def _(e):
            e.wait_ge(s_gp.h, gp0 + 1)
            e.tensor_scalar(seed[:], htmp[:], fl[:, 1:2], None, ALU.mult, ALU.bypass)
            e.drain()
            s_dve.inc(e.nop(), 1)
    phase_barrier()
    ssd2_es.close()

    assert T_MAIN % 512 == 0
    NTQ = NCH // 4
    host = {k: [] for k in range(NCH)}
    tail34 = []
    for tq in range(NTQ):
        items = [("g", tq, 0)] + [("u", tq, o8) for o8 in range(8)]
        if 4 * tq + 7 < NCH:
            host[4 * tq + 4] += items[0:3]
            host[4 * tq + 5] += items[3:5]
            host[4 * tq + 6] += items[5:7]
            host[4 * tq + 7] += items[7:9]
        else:
            tail34 += items
    ordered34 = [it for k in range(NCH) for it in host[k]] + tail34

    pe4_g, pe4_u, dve4_g, dve4_e, act4_g = {}, {}, {}, {}, {}
    _pe4 = _dve4 = _act4 = 0
    for it in ordered34:
        if it[0] == "g":
            _pe4 += 1; pe4_g[it[1]] = _pe4
            _dve4 += 1; dve4_g[it[1]] = _dve4
            _act4 += 1; act4_g[it[1]] = _act4
        else:
            ui = 8 * it[1] + it[2]
            for p in range(8):
                _pe4 += 1; pe4_u[(ui, p)] = _pe4
            for tt in range(4):
                _dve4 += 1; dve4_e[(ui, tt)] = _dve4

    with contextlib.ExitStack() as ph:
        psb = lambda n, shp, dt: ph.enter_context(nc.sbuf_tensor(n, shp, dt))
        yslab = psb("yslab", [128, DSSM], BF16)
        zslab = psb("zslab", [128, DSSM], BF16)
        silz = psb("silz", [128, DSSM], BF16)
        yg2 = [psb(f"yg2{i}", [128, 1024], BF16) for i in range(2)]
        ycB = [psb(f"yc{i}", [128, 1024], F32) for i in range(2)]
        cdaB = [psb(f"cda{i}", [128, 1024], BF16) for i in range(2)]
        expabsk = psb("expabsk", [33, 4096], BF16)
        ygr = psb("ygr", [128, NZT * 512], BF16)
        wosB = [psb(f"wos{i}", [128, 8 * 512], BF16) for i in range(2)]
        x1b = [psb(f"x1b{i}", [128, 512], F32) for i in range(4)]
        x2b = [psb(f"x2b{i}", [128, 512], F32) for i in range(4)]
        rsq = psb("rsq", [128, 4], F32)
        rln = psb("rln", [128, 4], F32)
        pB2 = ph.enter_context(nc.psum_tensor("pB2", [128, 1024], F32))
        pD = ph.enter_context(nc.psum_tensor("pD", [128, 1024], F32))
        pQ = [ph.enter_context(nc.psum_tensor(f"pQ{i}", [128, 512], F32)) for i in range(4)]
        eas = SEM("pbea")
        yls = SEM("pbyl")
        zls = SEM("pbzl")
        ygst = [SEM("pbga"), SEM("pbgb")]
        s_pe4 = SEM("spe4")
        s_dve4 = SEM("sdv4")
        s_act4 = SEM("sac4")
        p4yg = SEM("p4yg")
        p4w = [SEM("p4wa"), SEM("p4wb")]
        p4x1 = [SEM(f"p4x{i}") for i in range(4)]
        p4st = [SEM(f"p4s{i}") for i in range(4)]
        pe0, dve0, act0 = s_pe.n, s_dve.n, s_act.n

        def pe3(k, i):
            return pe0 + 16 * k + i

        def dve3(k, i):
            return dve0 + 24 * k + i

        def act3(k):
            return act0 + k + 1

        def prev2b(k, g):
            gg = g - 2
            if gg < 0:
                return k - 1, gg + 8
            return k, gg

        def ygr_t(c, tt):
            return ygr[:, (c * 4 + tt) * 128:(c * 4 + tt) * 128 + 128]

        def x2_stores(e, it):
            tq, o8 = it[1], it[2]
            ui = 8 * tq + o8
            for tt in range(4):
                m = 4 * ui + tt
                tg = tq * 4 + tt
                e.wait_ge(s_dve4.h, dve4_e[(ui, tt)])
                p4st[m % 4].inc(e.dma_start(
                    out=x2_s[tg * 128:tg * 128 + 128, ts(o8, 512)],
                    in_=x2b[m % 4][:, :]), 16)

        @blk.sync
        def _(e):
            for k in range(NCH):
                if k >= 1:
                    e.wait_ge(s_pe.h, pe3(k - 1, 15))
                eas.inc(e.dma_start(
                    out=expabsk[0:1, :].rearrange("p (h t) -> p h t", h=32),
                    in_=bncB[0:32, ts(k, L)].unsqueeze(0)), 16)
                eas.inc(e.dma_start(
                    out=expabsk[32:33, :].rearrange("p (h t) -> p h t", h=32),
                    in_=bncB[32:64, ts(k, L)].unsqueeze(0)), 16)
                if k >= 1:
                    e.wait_ge(s_dve.h, dve3(k - 1, 24))
                for g in range(8):
                    yls.inc(e.dma_start(
                        out=yslab[:, ts(g, 1024)].rearrange("p (h t) -> p h t", h=8),
                        in_=yraw_s[g * 1024:(g + 1) * 1024, ts(k, L)].rearrange(
                            "(h p) t -> p h t", p=128)), 16)
                if k >= 1:
                    e.wait_ge(s_act.h, act3(k - 1))
                for g in range(8):
                    zls.inc(e.dma_start(
                        out=zslab[:, ts(g, 1024)].rearrange("p (h t) -> p h t", h=8),
                        in_=z_s[g * 1024:(g + 1) * 1024, ts(k, L)].rearrange(
                            "(h p) t -> p h t", p=128)), 16)
                for g in range(8):
                    e.wait_ge(s_dve.h, dve3(k, 3 + 3 * g))
                    ygst[g % 2].inc(e.dma_start(
                        out=yg_s[g * 1024:(g + 1) * 1024, ts(k, L)].rearrange(
                            "(h p) t -> p h t", p=128),
                        in_=yg2[g % 2][:, :].rearrange("p (h t) -> p h t", h=8)), 16)
                if k % 4 == 3 and (k // 4) < NTQ:
                    tq = k // 4
                    e.wait_ge(ygst[0].h, 16 * 4 * (k + 1))
                    e.wait_ge(ygst[1].h, 16 * 4 * (k + 1))
                    if tq >= 1:
                        e.wait_ge(s_pe4.h, pe4_u[(8 * tq - 1, 7)])
                    for g in range(8):
                        p4yg.inc(e.dma_start(
                            out=ygr[:, g * 8 * 4 * 128:(g + 1) * 8 * 4 * 128].rearrange(
                                "p (a t) -> p a t", a=8),
                            in_=yg_s[g * 1024:(g + 1) * 1024,
                                     tq * 512:(tq + 1) * 512].rearrange(
                                "(a p) t -> p a t", p=128)), 16)
                for it in host[k]:
                    if it[0] == "u":
                        x2_stores(e, it)
            for it in tail34:
                if it[0] == "u":
                    x2_stores(e, it)

        @blk.sync
        def _(e):
            e.wait_ge(ygst[0].h, ygst[0].n)
            e.wait_ge(ygst[1].h, ygst[1].n)
            e.wait_ge(eas.h, eas.n)
            for i in range(4):
                e.wait_ge(p4st[i].h, p4st[i].n)
            s_ld.inc(e.nop(), 16)

        @blk.gpsimd
        def _(e):
            for it in ordered34:
                if it[0] != "u":
                    continue
                tq, o8 = it[1], it[2]
                ui = 8 * tq + o8
                for p in range(8):
                    q = 8 * ui + p
                    if q >= 2:
                        qq = q - 2
                        e.wait_ge(s_pe4.h, pe4_u[(qq // 8, qq % 8)])
                    p4w[q % 2].inc(e.dma_start(
                        out=wosB[q % 2][:, :].rearrange("p (a n) -> p a n", a=8),
                        in_=wout_s[p * 1024:(p + 1) * 1024, ts(o8, 512)].rearrange(
                            "(a p) n -> p a n", p=128)), 16)

        def p4_pe_pieces(it):
            if it[0] == "g":
                tq = it[1]

                def mkg(tt):
                    def f(e):
                        if tt == 0:
                            e.wait_ge(p4yg.h, 16 * 8 * (tq + 1))
                            if tq >= 1:
                                e.wait_ge(s_dve4.h, dve4_e[(8 * tq - 1, 3)])
                        for c in range(64):
                            _mi = e.matmul(pQ[tt][:, 0:128], ygr_t(c, tt), ygr_t(c, tt),
                                           start=(c == 0), stop=(c == 63),
                                           skip_group_check=True)
                        if tt == 3:
                            s_pe4.inc(_mi, 1)
                    return f
                return [mkg(tt) for tt in range(4)]
            tq, o8 = it[1], it[2]
            ui = 8 * tq + o8

            def mkp(p):
                def f(e):
                    if p == 0:
                        if o8 == 0:
                            e.wait_ge(s_dve4.h, dve4_g[tq])
                        else:
                            e.wait_ge(s_dve4.h, dve4_e[(ui - 1, 3)])
                    q = 8 * ui + p
                    e.wait_ge(p4w[q % 2].h, 16 * (q // 2 + 1))
                    for cc in range(8):
                        c = 8 * p + cc
                        for tt in range(4):
                            _mi = e.matmul(pQ[tt][:, :], ygr_t(c, tt),
                                           wosB[q % 2][:, cc * 512:(cc + 1) * 512],
                                           start=(c == 0), stop=(c == 63),
                                           skip_group_check=True)
                    s_pe4.inc(_mi, 1)
                return f
            return [mkp(p) for p in range(8)]

        @blk.tensor
        def _(e):
            for k in range(NCH):
                early, late = [], []
                for idx, it in enumerate(host[k]):
                    if it[0] == "g" or idx == 0:
                        early += p4_pe_pieces(it)
                    else:
                        late.append(it)
                ei = 0
                for g in range(8):
                    if ei < len(early):
                        early[ei](e)
                        ei += 1
                    kk, gg = (k, g - 1) if g >= 1 else (k - 1, 7)
                    if kk >= 0:
                        e.wait_ge(s_dve.h, dve3(kk, 1 + 3 * gg))
                    e.wait_ge(eas.h, 32 * (k + 1))
                    for hh in range(8):
                        h = g * 8 + hh
                        r = 32 * (h // 32)
                        _mi = e.matmul(pB2[:, ts(hh)], ones33[r:r + 1, :],
                                       expabsk[r:r + 1, (h % 32) * 128:(h % 32) * 128 + 128],
                                       start=(hh % 4 == 0), stop=(hh % 4 == 3),
                                       skip_group_check=True)
                    s_pe.inc(_mi, 1)
                    if ei < len(early):
                        early[ei](e)
                        ei += 1
                    e.wait_ge(s_dve.h, dve3(k, 1 + 3 * g))
                    kk, gg = (k, g - 1) if g >= 1 else (k - 1, 7)
                    if kk >= 0:
                        e.wait_ge(s_dve.h, dve3(kk, 2 + 3 * gg))
                    for hh in range(8):
                        h = g * 8 + hh
                        _mi = e.matmul(pD[:, ts(hh)], seed[:, ts(h)], cdaB[g % 2][:, ts(hh)],
                                       start=(hh % 4 == 0), stop=(hh % 4 == 3),
                                       skip_group_check=True)
                    s_pe.inc(_mi, 1)
                while ei < len(early):
                    early[ei](e)
                    ei += 1
                for it in late:
                    for f in p4_pe_pieces(it):
                        f(e)
            for it in tail34:
                for f in p4_pe_pieces(it):
                    f(e)

        def p4_dve(e, it):
            if it[0] == "g":
                tq = it[1]
                e.wait_ge(s_pe4.h, pe4_g[tq])
                if tq >= 1:
                    e.wait_ge(s_act4.h, act4_g[tq - 1])
                e.memset(rsq[:, :], 0.0)
                e.drain()
                for tt in range(4):
                    e.tensor_tensor(ycB[0][:, 0:128], pQ[tt][:, 0:128], identf[:], ALU.mult)
                    e.drain()
                    e.tensor_scalar(ycB[0][:, 128:256], ycB[0][:, 0:128], 1.0, None,
                                    ALU.mult, ALU.add, accum_out=rsq[:, tt:tt + 1])
                    e.drain()
                s_dve4.inc(e.nop(), 1)
            else:
                tq, o8 = it[1], it[2]
                ui = 8 * tq + o8
                e.wait_ge(s_pe4.h, pe4_u[(ui, 7)])
                e.wait_ge(s_act4.h, act4_g[tq])
                for tt in range(4):
                    m = 4 * ui + tt
                    tg = tq * 4 + tt
                    e.wait_ge(p4x1[m % 4].h, 16 * (m // 4 + 1))
                    if m >= 4:
                        e.wait_ge(p4st[m % 4].h, 16 * (m // 4))
                    e.scalar_tensor_tensor(x2b[m % 4][:, :], pQ[tt][:, :],
                                           rstd3p[:, tg:tg + 1], x1b[m % 4][:, :],
                                           ALU.mult, ALU.add)
                    e.drain()
                    s_dve4.inc(e.nop(), 1)

        @blk.vector
        def _(e):
            for k in range(NCH):
                for g in range(8):
                    e.wait_ge(s_pe.h, pe3(k, 2 * g + 1))
                    kk, gg = prev2b(k, g)
                    if kk >= 0:
                        e.wait_ge(s_pe.h, pe3(kk, 2 * gg + 2))
                    ct_rep = Ct_r[:, ts(k, L)].unsqueeze(1).broadcast_to([128, 8, 128])
                    e.tensor_tensor(cdaB[g % 2][:, :].rearrange("p (a b) -> p a b", a=8),
                                    pB2[:, :].rearrange("p (a b) -> p a b", a=8),
                                    ct_rep, ALU.mult)
                    e.drain()
                    s_dve.inc(e.nop(), 1)
                    e.wait_ge(s_pe.h, pe3(k, 2 * g + 2))
                    e.wait_ge(yls.h, 16 * 8 * (k + 1))
                    e.tensor_tensor(ycB[g % 2][:, :], yslab[:, ts(g, 1024)], pD[:, :], ALU.add)
                    e.drain()
                    s_dve.inc(e.nop(), 1)
                    e.wait_ge(s_act.h, act3(k))
                    v = 16 * (4 * k + g // 2)
                    if v > 0:
                        e.wait_ge(ygst[g % 2].h, v)
                    e.tensor_tensor(yg2[g % 2][:, :], ycB[g % 2][:, :],
                                    silz[:, ts(g, 1024)], ALU.mult)
                    e.drain()
                    s_dve.inc(e.nop(), 1)
                for it in host[k]:
                    p4_dve(e, it)
            for it in tail34:
                p4_dve(e, it)

        def p4_act(e, it):
            if it[0] == "g":
                tq = it[1]
                e.wait_ge(s_dve4.h, dve4_g[tq])
                e.activation(rln[:, 0:4], rsq[:, 0:4], AF.Ln, scale=1.0 / DSSM,
                             bias=epscol[:, 0:1])
                e.drain()
                e.activation(rstd3p[:, 4 * tq:4 * tq + 4], rln[:, 0:4], AF.Exp, scale=-0.5)
                e.drain()
                s_act4.inc(e.nop(), 1)
            else:
                tq, o8 = it[1], it[2]
                ui = 8 * tq + o8
                for tt in range(4):
                    m = 4 * ui + tt
                    tg = tq * 4 + tt
                    if m >= 4:
                        e.wait_ge(s_dve4.h, dve4_e[((m - 4) // 4, (m - 4) % 4)])
                    p4x1[m % 4].inc(e.dma_start(
                        out=x1b[m % 4][:, :],
                        in_=x1_s[3 + tg * 128:3 + tg * 128 + 128, ts(o8, 512)]), 16)

        @blk.scalar
        def _(e):
            for k in range(NCH):
                e.wait_ge(zls.h, 16 * 8 * (k + 1))
                if k >= 1:
                    e.wait_ge(s_dve.h, dve3(k - 1, 24))
                e.activation(silz[:], zslab[:], AF.Silu)
                e.drain()
                s_act.inc(e.nop(), 1)
                for it in host[k]:
                    p4_act(e, it)
            for it in tail34:
                p4_act(e, it)
    phase_barrier()
    ssd_es.close()
    mid_es.close()


    emit_ln("l3", None, x2_s, T_MAIN, ln3_s)
    phase_barrier()
    emit_ffn("f2", ln3_s, T_MAIN, None, x2_s, out_t)
    phase_barrier()

    @blk.sync
    def _(e):
        for name, dap in dbg.items():
            s_ld.inc(e.dma_start(out=dap, in_=dbg_map[name]), 16)
        e.wait_ge(s_ld.h, s_ld.n)
        e.wait_ge(s_ldg.h, s_ldg.n)
        e.wait_ge(s_tr.h, s_tr.n)

    es.close()
    return nc, dbg_map


def _aux_inputs():
    triu_np = np.triu(np.ones((128, 128), np.float32)).astype(ml_dtypes.bfloat16)
    ident_np = np.eye(128, dtype=np.float32)
    return triu_np, ident_np.astype(ml_dtypes.bfloat16), ident_np


def make_in_maps(inputs, T_MAIN=2048):
    x = np.ascontiguousarray(np.asarray(inputs["L0_attn_hidden_states"], np.float32))
    W_in = np.ascontiguousarray(np.asarray(inputs["L0_attn_W_in"], np.float32))
    W_out = np.ascontiguousarray(np.asarray(inputs["L0_attn_W_out"], np.float32))
    conv_w = np.ascontiguousarray(np.asarray(inputs["L0_attn_conv_w"], np.float32))
    dt_bias = np.asarray(inputs["L0_attn_dt_bias"], np.float32)
    A_log = np.asarray(inputs["L0_attn_A_log"], np.float32)
    D_param = np.asarray(inputs["L0_attn_D_param"], np.float32)
    nw = np.asarray(inputs["L0_attn_norm_weight"], np.float32).reshape(DSSM, 1)
    wp = np.asarray(inputs["L0_ffn_weight_packed"], np.int32)
    sce = np.asarray(inputs["L0_ffn_scales"])
    pT = np.ascontiguousarray(wp.view(np.uint8)[:, 0::4].T)
    scT = np.ascontiguousarray(np.repeat(
        np.asarray(sce).view(np.uint16).T, 16, axis=0)).view(np.float16)
    hvecs = np.stack([dt_bias, A_log, D_param, np.zeros_like(D_param)], axis=1)
    hvecs = np.ascontiguousarray(hvecs.astype(np.float32))
    triu_np, ident_b, ident_f = _aux_inputs()

    Bsz, S, _ = x.shape
    nhalf = S // T_MAIN
    in_maps = []
    for core in range(8):
        b, half = core // nhalf if nhalf > 1 else core // 2, core % nhalf if nhalf > 1 else core % 2
        b, half = divmod(core, nhalf) if nhalf > 1 else (core // 2, core % 2)
        xm = np.ascontiguousarray(x[b, half * T_MAIN:(half + 1) * T_MAIN])
        if half == 0:
            xp = np.zeros((3, H), np.float32)
        else:
            xp = np.ascontiguousarray(x[b, half * T_MAIN - 3:half * T_MAIN])
        flags = np.array([[1.0 if half == 0 else 0.0,
                           0.0 if half == 0 else 1.0]], np.float32)
        in_maps.append({
            "x_main": xm, "x_pre": xp, "W_in": W_in, "W_out": W_out,
            "packedT": pT, "scalesE": scT, "conv_win": conv_w, "hvecs": hvecs,
            "nw_in": np.ascontiguousarray(nw), "flags_in": flags,
            "triu_in": triu_np, "ident_in": ident_b, "identf_in": ident_f,
        })
    return in_maps


_BUILT = {}


def kernel(**inputs):
    T_MAIN = 2048
    key = T_MAIN
    if key not in _BUILT:
        _BUILT[key] = build_program(T_MAIN)
    nc, _ = _BUILT[key]
    in_maps = make_in_maps(inputs, T_MAIN)
    res = run_bass_kernel_spmd(nc, in_maps, core_ids=list(range(8)))
    x = np.asarray(inputs["L0_attn_hidden_states"])
    Bsz, S, _ = x.shape
    out = np.empty((Bsz, S, H), np.float32)
    for core in range(8):
        b, half = divmod(core, S // T_MAIN)
        out[b, half * T_MAIN:(half + 1) * T_MAIN] = res.results[core]["out"]
    return out


# version 14
# speedup vs baseline: 1.0500x; 1.0013x over previous
import contextlib
import numpy as np
import ml_dtypes
import concourse.bass as bass
import concourse.mybir as mybir
from concourse.bass_utils import run_bass_kernel_spmd

F32 = mybir.dt.float32
BF16 = mybir.dt.bfloat16
F16 = mybir.dt.float16
U8 = mybir.dt.uint8
I32 = mybir.dt.int32
AF = mybir.ActivationFunctionType
ALU = mybir.AluOpType

H = 4096
DSSM = 8192
NH = 64
CONV = 8448
PROJ = 16704
KTAPS = 4
EPS = 1e-5
L = 128
NDT = H // 128
NZT = DSSM // 128
NCT = CONV // 128


class Sem:
    def __init__(self, h):
        self.h = h
        self.n = 0
    def inc(self, instr, k):
        instr.then_inc(self.h, k)
        self.n += k
        return self.n


def ts(i, n=128):
    return slice(i * n, (i + 1) * n)


def build_program(T_MAIN=2048, debug_outs=(), n_cc=8):
    NCH = T_MAIN // L
    NTT = T_MAIN // 128
    TP = T_MAIN + 3
    TPAD = ((TP + 15) // 16) * 16
    NGRP = 8
    GH = NH // NGRP

    CHUNKS = []
    off = 0
    while TP - off > 3:
        sz = min(512, TP - 3 - off)
        CHUNKS.append((off, sz))
        off += sz
    CHUNKS.append((off, 3))
    NCHK = len(CHUNKS)

    nc = bass.Bass("TRN2", target_bir_lowering=False, debug=False)

    x_main = nc.dram_tensor("x_main", [T_MAIN, H], F32, kind="ExternalInput").ap()
    x_pre = nc.dram_tensor("x_pre", [3, H], F32, kind="ExternalInput").ap()
    W_in = nc.dram_tensor("W_in", [H, PROJ], F32, kind="ExternalInput").ap()
    W_out = nc.dram_tensor("W_out", [DSSM, H], F32, kind="ExternalInput").ap()
    packedT = nc.dram_tensor("packedT", [2048, H], U8, kind="ExternalInput").ap()
    scalesE = nc.dram_tensor("scalesE", [2048, H], F16, kind="ExternalInput").ap()
    conv_win = nc.dram_tensor("conv_win", [CONV, KTAPS], F32, kind="ExternalInput").ap()
    hvecs = nc.dram_tensor("hvecs", [NH, 4], F32, kind="ExternalInput").ap()
    nw_in = nc.dram_tensor("nw_in", [DSSM, 1], F32, kind="ExternalInput").ap()
    flags_in = nc.dram_tensor("flags_in", [1, 2], F32, kind="ExternalInput").ap()
    triu_in = nc.dram_tensor("triu_in", [128, 128], BF16, kind="ExternalInput").ap()
    ident_in = nc.dram_tensor("ident_in", [128, 128], BF16, kind="ExternalInput").ap()
    identf_in = nc.dram_tensor("identf_in", [128, 128], F32, kind="ExternalInput").ap()
    out_t = nc.dram_tensor("out", [T_MAIN, H], F32, kind="ExternalOutput").ap()

    wdeq_s = nc.dram_tensor("wdeq_s", [H, H], BF16).ap()
    ln1_s = nc.dram_tensor("ln1_s", [TPAD, H], BF16).ap()
    x1_s = nc.dram_tensor("x1_s", [TP, H], F32).ap()
    ln2_s = nc.dram_tensor("ln2_s", [TPAD, H], BF16).ap()
    z_s = nc.dram_tensor("z_s", [DSSM, T_MAIN], BF16).ap()
    xf_s = nc.dram_tensor("xf_s", [DSSM, T_MAIN], BF16).ap()
    yraw_s = nc.dram_tensor("yraw_s", [DSSM, T_MAIN], BF16).ap()
    yg_s = nc.dram_tensor("yg_s", [DSSM, T_MAIN], BF16).ap()
    x2_s = nc.dram_tensor("x2_s", [T_MAIN, H], F32).ap()
    TPAD3 = ((T_MAIN + 15) // 16) * 16
    ln3_s = nc.dram_tensor("ln3_s", [TPAD3, H], BF16).ap()
    wout_s = nc.dram_tensor("wout_s", [DSSM, H], BF16).ap()
    cc_in = nc.dram_tensor("cc_in", [128, DSSM], F32).ap()
    cc_out = nc.dram_tensor("cc_out", [128, DSSM], F32).ap()
    bncA = nc.dram_tensor("bncA", [NH, T_MAIN], BF16).ap()
    bncB = nc.dram_tensor("bncB", [NH, T_MAIN], BF16).ap()
    bncC = nc.dram_tensor("bncC", [NH, NCH], F32).ap()

    dbg_map = {"wdeq": wdeq_s, "ln1": ln1_s, "x1": x1_s, "ln2": ln2_s, "z": z_s,
               "xf": xf_s, "yraw": yraw_s, "yg": yg_s, "x2": x2_s, "ln3": ln3_s}
    dbg = {}
    for name in debug_outs:
        src = dbg_map[name]
        dbg[name] = nc.dram_tensor("dbg_" + name, list(src.tensor.shape), src.dtype,
                                   kind="ExternalOutput").ap()

    es = contextlib.ExitStack()
    sbuf = lambda n, shp, dt: es.enter_context(nc.sbuf_tensor(n, shp, dt))
    SEM = lambda n: Sem(es.enter_context(nc.semaphore(n)))

    triu = sbuf("triu", [128, 128], BF16)
    ident = sbuf("ident", [128, 128], BF16)
    identf = sbuf("identf", [128, 128], F32)
    onescol = sbuf("onescol", [1, 128], BF16)
    ones33 = sbuf("ones33", [33, 128], BF16)
    onescolf = sbuf("onescolf", [1, 128], F32)
    onesfull = sbuf("onesfull", [128, 128], BF16)
    fl = sbuf("fl", [128, 2], F32)
    hv = sbuf("hv", [NH, 4], F32)
    cw2 = sbuf("cw2", [128, NCT * KTAPS], F32)
    nw2 = sbuf("nw2", [128, NZT], F32)
    epscol = sbuf("epscol", [128, 1], F32)
    onecolf2 = sbuf("onecolf2", [128, 1], F32)
    rstd3p = sbuf("rstd3p", [128, NCH], F32)

    s_ld = SEM("s_ld")
    s_ldg = SEM("s_ldg")
    s_tr = SEM("s_tr")
    s_pe = SEM("s_pe")
    s_dve = SEM("s_dve")
    s_act = SEM("s_act")
    s_gp = SEM("s_gp")
    s_cc = SEM("s_cc")

    blk = es.enter_context(nc.Block())

    def phase_barrier():
        c = dict(ld=s_ld.n, ldg=s_ldg.n, tr=s_tr.n, pe=s_pe.n, dve=s_dve.n,
                 act=s_act.n, gp=s_gp.n)
        for sec in (blk.sync, blk.gpsimd, blk.vector, blk.scalar, blk.tensor):
            @sec
            def _(e):
                e.wait_ge(s_ld.h, c["ld"]); e.wait_ge(s_ldg.h, c["ldg"])
                e.wait_ge(s_tr.h, c["tr"]); e.wait_ge(s_pe.h, c["pe"])
                e.wait_ge(s_dve.h, c["dve"]); e.wait_ge(s_act.h, c["act"])
                e.wait_ge(s_gp.h, c["gp"])

    @blk.sync
    def _(e):
        s_ld.inc(e.dma_start(out=triu[:], in_=triu_in), 16)
        s_ld.inc(e.dma_start(out=ident[:], in_=ident_in), 16)
        s_ld.inc(e.dma_start(out=identf[:], in_=identf_in), 16)
        s_ld.inc(e.dma_start(out=hv[:], in_=hvecs), 16)
        s_ld.inc(e.dma_start(out=fl[:], in_=flags_in.squeeze(0).partition_broadcast(128)), 16)
        s_ld.inc(e.dma_start(out=cw2[:].rearrange("p (a k) -> p a k", k=KTAPS), in_=conv_win.rearrange("(a p) k -> p a k", p=128)), 16)
        s_ld.inc(e.dma_start(out=nw2[:].unsqueeze(2), in_=nw_in.rearrange("(a p) b -> p a b", p=128), allow_slow_non_contiguous=True), 16)

    @blk.vector
    def _(e):
        e.wait_ge(s_ld.h, s_ld.n)
        e.memset(onescol[:], 1.0)
        e.memset(ones33[:], 1.0)
        e.memset(onescolf[:], 1.0)
        e.memset(onesfull[:], 1.0)
        e.memset(epscol[:], float(EPS))
        e.memset(onecolf2[:], 1.0)
        e.drain()
        s_dve.inc(e.nop(), 1)
    phase_barrier()

    with contextlib.ExitStack() as ph:
        psb = lambda n, shp, dt: ph.enter_context(nc.sbuf_tensor(n, shp, dt))
        pk = [psb(f"pk{i}", [128, H], U8) for i in range(2)]
        sc = [psb(f"sc{i}", [128, H], F16) for i in range(2)]
        lohi = psb("lohi", [128, 2 * H], U8)
        tg0 = psb("tg0", [128, 2 * H], BF16)
        wdq = [psb(f"wdq{i}", [128, 2 * H], BF16) for i in range(2)]
        wofp = [psb(f"wofp{i}", [128, 2 * H], F32) for i in range(2)]
        wosp = [psb(f"wosp{i}", [128, 2 * H], BF16) for i in range(2)]
        p0ld = [SEM("p0lda"), SEM("p0ldb")]
        p0st = [SEM("p0sta"), SEM("p0stb")]
        wls = [SEM("p0wla"), SEM("p0wlb")]
        wss = [SEM("p0wsa"), SEM("p0wsb")]
        gp2 = SEM("p0gp2")
        dv2 = SEM("p0dv2")
        dve0, gp0 = s_dve.n, s_gp.n

        @blk.sync
        def _(e):
            for i in range(16):
                if i >= 2:
                    e.wait_ge(s_gp.h, gp0 + (i - 1))
                p0ld[i % 2].inc(e.dma_start(out=pk[i % 2][:], in_=packedT[ts(i), :]), 16)
                p0ld[i % 2].inc(e.dma_start(out=sc[i % 2][:], in_=scalesE[ts(i), :]), 16)

        @blk.vector
        def _(e):
            for i in range(16):
                e.wait_ge(p0ld[i % 2].h, 32 * (i // 2 + 1))
                if i >= 2:
                    e.wait_ge(p0st[i % 2].h, 32 * (i // 2))
                p, w, sci = pk[i % 2], wdq[i % 2], sc[i % 2]
                e.tensor_scalar(lohi[:, 0:H], p[:], 15, None, ALU.bitwise_and, ALU.bypass)
                e.tensor_scalar(lohi[:, H:2 * H].bitcast(I32), p[:].bitcast(I32), 4, 0x0F0F0F0F,
                                ALU.logical_shift_right, ALU.bitwise_and)
                e.drain()
                e.tensor_scalar(tg0[:, :], lohi[:, :], -8.0, None, ALU.add, ALU.bypass)
                e.drain()
                e.tensor_tensor(w[:, 0:H], tg0[:, 0:H], sci[:], ALU.mult)
                e.tensor_tensor(w[:, H:2 * H], tg0[:, H:2 * H], sci[:], ALU.mult)
                e.drain()
                s_gp.inc(e.nop(), 1)

        own_gp = [i % 4 == 3 for i in range(32)]
        dcnt = [sum(1 for j in range(i + 1) if not own_gp[j]) for i in range(32)]
        gcnt = [sum(1 for j in range(i + 1) if own_gp[j]) for i in range(32)]

        def presc(e, i, sem):
            e.wait_ge(wls[i % 2].h, 16 * (i // 2 + 1))
            if i >= 2:
                e.wait_ge(wss[i % 2].h, 16 * (i // 2))
            nwrep = nw2[:, i * 2:(i + 1) * 2].unsqueeze(2).broadcast_to([128, 2, H])
            e.tensor_tensor(wosp[i % 2][:, :].rearrange("p (a n) -> p a n", a=2),
                            wofp[i % 2][:, :].rearrange("p (a n) -> p a n", a=2),
                            nwrep, ALU.mult)
            e.drain()
            sem.inc(e.nop(), 1)

        @blk.vector
        def _(e):
            for i in range(32):
                if not own_gp[i]:
                    presc(e, i, dv2)

        @blk.gpsimd
        def _(e):
            def load(i):
                wls[i % 2].inc(e.dma_start(
                    out=wofp[i % 2][:, :].rearrange("p (a n) -> p a n", a=2),
                    in_=W_out[i * 256:(i + 1) * 256, :].rearrange("(a p) n -> p a n", p=128)), 16)
            load(0)
            load(1)
            for i in range(32):
                if own_gp[i]:
                    presc(e, i, gp2)
                else:
                    e.wait_ge(dv2.h, dcnt[i])
                if i + 2 < 32:
                    load(i + 2)

        @blk.scalar
        def _(e):
            for i in range(16):
                e.wait_ge(s_gp.h, gp0 + (i + 1))
                p0st[i % 2].inc(e.dma_start(out=wdeq_s[ts(i), :], in_=wdq[i % 2][:, 0:H]), 16)
                p0st[i % 2].inc(e.dma_start(out=wdeq_s[ts(16 + i), :], in_=wdq[i % 2][:, H:2 * H]), 16)

        @blk.sync
        def _(e):
            for i in range(32):
                if not own_gp[i]:
                    e.wait_ge(dv2.h, dcnt[i])
                else:
                    e.wait_ge(gp2.h, gcnt[i])
                wss[i % 2].inc(e.dma_start(
                    out=wout_s[i * 256:(i + 1) * 256, :].rearrange("(a p) n -> p a n", p=128),
                    in_=wosp[i % 2][:, :].rearrange("p (a n) -> p a n", a=2)), 16)

        @blk.scalar
        def _(e):
            e.wait_ge(p0st[0].h, p0st[0].n)
            e.wait_ge(p0st[1].h, p0st[1].n)
            s_act.inc(e.nop(), 1)

        @blk.sync
        def _(e):
            e.wait_ge(wss[0].h, wss[0].n)
            e.wait_ge(wss[1].h, wss[1].n)
            s_ld.inc(e.nop(), 16)
    phase_barrier()

    def resid_load2(e, sem, buf, r0, r1, cols, pre_ap, main_ap):
        w = cols.stop - cols.start
        if pre_ap is not None:
            if r0 < 3:
                sem.inc(e.dma_start(out=buf[0:3 - r0, 0:w], in_=pre_ap[r0:3, cols]), 16)
            if r1 > 3:
                a = max(r0, 3)
                sem.inc(e.dma_start(out=buf[a - r0:r1 - r0, 0:w], in_=main_ap[a - 3:r1 - 3, cols]), 16)
        else:
            sem.inc(e.dma_start(out=buf[0:r1 - r0, 0:w], in_=main_ap[r0:r1, cols]), 16)

    def resid_load(e, buf, r0, r1, cols, pre_ap, main_ap):
        w = cols.stop - cols.start
        if pre_ap is not None:
            if r0 < 3:
                s_ld.inc(e.dma_start(out=buf[0:3 - r0, 0:w], in_=pre_ap[r0:3, cols]), 16)
            if r1 > 3:
                a = max(r0, 3)
                s_ld.inc(e.dma_start(out=buf[a - r0:r1 - r0, 0:w], in_=main_ap[a - 3:r1 - 3, cols]), 16)
        else:
            s_ld.inc(e.dma_start(out=buf[0:r1 - r0, 0:w], in_=main_ap[r0:r1, cols]), 16)

    def resid_nld(r0, r1, has_pre):
        return ((1 if r0 < 3 else 0) + (1 if r1 > 3 else 0)) if has_pre else 1

    def emit_ln(tag, src_pre, src_main, nrows, dst):
        ntile = (nrows + 127) // 128
        hp = src_pre is not None
        with contextlib.ExitStack() as ph:
            psb = lambda n, shp, dt: ph.enter_context(nc.sbuf_tensor(n, shp, dt))
            xs = [psb(f"{tag}x{i}", [128, H], F32) for i in range(2)]
            xs2 = [psb(f"{tag}y{i}", [128, H], F32) for i in range(2)]
            disc = psb(f"{tag}d", [128, H], BF16)
            disc2 = psb(f"{tag}d2", [128, H], BF16)
            zpad = psb(f"{tag}z", [16, H], BF16)
            lnt = [psb(f"{tag}l{i}", [128, H], BF16) for i in range(2)]
            cols = psb(f"{tag}c", [128, 8 * ntile], F32)
            lda = [SEM(tag + "la"), SEM(tag + "lb")]
            ldb = [SEM(tag + "ma"), SEM(tag + "mb")]
            stp = [SEM(tag + "sa"), SEM(tag + "sb")]
            dve0, act0 = s_dve.n, s_act.n

            npt = [resid_nld(i * 128, min((i + 1) * 128, nrows), hp) for i in range(ntile)]

            def cum_par(npt_list, i):
                return 16 * sum(npt_list[j] for j in range(i % 2, i + 1, 2))

            @blk.sync
            def _(e):
                for i in range(ntile):
                    if i >= 2:
                        e.wait_ge(s_act.h, act0 + (i - 1))
                    resid_load2(e, lda[i % 2], xs[i % 2], i * 128,
                                min((i + 1) * 128, nrows), slice(0, H), src_pre, src_main)

            @blk.vector
            def _(e):
                e.memset(cols[:], 0.0)
                e.memset(zpad[:], 0.0)
                e.drain()
                for i in range(ntile):
                    r = min((i + 1) * 128, nrows) - i * 128
                    e.wait_ge(lda[i % 2].h, cum_par(npt, i))
                    if i >= 2:
                        e.wait_ge(s_act.h, act0 + i - 1)
                    e.tensor_scalar(disc[0:r, :], xs[i % 2][0:r, :], 1.0, None, ALU.mult, ALU.add,
                                    accum_out=cols[0:r, i:i + 1])
                    e.drain()
                    s_dve.inc(e.nop(), 1)

            @blk.scalar
            def _(e):
                for i in range(ntile):
                    r = min((i + 1) * 128, nrows) - i * 128
                    e.wait_ge(s_dve.h, dve0 + i + 1)
                    e.activation(disc2[0:r, :], xs[i % 2][0:r, :], AF.Square,
                                 accum_out=cols[0:r, ntile + i:ntile + i + 1])
                    e.drain()
                    s_act.inc(e.nop(), 1)
                e.activation(cols[:, 2 * ntile:3 * ntile], cols[:, 0:ntile], AF.Copy, scale=1.0 / H)
                e.drain()
                e.activation(cols[:, 3 * ntile:4 * ntile], cols[:, 2 * ntile:3 * ntile], AF.Square)
                e.drain()
                s_act.inc(e.nop(), 1)

            @blk.vector
            def _(e):
                e.wait_ge(s_act.h, act0 + ntile + 1)
                e.tensor_scalar(cols[:, 4 * ntile:5 * ntile], cols[:, ntile:2 * ntile], 1.0 / H,
                                None, ALU.mult, ALU.bypass)
                e.drain()
                e.tensor_tensor(cols[:, 5 * ntile:6 * ntile], cols[:, 4 * ntile:5 * ntile],
                                cols[:, 3 * ntile:4 * ntile], ALU.subtract)
                e.drain()
                s_dve.inc(e.nop(), 1)

            @blk.scalar
            def _(e):
                e.wait_ge(s_dve.h, dve0 + ntile + 1)
                e.activation(cols[:, 6 * ntile:7 * ntile], cols[:, 5 * ntile:6 * ntile],
                             AF.Ln, bias=epscol[:, 0:1])
                e.drain()
                e.activation(cols[:, 7 * ntile:8 * ntile], cols[:, 6 * ntile:7 * ntile],
                             AF.Exp, scale=-0.5)
                e.drain()
                s_act.inc(e.nop(), 1)

            dve1 = s_dve.n
            @blk.sync
            def _(e):
                for i in range(ntile):
                    if i >= 2:
                        e.wait_ge(s_dve.h, dve1 + (i - 1))
                    resid_load2(e, ldb[i % 2], xs2[i % 2], i * 128,
                                min((i + 1) * 128, nrows), slice(0, H), src_pre, src_main)

            @blk.vector
            def _(e):
                e.wait_ge(s_act.h, act0 + ntile + 2)
                for i in range(ntile):
                    r = min((i + 1) * 128, nrows) - i * 128
                    e.wait_ge(ldb[i % 2].h, cum_par(npt, i))
                    if i >= 2:
                        e.wait_ge(stp[i % 2].h, 16 * (i // 2))
                    e.tensor_scalar(lnt[i % 2][0:r, :].rearrange("p (eo a) -> p eo a", eo=2),
                                    xs2[i % 2][0:r, :].rearrange("p (a eo) -> p eo a", eo=2),
                                    cols[0:r, 2 * ntile + i:2 * ntile + i + 1],
                                    cols[0:r, 7 * ntile + i:7 * ntile + i + 1],
                                    ALU.subtract, ALU.mult)
                    e.drain()
                    s_dve.inc(e.nop(), 1)

            @blk.scalar
            def _(e):
                for i in range(ntile):
                    r0, r1 = i * 128, min((i + 1) * 128, nrows)
                    e.wait_ge(s_dve.h, dve1 + i + 1)
                    stp[i % 2].inc(e.dma_start(out=dst[r0:r1, :], in_=lnt[i % 2][0:r1 - r0, :]), 16)
                dstpad = dst.tensor.shape[0]
                if dstpad > nrows:
                    stp[0].inc(e.dma_start(out=dst[nrows:dstpad, :],
                                           in_=zpad[0:dstpad - nrows, :]), 16)

            @blk.scalar
            def _(e):
                e.wait_ge(stp[0].h, stp[0].n)
                e.wait_ge(stp[1].h, stp[1].n)
                s_act.inc(e.nop(), 1)

    def emit_ffn(tag, ln_src, ln_rows, resid_pre, resid_main, dst):
        ntile = (ln_rows + 127) // 128
        hp = resid_pre is not None
        qgrp = [list(range(g, min(g + 2, ntile))) for g in range(0, ntile, 2)]
        NPAIR = 2
        with contextlib.ExitStack() as ph:
            psb = lambda n, shp, dt: ph.enter_context(nc.sbuf_tensor(n, shp, dt))
            wq = [psb(f"{tag}wq{i}", [128, NDT * 1024], BF16) for i in range(2)]
            lnq = [psb(f"{tag}lq{i}", [128, NDT * 272], BF16) for i in range(2)]
            xres = [psb(f"{tag}xr{i}", [128, 2048], F32) for i in range(2)]
            ot = [psb(f"{tag}ot{i}", [128, 2048], F32) for i in range(2)]
            pss = [ph.enter_context(nc.psum_tensor(f"{tag}p{i}", [128, 2048], F32)) for i in range(2)]
            ldx = [SEM(tag + "xa"), SEM(tag + "xb")]
            trq = [SEM(tag + "ta"), SEM(tag + "tb")]
            wqs = [SEM(tag + "wqa"), SEM(tag + "wqb")]
            stp = [SEM(tag + "sa"), SEM(tag + "sb")]
            pe0, dve0 = s_pe.n, s_dve.n

            tl = [(p, qi, t) for p in range(NPAIR) for qi, q in enumerate(qgrp) for t in q]
            npt = [resid_nld(t * 128, min((t + 1) * 128, ln_rows), hp) for (_, _, t) in tl]

            def cum_par(i):
                return 16 * sum(npt[j] for j in range(i % 2, i + 1, 2))

            grps = [(p, qi) for p in range(NPAIR) for qi in range(len(qgrp))]
            gidx = {g: i for i, g in enumerate(grps)}
            tiles_thru = {}
            c = 0
            for g in grps:
                c += len(qgrp[g[1]])
                tiles_thru[g] = c

            def tr_par(gi):
                return 16 * NDT * (gi // 2 + 1)

            @blk.gpsimd
            def _(e):
                kglob = 0
                for p in range(NPAIR):
                    if p >= 1:
                        e.wait_ge(s_pe.h, pe0 + p * ntile)
                    for q2 in range(2):
                        nq = 2 * p + q2
                        for piece in range(4):
                            rows = slice(piece * 1024, (piece + 1) * 1024)
                            wqs[q2].inc(e.dma_start(
                                out=wq[q2][:, piece * 8 * 1024:(piece + 1) * 8 * 1024].rearrange(
                                    "p (a n) -> p a n", a=8),
                                in_=wdeq_s[rows, ts(nq, 1024)].rearrange("(a p) n -> p a n", p=128)), 16)
                    for _i in range(ntile):
                        k = kglob
                        t = tl[k][2]
                        r0, r1 = t * 128, min((t + 1) * 128, ln_rows)
                        e.wait_ge(s_dve.h, dve0 + k + 1)
                        stp[k % 2].inc(e.dma_start(out=dst[r0:r1, p * 2048:(p + 1) * 2048],
                                                   in_=ot[k % 2][0:r1 - r0, :]), 16)
                        kglob += 1

            @blk.scalar
            def _(e):
                for gi, g in enumerate(grps):
                    if gi >= 2:
                        e.wait_ge(s_pe.h, pe0 + tiles_thru[grps[gi - 2]])
                    q = qgrp[g[1]]
                    r0 = q[0] * 128
                    rq = ((min((q[-1] + 1) * 128, ln_rows) - r0 + 15) // 16) * 16
                    for d in range(NDT):
                        trq[gi % 2].inc(e.dma_start_transpose(
                            lnq[gi % 2][:, d * 272:d * 272 + rq], ln_src[r0:r0 + rq, ts(d)]), 16)

            @blk.sync
            def _(e):
                for k, (p, qi, t) in enumerate(tl):
                    if k >= 2:
                        e.wait_ge(s_dve.h, dve0 + k - 1)
                    resid_load2(e, ldx[k % 2], xres[k % 2], t * 128,
                                min((t + 1) * 128, ln_rows), slice(p * 2048, (p + 1) * 2048),
                                resid_pre, resid_main)

            @blk.tensor
            def _(e):
                for k, (p, qi, t) in enumerate(tl):
                    gi = gidx[(p, qi)]
                    e.wait_ge(trq[gi % 2].h, tr_par(gi))
                    e.wait_ge(wqs[0].h, 16 * 4 * (p + 1))
                    e.wait_ge(wqs[1].h, 16 * 4 * (p + 1))
                    if k >= 2:
                        e.wait_ge(s_dve.h, dve0 + k - 1)
                    r0 = t * 128
                    m = min(128, ln_rows - r0)
                    toff = r0 - qgrp[qi][0] * 128
                    for d in range(NDT):
                        for q2 in range(2):
                            for c2 in range(2):
                                _mi = e.matmul(pss[k % 2][0:m, q2 * 1024 + c2 * 512:q2 * 1024 + (c2 + 1) * 512],
                                               lnq[gi % 2][:, d * 272 + toff:d * 272 + toff + m],
                                               wq[q2][:, d * 1024 + c2 * 512:d * 1024 + (c2 + 1) * 512],
                                               start=(d == 0), stop=(d == NDT - 1))
                    s_pe.inc(_mi, 1)

            @blk.vector
            def _(e):
                for k, (p, qi, t) in enumerate(tl):
                    m = min(128, ln_rows - t * 128)
                    e.wait_ge(s_pe.h, pe0 + k + 1)
                    e.wait_ge(ldx[k % 2].h, cum_par(k))
                    if k >= 2:
                        e.wait_ge(stp[k % 2].h, 16 * (k // 2))
                    e.scalar_tensor_tensor(ot[k % 2][0:m, :], pss[k % 2][0:m, :], 0.5,
                                           xres[k % 2][0:m, :], ALU.mult, ALU.add)
                    e.drain()
                    s_dve.inc(e.nop(), 1)

            @blk.gpsimd
            def _(e):
                e.wait_ge(stp[0].h, stp[0].n)
                e.wait_ge(stp[1].h, stp[1].n)
                s_gp.inc(e.nop(), 1)


    emit_ln("l1", x_pre, x_main, TP, ln1_s)
    phase_barrier()
    emit_ffn("f1", ln1_s, TP, x_pre, x_main, x1_s)
    phase_barrier()
    emit_ln("l2", x1_s[0:3, :], x1_s[3:TP, :], TP, ln2_s)
    phase_barrier()

    W_in_p = W_in.rearrange("(blk p eo) n -> eo p blk n", blk=16, p=128, eo=2)
    jorder = [130] + list(range(64)) + list(range(64, 130))
    conv_chunks = [(o, min(512, T_MAIN - o)) for o in range(0, T_MAIN, 512)]
    NCV = len(conv_chunks)

    def pe_before_main(k):
        return 5 * 0 + NCHK * k + NCV * min(max(k - 66, 0), 65) + (NCV if k > 130 else 0)

    def pe_before_conv(b):
        kb = 66 + b if b <= 64 else 130
        return NCHK * (kb + 1) + NCV * b

    def act_before_main(k):
        return NCHK * k + NCV * min(max(k - 66, 0), 65)

    def act_before_conv(b):
        kb = 66 + b if b <= 64 else 130
        return NCHK * (kb + 1) + NCV * b

    mid_es = contextlib.ExitStack()
    Bt_r = mid_es.enter_context(nc.sbuf_tensor("Bt_r", [128, T_MAIN], BF16))
    Ct_r = mid_es.enter_context(nc.sbuf_tensor("Ct_r", [128, T_MAIN], BF16))
    dtraw = mid_es.enter_context(nc.sbuf_tensor("dtraw", [NH, TP], F32))
    with contextlib.ExitStack() as ph:
        psb = lambda n, shp, dt: ph.enter_context(nc.sbuf_tensor(n, shp, dt))
        ln2T = psb("ln2T", [128, NDT * TPAD], BF16)
        wi = [psb(f"wi{i}", [128, NDT * 128], BF16) for i in range(2)]
        stripes = [psb(f"strp{i}", [128, TP], BF16) for i in range(2)]
        zstr = [psb(f"zs{i}", [128, TP], BF16) for i in range(2)]
        xfb = [psb(f"xfb{i}", [128, 512], BF16) for i in range(2)]
        diag4 = [psb(f"dg{i}", [128, KTAPS * 128], BF16) for i in range(2)]
        pMain = ph.enter_context(nc.psum_tensor("pMain", [128, 2048], F32))
        pRag = ph.enter_context(nc.psum_tensor("pRag", [128, 512], F32))
        pCv = [ph.enter_context(nc.psum_tensor(f"pCv{i}", [128, 512], F32)) for i in range(2)]
        wis = [SEM("p2wia"), SEM("p2wib")]
        zst = [SEM("p2za"), SEM("p2zb")]
        xfst = [SEM("p2xa"), SEM("p2xb")]
        ld0, ldg0, tr0, pe0, dve0, act0 = s_ld.n, s_ldg.n, s_tr.n, s_pe.n, s_dve.n, s_act.n

        @blk.scalar
        def _(e):
            for d in range(NDT):
                s_tr.inc(e.dma_start_transpose(ln2T[:, d * TPAD:d * TPAD + TPAD],
                                               ln2_s[:, ts(d)]), 16)
        TR_LN2T = s_tr.n

        @blk.gpsimd
        def _(e):
            for k, j in enumerate(jorder):
                if k >= 2:
                    e.wait_ge(s_pe.h, pe0 + pe_before_main(k - 2) + NCHK)
                jw = 64 if j == 130 else 128
                for half in range(2):
                    a0 = half * 16
                    wis[k % 2].inc(e.dma_start(
                        out=wi[k % 2][:, a0 * 128:(a0 + 16) * 128].rearrange(
                            "p (a n) -> p a n", n=128)[:, :, 0:jw],
                        in_=W_in_p[half, :, :, j * 128:j * 128 + jw]), 16)

        @blk.vector
        def _(e):
            for b in range(NCT):
                if b >= 2:
                    e.wait_ge(s_pe.h, pe0 + pe_before_conv(b - 2) + NCV)
                for tap in range(KTAPS):
                    e.tensor_scalar(diag4[b % 2][:, ts(tap)], ident[:],
                                    cw2[:, b * KTAPS + tap:b * KTAPS + tap + 1], None,
                                    ALU.mult, ALU.bypass)
                e.drain()
                s_dve.inc(e.nop(), 1)

        @blk.tensor
        def _(e):
            def emit_conv(b):
                e.wait_ge(s_act.h, act0 + act_before_conv(b))
                e.wait_ge(s_dve.h, dve0 + b + 1)
                st_buf = stripes[(65 + b) % 2]
                for oi, (o, osz) in enumerate(conv_chunks):
                    gci = b * NCV + oi
                    if gci >= 2:
                        bb, oo = divmod(gci - 2, NCV)
                        e.wait_ge(s_act.h, act0 + act_before_conv(bb) + oo + 1)
                    for tap in range(KTAPS):
                        _mi = e.matmul(pCv[gci % 2][:, 0:osz], diag4[b % 2][:, ts(tap)],
                                       st_buf[:, o + tap:o + tap + osz],
                                       start=(tap == 0), stop=(tap == KTAPS - 1))
                    s_pe.inc(_mi, 1)

            for k, j in enumerate(jorder):
                jw = 64 if j == 130 else 128
                e.wait_ge(s_tr.h, TR_LN2T)
                e.wait_ge(wis[k % 2].h, 32 * (k // 2 + 1))
                for ci, (off, sz) in enumerate(CHUNKS):
                    if k >= 1:
                        e.wait_ge(s_act.h, act0 + act_before_main(k - 1) + ci + 1)
                    tgt = pRag[0:jw, 0:3] if sz == 3 else pMain[0:jw, off:off + sz]
                    for d in range(NDT):
                        _mi = e.matmul(tgt, wi[k % 2][:, d * 128:d * 128 + jw],
                                       ln2T[:, d * TPAD + off:d * TPAD + off + sz],
                                       start=(d == 0), stop=(d == NDT - 1))
                    s_pe.inc(_mi, 1)
                if 66 <= k <= 130:
                    emit_conv(k - 66)
            emit_conv(65)

        def Z_ST(k):
            return 16 * ((k - 1) // 2 + 1)

        def XF_ST(gci):
            return 16 * (gci // 2 + 1)

        @blk.scalar
        def _(e):
            def silu_evacs(b):
                ct = b
                for oi, (o, osz) in enumerate(conv_chunks):
                    gci = b * NCV + oi
                    e.wait_ge(s_pe.h, pe0 + pe_before_conv(b) + oi + 1)
                    if ct == 64:
                        dst_ap = Bt_r[:, o:o + osz]
                    elif ct == 65:
                        dst_ap = Ct_r[:, o:o + osz]
                    else:
                        dst_ap = xfb[gci % 2][:, 0:osz]
                        if gci >= 2:
                            e.wait_ge(xfst[gci % 2].h, XF_ST(gci - 2))
                    e.activation(dst_ap, pCv[gci % 2][:, 0:osz], AF.Silu)
                    e.drain()
                    s_act.inc(e.nop(), 1)

            for k, j in enumerate(jorder):
                jw = 64 if j == 130 else 128
                for ci, (off, sz) in enumerate(CHUNKS):
                    e.wait_ge(s_pe.h, pe0 + pe_before_main(k) + ci + 1)
                    src = pRag[0:jw, 0:3] if sz == 3 else pMain[0:jw, off:off + sz]
                    if j == 130:
                        e.activation(dtraw[:, off:off + sz], src, AF.Copy)
                    elif j < 64:
                        if ci == 0 and k >= 3:
                            e.wait_ge(zst[k % 2].h, Z_ST(k - 2))
                        e.activation(zstr[k % 2][:, off:off + sz], src, AF.Copy)
                    else:
                        if ci == 0 and k >= 67:
                            e.wait_ge(s_pe.h, pe0 + pe_before_conv(j - 64 - 2) + NCV)
                        e.activation(stripes[k % 2][:, off:off + sz], src, AF.Copy)
                    e.drain()
                    s_act.inc(e.nop(), 1)
                if 66 <= k <= 130:
                    silu_evacs(k - 66)
            silu_evacs(65)

        @blk.sync
        def _(e):
            for k, j in enumerate(jorder):
                if j < 64:
                    e.wait_ge(s_act.h, act0 + act_before_main(k) + NCHK)
                    zst[k % 2].inc(e.dma_start(out=z_s[ts(j), :], in_=zstr[k % 2][:, 3:TP]), 16)
            for b in range(64):
                for oi, (o, osz) in enumerate(conv_chunks):
                    gci = b * NCV + oi
                    e.wait_ge(s_act.h, act0 + act_before_conv(b) + oi + 1)
                    xfst[gci % 2].inc(e.dma_start(out=xf_s[ts(b), o:o + osz],
                                                  in_=xfb[gci % 2][:, 0:osz]), 16)
        @blk.sync
        def _(e):
            for p in range(2):
                e.wait_ge(zst[p].h, zst[p].n)
                e.wait_ge(xfst[p].h, xfst[p].n)
            s_ld.inc(e.nop(), 16)
    phase_barrier()


    ssd_es = contextlib.ExitStack()
    ssb = lambda n, shp, dt: ssd_es.enter_context(nc.sbuf_tensor(n, shp, dt))
    seed = ssb("seed", [128, DSSM], BF16)
    ssd2_es = contextlib.ExitStack()
    ssb2 = lambda n, shp, dt: ssd2_es.enter_context(nc.sbuf_tensor(n, shp, dt))
    lamb = ssb2("lamb", [128, NH * NCH], F32)
    wTp = ssb2("wTp", [128, NCH * 64], F32)
    colterm = ssb2("colterm", [128, NCH * 64], F32)
    logdAT = ssb2("logdAT", [128, NCH * 64], BF16)
    hslab = ssb2("hslab", [128, DSSM], F32)
    hbf = ssb2("hbf", [128, DSSM], BF16)
    DIall = ssb2("DIall", [128, NH * 128], BF16)

    with contextlib.ExitStack() as ph:
        psb = lambda n, shp, dt: ph.enter_context(nc.sbuf_tensor(n, shp, dt))
        dtv = psb("dtv", [NH, T_MAIN], F32)
        logdt = psb("logdt", [NH, T_MAIN], F32)
        logdA_b = psb("logdA_b", [NH, T_MAIN], BF16)
        cumhs = psb("cumhs", [NH, T_MAIN], F32)
        cumabs = psb("cumabs", [NH, T_MAIN], F32)
        w_hs = psb("w_hs", [NH, T_MAIN], F32)
        ta = psb("ta", [NH, T_MAIN], F32)
        tb = psb("tb", [NH, T_MAIN], F32)
        tc = psb("tcx", [NH, T_MAIN], F32)
        td = psb("td", [NH, T_MAIN], F32)
        tbf = psb("tbf", [NH, T_MAIN], BF16)
        ones64 = psb("ones64", [NH, L], F32)
        negA = psb("negA", [NH, 1], F32)
        lam64 = psb("lam64", [NH, NCH], F32)
        lamflat = psb("lamflat", [1, NH * NCH], F32)
        ldtT = psb("ldtT", [128, 64], F32)
        Drow = psb("Drow", [1, 64], F32)
        Dbc = psb("Dbc", [128, 64], F32)
        pTb = ph.enter_context(nc.psum_tensor("pTb", [128, 64], BF16))
        pTf = ph.enter_context(nc.psum_tensor("pTf", [128, 64], F32))
        pTf2 = ph.enter_context(nc.psum_tensor("pTf2", [128, 64], F32))
        pTlam = ph.enter_context(nc.psum_tensor("pTlam", [128, 512], F32))
        dve0, act0, gp0, pe0, ldg0 = s_dve.n, s_act.n, s_gp.n, s_pe.n, s_ldg.n

        @blk.vector
        def _(e):
            e.memset(ones64[:], 1.0)
            e.memset(hslab[:], 0.0)
            e.tensor_scalar(ta[:], dtraw[:, 3:TP], hv[:, 0:1], None, ALU.add, ALU.bypass)
            e.drain()
            e.tensor_scalar(tb[:], ta[:], 0.0, None, ALU.max, ALU.bypass)
            e.tensor_scalar(tc[:], ta[:], -1.0, None, ALU.mult, ALU.bypass)
            e.drain()
            e.tensor_tensor(tc[:], ta[:], tc[:], ALU.max)
            e.drain()
            s_dve.inc(e.nop(), 1)

        @blk.scalar
        def _(e):
            e.wait_ge(s_dve.h, dve0 + 1)
            e.activation(td[:], tc[:], AF.Exp, scale=-1.0)
            e.drain()
            e.activation(tc[:], td[:], AF.Ln, bias=onecolf2[0:NH, 0:1])
            e.drain()
            e.activation(negA[:], hv[:, 1:2], AF.Exp)
            e.drain()
            s_act.inc(e.nop(), 1)

        @blk.vector
        def _(e):
            e.wait_ge(s_act.h, act0 + 1)
            e.tensor_tensor(dtv[:], tb[:], tc[:], ALU.add)
            e.drain()
            e.tensor_scalar(ta[:], dtv[:], 1e-19, None, ALU.max, ALU.bypass)
            e.tensor_scalar(negA[:], negA[:], -1.0, None, ALU.mult, ALU.bypass)
            e.drain()
            e.tensor_scalar(tb[:], dtv[:], negA[:, 0:1], None, ALU.mult, ALU.bypass)
            e.drain()
            e.tensor_copy(logdA_b[:], tb[:])
            for k in range(NCH):
                e.tensor_tensor_scan(cumhs[:, ts(k, L)], ones64[:], tb[:, ts(k, L)],
                                     0.0, ALU.mult, ALU.add)
            e.drain()
            for k in range(NCH):
                init = 0.0 if k == 0 else cumabs[:, k * L - 1:k * L]
                e.tensor_tensor_scan(cumabs[:, ts(k, L)], ones64[:], tb[:, ts(k, L)],
                                     init, ALU.mult, ALU.add)
                e.drain()
            s_dve.inc(e.nop(), 1)

        @blk.scalar
        def _(e):
            e.wait_ge(s_dve.h, dve0 + 2)
            e.activation(logdt[:], ta[:], AF.Ln)
            e.drain()
            e.activation(tbf[:], cumhs[:], AF.Exp)
            e.drain()
            s_act.inc(e.nop(), 1)
            for k in range(NCH):
                e.activation(ta[:, ts(k, L)], cumhs[:, ts(k, L)], AF.Exp, scale=-1.0,
                             bias=cumhs[:, k * L + L - 1:k * L + L])
            e.drain()
            e.activation(lam64[:], cumhs[:, L - 1::L], AF.Exp)
            e.drain()
            s_act.inc(e.nop(), 1)

        @blk.gpsimd
        def _(e):
            e.wait_ge(s_act.h, act0 + 2)
            s_ldg.inc(e.dma_start(out=bncA, in_=tbf[:]), 16)
            e.wait_ge(s_ldg.h, ldg0 + 16)

        @blk.vector
        def _(e):
            e.wait_ge(s_act.h, act0 + 3)
            e.wait_ge(s_ldg.h, ldg0 + 16)
            e.tensor_tensor(w_hs[:], ta[:], dtv[:], ALU.mult)
            e.drain()
            s_dve.inc(e.nop(), 1)

        @blk.scalar
        def _(e):
            e.wait_ge(s_dve.h, dve0 + 3)
            e.activation(tbf[:], cumabs[:], AF.Exp)
            e.drain()
            s_act.inc(e.nop(), 1)

        @blk.gpsimd
        def _(e):
            e.wait_ge(s_act.h, act0 + 4)
            s_ldg.inc(e.dma_start(out=bncB, in_=tbf[:]), 16)
            s_ldg.inc(e.dma_start(out=bncC, in_=lam64[:]), 16)
            e.wait_ge(s_ldg.h, ldg0 + 48)
            s_ldg.inc(e.dma_start(out=lamflat[:],
                                  in_=bncC.rearrange("a b -> (a b)").unsqueeze(0)), 16)
            e.wait_ge(s_ldg.h, ldg0 + 64)
            s_gp.inc(e.nop(), 1)

        NLAMMM = (NH * NCH + 511) // 512
        @blk.tensor
        def _(e):
            e.wait_ge(s_gp.h, gp0 + 1)
            for i in range(NLAMMM):
                if i >= 1:
                    e.wait_ge(s_act.h, act0 + 4 + i)
                w = min(512, NH * NCH - i * 512)
                _mi = e.matmul(pTlam[:, 0:w], onescolf[:], lamflat[0:1, i * 512:i * 512 + w],
                               start=True, stop=True)
                s_pe.inc(_mi, 1)

        @blk.scalar
        def _(e):
            for i in range(NLAMMM):
                e.wait_ge(s_pe.h, pe0 + i + 1)
                w = min(512, NH * NCH - i * 512)
                e.activation(lamb[:, i * 512:i * 512 + w], pTlam[:, 0:w], AF.Copy)
                e.drain()
                s_act.inc(e.nop(), 1)

        pe1, dve1, act1 = pe0 + NLAMMM, dve0 + 3, act0 + 4 + NLAMMM
        @blk.tensor
        def _(e):
            e.wait_ge(s_dve.h, dve1)
            e.wait_ge(s_act.h, act1)
            for k in range(NCH):
                if k > 0:
                    e.wait_ge(s_dve.h, dve1 + 2 * k)
                    e.wait_ge(s_act.h, act1 + 2 * k)
                e.transpose(pTb[:, :], logdA_b[:, ts(k, L)], ident[0:64, 0:64])
                _mi = e.transpose(pTf[:, :], logdt[:, ts(k, L)], identf[0:64, 0:64])
                s_pe.inc(_mi, 1)
                e.wait_ge(s_dve.h, dve1 + 2 * k + 1)
                e.wait_ge(s_act.h, act1 + 2 * k + 1)
                e.transpose(pTf2[:, :], w_hs[:, ts(k, L)], identf[0:64, 0:64])
                _mi = e.matmul(pTf[:, :], triu[:], logdAT[:, k * 64:(k + 1) * 64],
                               start=True, stop=True)
                s_pe.inc(_mi, 1)

        @blk.vector
        def _(e):
            for k in range(NCH):
                e.wait_ge(s_pe.h, pe1 + 2 * k + 1)
                e.tensor_copy(logdAT[:, k * 64:(k + 1) * 64], pTb[:, :])
                e.drain()
                s_dve.inc(e.nop(), 1)
                e.wait_ge(s_pe.h, pe1 + 2 * k + 2)
                e.wait_ge(s_act.h, act1 + 2 * k + 1)
                e.tensor_tensor(colterm[:, k * 64:(k + 1) * 64], ldtT[:],
                                pTf[:, :], ALU.subtract)
                e.drain()
                s_dve.inc(e.nop(), 1)

        @blk.scalar
        def _(e):
            for k in range(NCH):
                e.wait_ge(s_pe.h, pe1 + 2 * k + 1)
                e.activation(ldtT[:], pTf[:, :], AF.Copy)
                e.drain()
                s_act.inc(e.nop(), 1)
                e.wait_ge(s_pe.h, pe1 + 2 * k + 2)
                e.wait_ge(s_dve.h, dve1 + 2 * k + 2)
                e.activation(wTp[:, k * 64:(k + 1) * 64], pTf2[:, :], AF.Copy)
                e.drain()
                s_act.inc(e.nop(), 1)

        pe2, act2, dve2 = pe1 + 2 * NCH, act1 + 2 * NCH, dve1 + 2 * NCH
        @blk.tensor
        def _(e):
            e.wait_ge(s_act.h, act2)
            _mi = e.transpose(pTf[0:1, 0:64], hv[:, 2:3], identf[0:64, 0:64])
            s_pe.inc(_mi, 1)
            e.wait_ge(s_act.h, act2 + 1)
            _mi = e.matmul(pTf2[:, 0:64], onescolf[:], Drow[:], start=True, stop=True)
            s_pe.inc(_mi, 1)

        @blk.scalar
        def _(e):
            e.wait_ge(s_pe.h, pe2 + 1)
            e.activation(Drow[:], pTf[0:1, 0:64], AF.Copy)
            e.drain()
            s_act.inc(e.nop(), 1)
            e.wait_ge(s_pe.h, pe2 + 2)
            e.activation(Dbc[:], pTf2[:, 0:64], AF.Copy)
            e.drain()
            s_act.inc(e.nop(), 1)

        @blk.vector
        def _(e):
            e.wait_ge(s_act.h, act2 + 2)
            for h in range(NH):
                e.tensor_scalar(DIall[:, ts(h)], ident[:], Dbc[:, h:h + 1], None,
                                ALU.mult, ALU.bypass)
            e.drain()
            s_dve.inc(e.nop(), 1)
    phase_barrier()

    with contextlib.ExitStack() as ph:
        psb = lambda n, shp, dt: ph.enter_context(nc.sbuf_tensor(n, shp, dt))
        xq = [psb(f"xq{i}", [128, DSSM], BF16) for i in range(2)]
        smb = psb("smb", [128, L], BF16)
        btT = psb("btT", [128, 128], BF16)
        striB = [psb(f"stri{i}", [128, 1024], BF16) for i in range(2)]
        xwB = [psb(f"xw{i}", [128, 1024], BF16) for i in range(2)]
        EclB = [psb(f"ecl{i}", [128, 1024], BF16) for i in range(2)]
        EexpB = [psb(f"eexp{i}", [128, 1024], BF16) for i in range(2)]
        mppB = [psb(f"mpp{i}", [128, 1024], BF16) for i in range(2)]
        cdecB = [psb(f"cdec{i}", [128, 1024], BF16) for i in range(2)]
        ystB = [psb(f"yst{i}", [128, 1024], BF16) for i in range(2)]
        expfk = psb("expfk", [1, NH * L], BF16)
        pE = ph.enter_context(nc.psum_tensor("pE", [128, 1024], F32))
        pY = ph.enter_context(nc.psum_tensor("pY", [128, 1024], F32))
        pBS = ph.enter_context(nc.psum_tensor("pBS", [128, 1024], F32))
        pSC = ph.enter_context(nc.psum_tensor("pSC", [128, L], F32))
        pBT = ph.enter_context(nc.psum_tensor("pBT", [128, 128], BF16))
        xqs = [SEM("p3xa"), SEM("p3xb")]
        yst = [SEM("p3ya"), SEM("p3yb")]
        efs = SEM("p3ef")
        pe0, dve0, act0, gp0, tr0, ld0 = s_pe.n, s_dve.n, s_act.n, s_gp.n, s_tr.n, s_ld.n

        def pe_at(k, idx):
            return pe0 + 42 * k + idx

        def dve_at(k, idx):
            return dve0 + 42 * k + idx

        def act_at(k, idx):
            return act0 + 17 * k + idx

        def gp_at(k, idx):
            return gp0 + 8 * k + idx

        def prev2(k, g):
            gg = g - 2
            if gg < 0:
                return k - 1, gg + 8
            return k, gg

        @blk.tensor
        def _(e):
            for k in range(NCH):
                if k >= 1:
                    e.wait_ge(s_dve.h, dve_at(k - 1, 1))
                _mi = e.matmul(pSC[:, :], Bt_r[:, ts(k, L)], Ct_r[:, ts(k, L)], start=True, stop=True)
                s_pe.inc(_mi, 1)
                if k >= 1:
                    e.wait_ge(s_dve.h, dve_at(k - 1, 2))
                _mi = e.transpose(pBT[:, :], Bt_r[:, ts(k, L)], ident[:])
                s_pe.inc(_mi, 1)
                for g in range(8):
                    e.wait_ge(s_dve.h, dve_at(k, 3 + 5 * g))
                    kk, gg = (k, g - 1) if g >= 1 else (k - 1, 7)
                    if kk >= 0:
                        e.wait_ge(s_dve.h, dve_at(kk, 5 + 5 * gg))
                    for c2 in range(2):
                        _mi = e.matmul(pE[:, ts(c2, 512)], onesfull[:],
                                       striB[g % 2][:, ts(c2, 512)], start=True, stop=True)
                    s_pe.inc(_mi, 1)
                    kk, gg = (k, g - 1) if g >= 1 else (k - 1, 7)
                    if kk >= 0:
                        e.wait_ge(s_dve.h, dve_at(kk, 7 + 5 * gg))
                    e.wait_ge(efs.h, 16 * (k + 1))
                    for hh in range(8):
                        h = g * 8 + hh
                        _mi = e.matmul(pBS[:, ts(hh)], onescol[:],
                                       expfk[0:1, ts(h)],
                                       start=(hh % 4 == 0), stop=(hh % 4 == 3),
                                       skip_group_check=True)
                    s_pe.inc(_mi, 1)
                    e.wait_ge(s_gp.h, gp_at(k, 1 + g))
                    kk, gg = (k, g - 1) if g >= 1 else (k - 1, 7)
                    if kk >= 0:
                        e.wait_ge(s_act.h, act_at(kk, 3 + 2 * gg))
                    for hh in range(8):
                        h = g * 8 + hh
                        _mi = e.matmul(pY[:, ts(hh)], xq[k % 2][:, ts(h)], mppB[g % 2][:, ts(hh)],
                                       start=(hh % 4 == 0), stop=False, skip_group_check=True)
                    s_pe.inc(_mi, 1)
                    e.wait_ge(s_dve.h, dve_at(k, 6 + 5 * g))
                    e.wait_ge(s_act.h, act_at(k, 1))
                    for hh in range(8):
                        h = g * 8 + hh
                        _mi = e.matmul(pY[:, ts(hh)], hbf[:, ts(h)], cdecB[g % 2][:, ts(hh)],
                                       start=False, stop=(hh % 4 == 3), skip_group_check=True)
                    s_pe.inc(_mi, 1)
                    e.wait_ge(s_dve.h, dve_at(k, 4 + 5 * g))
                    for c2 in range(2):
                        _mi = e.matmul(pBS[:, ts(c2, 512)], btT[:], xwB[g % 2][:, ts(c2, 512)],
                                       start=True, stop=True, skip_group_check=True)
                    s_pe.inc(_mi, 1)

        @blk.vector
        def _(e):
            for k in range(NCH):
                e.wait_ge(s_pe.h, pe_at(k, 1))
                if k >= 1:
                    e.wait_ge(s_gp.h, gp_at(k - 1, 8))
                e.tensor_tensor(smb[:], pSC[:, :], triu[:], ALU.mult)
                e.drain()
                s_dve.inc(e.nop(), 1)
                e.wait_ge(s_pe.h, pe_at(k, 2))
                if k >= 1:
                    e.wait_ge(s_pe.h, pe_at(k - 1, 42))
                e.tensor_copy(btT[:], pBT[:, :])
                e.drain()
                s_dve.inc(e.nop(), 1)
                for g in range(8):
                    kk, gg = prev2(k, g)
                    if kk >= 0:
                        e.wait_ge(s_pe.h, pe_at(kk, 3 + 5 * gg))
                    tri_rep = triu[:, :].unsqueeze(1).broadcast_to([128, 8, 128])
                    ldA_rep = logdAT[:, k * 64 + g * 8:k * 64 + (g + 1) * 8].unsqueeze(2).broadcast_to([128, 8, 128])
                    e.tensor_tensor(striB[g % 2][:, :].rearrange("p (a b) -> p a b", a=8),
                                    tri_rep, ldA_rep, ALU.mult)
                    e.drain()
                    s_dve.inc(e.nop(), 1)
                    if g == 0:
                        e.wait_ge(xqs[k % 2].h, 16 * (k // 2 + 1))
                    kk, gg = prev2(k, g)
                    if kk >= 0:
                        e.wait_ge(s_pe.h, pe_at(kk, 7 + 5 * gg))
                    w_rep = wTp[:, k * 64 + g * 8:k * 64 + (g + 1) * 8].unsqueeze(2).broadcast_to([128, 8, 128])
                    e.tensor_tensor(xwB[g % 2][:, :].rearrange("p (a b) -> p a b", a=8),
                                    xq[k % 2][:, ts(g, 1024)].rearrange("p (a b) -> p a b", a=8),
                                    w_rep, ALU.mult)
                    e.drain()
                    s_dve.inc(e.nop(), 1)
                    e.wait_ge(s_pe.h, pe_at(k, 3 + 5 * g))
                    kk, gg = prev2(k, g)
                    if kk >= 0:
                        e.wait_ge(s_act.h, act_at(kk, 2 + 2 * gg))
                    ct_rep2 = colterm[:, k * 64 + g * 8:k * 64 + (g + 1) * 8].unsqueeze(2).broadcast_to([128, 8, 128])
                    e.tensor_tensor(EclB[g % 2][:, :].rearrange("p (a b) -> p a b", a=8),
                                    pE[:, :].rearrange("p (a b) -> p a b", a=8),
                                    ct_rep2, ALU.add)
                    e.drain()
                    e.tensor_scalar(EclB[g % 2][:, :], EclB[g % 2][:, :], 60.0, None,
                                    ALU.min, ALU.bypass)
                    e.drain()
                    s_dve.inc(e.nop(), 1)
                    e.wait_ge(s_pe.h, pe_at(k, 4 + 5 * g))
                    kk, gg = prev2(k, g)
                    if kk >= 0:
                        e.wait_ge(s_pe.h, pe_at(kk, 6 + 5 * gg))
                    ct_rep = Ct_r[:, ts(k, L)].unsqueeze(1).broadcast_to([128, 8, 128])
                    e.tensor_tensor(cdecB[g % 2][:, :].rearrange("p (a b) -> p a b", a=8),
                                    pBS[:, :].rearrange("p (a b) -> p a b", a=8),
                                    ct_rep, ALU.mult)
                    e.drain()
                    s_dve.inc(e.nop(), 1)
                    e.wait_ge(s_pe.h, pe_at(k, 7 + 5 * g))
                    e.wait_ge(s_act.h, act_at(k, 1))
                    lam_rep = lamb[:, k::NCH][:, g * 8:(g + 1) * 8].unsqueeze(2).broadcast_to([128, 8, 128])
                    e.tensor_tensor(hslab[:, ts(g, 1024)].rearrange("p (a b) -> p a b", a=8),
                                    hslab[:, ts(g, 1024)].rearrange("p (a b) -> p a b", a=8),
                                    lam_rep, ALU.mult)
                    e.drain()
                    e.tensor_tensor(hslab[:, ts(g, 1024)], hslab[:, ts(g, 1024)],
                                    pBS[:, :], ALU.add)
                    e.drain()
                    s_dve.inc(e.nop(), 1)

        @blk.scalar
        def _(e):
            for k in range(2):
                if k < NCH:
                    xqs[k % 2].inc(e.dma_start_transpose(xq[k % 2][:, :], xf_s[:, ts(k, L)]), 16)
            for k in range(NCH):
                if k >= 1:
                    e.wait_ge(s_dve.h, dve_at(k - 1, 42))
                e.activation(hbf[:], hslab[:], AF.Copy)
                e.drain()
                s_act.inc(e.nop(), 1)
                for g in range(8):
                    e.wait_ge(s_dve.h, dve_at(k, 5 + 5 * g))
                    kk, gg = prev2(k, g)
                    if kk >= 0:
                        e.wait_ge(s_gp.h, gp_at(kk, 1 + gg))
                    e.activation(EexpB[g % 2][:, :], EclB[g % 2][:, :], AF.Exp)
                    e.drain()
                    s_act.inc(e.nop(), 1)
                    e.wait_ge(s_pe.h, pe_at(k, 6 + 5 * g))
                    kk, gg = prev2(k, g)
                    if kk >= 0:
                        e.wait_ge(yst[gg % 2].h, 16 * ((8 * kk + gg) // 2 + 1))
                    e.activation(ystB[g % 2][:, :], pY[:, :], AF.Copy)
                    e.drain()
                    s_act.inc(e.nop(), 1)
                if k + 2 < NCH:
                    e.wait_ge(s_dve.h, dve_at(k, 39))
                    xqs[k % 2].inc(e.dma_start_transpose(xq[k % 2][:, :], xf_s[:, ts(k + 2, L)]), 16)

        @blk.gpsimd
        def _(e):
            for k in range(NCH):
                for g in range(8):
                    e.wait_ge(s_act.h, act_at(k, 2 + 2 * g))
                    e.wait_ge(s_dve.h, dve_at(k, 1))
                    kk, gg = prev2(k, g)
                    if kk >= 0:
                        e.wait_ge(s_pe.h, pe_at(kk, 5 + 5 * gg))
                    sm_rep = smb[:, :].unsqueeze(1).broadcast_to([128, 8, 128])
                    e.tensor_tensor(mppB[g % 2][:, :].rearrange("p (a b) -> p a b", a=8),
                                    EexpB[g % 2][:, :].rearrange("p (a b) -> p a b", a=8),
                                    sm_rep, ALU.mult)
                    e.drain()
                    e.tensor_tensor(mppB[g % 2][:, :], mppB[g % 2][:, :],
                                    DIall[:, ts(g, 1024)], ALU.add)
                    e.drain()
                    s_gp.inc(e.nop(), 1)

        @blk.sync
        def _(e):
            for k in range(NCH):
                if k >= 1:
                    e.wait_ge(s_pe.h, pe_at(k - 1, 39))
                efs.inc(e.dma_start(
                    out=expfk[0:1, :].rearrange("p (h t) -> p h t", h=NH),
                    in_=bncA[:, ts(k, L)].unsqueeze(0)), 16)
                for g in range(8):
                    e.wait_ge(s_act.h, act_at(k, 3 + 2 * g))
                    yst[g % 2].inc(e.dma_start(
                        out=yraw_s[g * 1024:(g + 1) * 1024, ts(k, L)].rearrange(
                            "(h p) t -> p h t", p=128),
                        in_=ystB[g % 2][:, :].rearrange("p (h t) -> p h t", h=8)), 16)

        @blk.sync
        def _(e):
            e.wait_ge(yst[0].h, yst[0].n)
            e.wait_ge(yst[1].h, yst[1].n)
            e.wait_ge(efs.h, efs.n)
            s_ld.inc(e.nop(), 16)
    phase_barrier()

    with contextlib.ExitStack() as ph:
        htmp = ph.enter_context(nc.sbuf_tensor("htmp", [128, DSSM], F32))
        dve0, gp0, cc0, ldg0 = s_dve.n, s_gp.n, s_cc.n, s_ldg.n

        @blk.vector
        def _(e):
            e.tensor_scalar(htmp[:], hslab[:], fl[:, 0:1], None, ALU.mult, ALU.bypass)
            e.drain()
            s_dve.inc(e.nop(), 1)

        @blk.gpsimd
        def _(e):
            e.wait_ge(s_dve.h, dve0 + 1)
            s_ldg.inc(e.dma_start(out=cc_in, in_=htmp[:]), 16)
            e.wait_ge(s_ldg.h, ldg0 + 16)
            cci = e.collective_compute(
                "AllReduce", ALU.add,
                replica_groups=[[2 * i, 2 * i + 1] for i in range(n_cc // 2)],
                ins=[cc_in.opt()], outs=[cc_out.opt()],
            )
            cci.then_inc(s_cc.h, 1)
            s_cc.n += 1
            e.wait_ge(s_cc.h, cc0 + 1)
            s_ldg.inc(e.dma_start(out=htmp[:], in_=cc_out), 16)
            e.wait_ge(s_ldg.h, ldg0 + 32)
            s_gp.inc(e.nop(), 1)

        @blk.vector
        def _(e):
            e.wait_ge(s_gp.h, gp0 + 1)
            e.tensor_scalar(seed[:], htmp[:], fl[:, 1:2], None, ALU.mult, ALU.bypass)
            e.drain()
            s_dve.inc(e.nop(), 1)
    phase_barrier()
    ssd2_es.close()

    assert T_MAIN % 512 == 0
    NTQ = NCH // 4
    host = {k: [] for k in range(NCH)}
    tail34 = []
    for tq in range(NTQ):
        items = [("g", tq, 0)] + [("u", tq, o8) for o8 in range(8)]
        if 4 * tq + 7 < NCH:
            host[4 * tq + 4] += items[0:3]
            host[4 * tq + 5] += items[3:5]
            host[4 * tq + 6] += items[5:7]
            host[4 * tq + 7] += items[7:9]
        else:
            tail34 += items
    ordered34 = [it for k in range(NCH) for it in host[k]] + tail34

    pe4_g, pe4_u, dve4_g, dve4_e, act4_g = {}, {}, {}, {}, {}
    _pe4 = _dve4 = _act4 = 0
    for it in ordered34:
        if it[0] == "g":
            _pe4 += 1; pe4_g[it[1]] = _pe4
            _dve4 += 1; dve4_g[it[1]] = _dve4
            _act4 += 1; act4_g[it[1]] = _act4
        else:
            ui = 8 * it[1] + it[2]
            for p in range(8):
                _pe4 += 1; pe4_u[(ui, p)] = _pe4
            for tt in range(4):
                _dve4 += 1; dve4_e[(ui, tt)] = _dve4

    with contextlib.ExitStack() as ph:
        psb = lambda n, shp, dt: ph.enter_context(nc.sbuf_tensor(n, shp, dt))
        yslab = psb("yslab", [128, DSSM], BF16)
        zslab = psb("zslab", [128, DSSM], BF16)
        silz = psb("silz", [128, DSSM], BF16)
        yg2 = [psb(f"yg2{i}", [128, 1024], BF16) for i in range(2)]
        ycB = [psb(f"yc{i}", [128, 1024], F32) for i in range(2)]
        cdaB = [psb(f"cda{i}", [128, 1024], BF16) for i in range(2)]
        expabsk = psb("expabsk", [33, 4096], BF16)
        ygr = psb("ygr", [128, NZT * 512], BF16)
        wosB = [psb(f"wos{i}", [128, 8 * 512], BF16) for i in range(2)]
        x1b = [psb(f"x1b{i}", [128, 512], F32) for i in range(4)]
        x2b = [psb(f"x2b{i}", [128, 512], F32) for i in range(4)]
        rsq = psb("rsq", [128, 4], F32)
        rln = psb("rln", [128, 4], F32)
        pB2 = ph.enter_context(nc.psum_tensor("pB2", [128, 1024], F32))
        pD = ph.enter_context(nc.psum_tensor("pD", [128, 1024], F32))
        pQ = [ph.enter_context(nc.psum_tensor(f"pQ{i}", [128, 512], F32)) for i in range(4)]
        eas = SEM("pbea")
        yls = SEM("pbyl")
        zls = SEM("pbzl")
        ygst = [SEM("pbga"), SEM("pbgb")]
        s_pe4 = SEM("spe4")
        s_dve4 = SEM("sdv4")
        s_act4 = SEM("sac4")
        p4yg = SEM("p4yg")
        p4w = [SEM("p4wa"), SEM("p4wb")]
        p4x1 = [SEM(f"p4x{i}") for i in range(4)]
        p4st = [SEM(f"p4s{i}") for i in range(4)]
        pe0, dve0, act0 = s_pe.n, s_dve.n, s_act.n

        def pe3(k, i):
            return pe0 + 16 * k + i

        def dve3(k, i):
            return dve0 + 24 * k + i

        def act3(k):
            return act0 + k + 1

        def prev2b(k, g):
            gg = g - 2
            if gg < 0:
                return k - 1, gg + 8
            return k, gg

        def ygr_t(c, tt):
            return ygr[:, (c * 4 + tt) * 128:(c * 4 + tt) * 128 + 128]

        def x2_stores(e, it):
            tq, o8 = it[1], it[2]
            ui = 8 * tq + o8
            for tt in range(4):
                m = 4 * ui + tt
                tg = tq * 4 + tt
                e.wait_ge(s_dve4.h, dve4_e[(ui, tt)])
                p4st[m % 4].inc(e.dma_start(
                    out=x2_s[tg * 128:tg * 128 + 128, ts(o8, 512)],
                    in_=x2b[m % 4][:, :]), 16)

        @blk.sync
        def _(e):
            for k in range(NCH):
                if k >= 1:
                    e.wait_ge(s_pe.h, pe3(k - 1, 15))
                eas.inc(e.dma_start(
                    out=expabsk[0:1, :].rearrange("p (h t) -> p h t", h=32),
                    in_=bncB[0:32, ts(k, L)].unsqueeze(0)), 16)
                eas.inc(e.dma_start(
                    out=expabsk[32:33, :].rearrange("p (h t) -> p h t", h=32),
                    in_=bncB[32:64, ts(k, L)].unsqueeze(0)), 16)
                if k >= 1:
                    e.wait_ge(s_dve.h, dve3(k - 1, 24))
                for g in range(8):
                    yls.inc(e.dma_start(
                        out=yslab[:, ts(g, 1024)].rearrange("p (h t) -> p h t", h=8),
                        in_=yraw_s[g * 1024:(g + 1) * 1024, ts(k, L)].rearrange(
                            "(h p) t -> p h t", p=128)), 16)
                if k >= 1:
                    e.wait_ge(s_act.h, act3(k - 1))
                for g in range(8):
                    zls.inc(e.dma_start(
                        out=zslab[:, ts(g, 1024)].rearrange("p (h t) -> p h t", h=8),
                        in_=z_s[g * 1024:(g + 1) * 1024, ts(k, L)].rearrange(
                            "(h p) t -> p h t", p=128)), 16)
                for g in range(8):
                    e.wait_ge(s_dve.h, dve3(k, 3 + 3 * g))
                    ygst[g % 2].inc(e.dma_start(
                        out=yg_s[g * 1024:(g + 1) * 1024, ts(k, L)].rearrange(
                            "(h p) t -> p h t", p=128),
                        in_=yg2[g % 2][:, :].rearrange("p (h t) -> p h t", h=8)), 16)
                if k % 4 == 3 and (k // 4) < NTQ:
                    tq = k // 4
                    e.wait_ge(ygst[0].h, 16 * 4 * (k + 1))
                    e.wait_ge(ygst[1].h, 16 * 4 * (k + 1))
                    if tq >= 1:
                        e.wait_ge(s_pe4.h, pe4_u[(8 * tq - 1, 7)])
                    for g in range(8):
                        p4yg.inc(e.dma_start(
                            out=ygr[:, g * 8 * 4 * 128:(g + 1) * 8 * 4 * 128].rearrange(
                                "p (a t) -> p a t", a=8),
                            in_=yg_s[g * 1024:(g + 1) * 1024,
                                     tq * 512:(tq + 1) * 512].rearrange(
                                "(a p) t -> p a t", p=128)), 16)
                for it in host[k]:
                    if it[0] == "u":
                        x2_stores(e, it)
            for it in tail34:
                if it[0] == "u":
                    x2_stores(e, it)

        @blk.sync
        def _(e):
            e.wait_ge(ygst[0].h, ygst[0].n)
            e.wait_ge(ygst[1].h, ygst[1].n)
            e.wait_ge(eas.h, eas.n)
            for i in range(4):
                e.wait_ge(p4st[i].h, p4st[i].n)
            s_ld.inc(e.nop(), 16)

        @blk.gpsimd
        def _(e):
            for it in ordered34:
                if it[0] != "u":
                    continue
                tq, o8 = it[1], it[2]
                ui = 8 * tq + o8
                for p in range(8):
                    q = 8 * ui + p
                    if q >= 2:
                        qq = q - 2
                        e.wait_ge(s_pe4.h, pe4_u[(qq // 8, qq % 8)])
                    p4w[q % 2].inc(e.dma_start(
                        out=wosB[q % 2][:, :].rearrange("p (a n) -> p a n", a=8),
                        in_=wout_s[p * 1024:(p + 1) * 1024, ts(o8, 512)].rearrange(
                            "(a p) n -> p a n", p=128)), 16)

        def p4_pe_pieces(it):
            if it[0] == "g":
                tq = it[1]

                def mkg(tt):
                    def f(e):
                        if tt == 0:
                            e.wait_ge(p4yg.h, 16 * 8 * (tq + 1))
                            if tq >= 1:
                                e.wait_ge(s_dve4.h, dve4_e[(8 * tq - 1, 3)])
                        for c in range(64):
                            _mi = e.matmul(pQ[tt][:, 0:128], ygr_t(c, tt), ygr_t(c, tt),
                                           start=(c == 0), stop=(c == 63),
                                           skip_group_check=True)
                        if tt == 3:
                            s_pe4.inc(_mi, 1)
                    return f
                return [mkg(tt) for tt in range(4)]
            tq, o8 = it[1], it[2]
            ui = 8 * tq + o8

            def mkp(p):
                def f(e):
                    if p == 0:
                        if o8 == 0:
                            e.wait_ge(s_dve4.h, dve4_g[tq])
                        else:
                            e.wait_ge(s_dve4.h, dve4_e[(ui - 1, 3)])
                    q = 8 * ui + p
                    e.wait_ge(p4w[q % 2].h, 16 * (q // 2 + 1))
                    for cc in range(8):
                        c = 8 * p + cc
                        for tt in range(4):
                            _mi = e.matmul(pQ[tt][:, :], ygr_t(c, tt),
                                           wosB[q % 2][:, cc * 512:(cc + 1) * 512],
                                           start=(c == 0), stop=(c == 63),
                                           skip_group_check=True)
                    s_pe4.inc(_mi, 1)
                return f
            return [mkp(p) for p in range(8)]

        @blk.tensor
        def _(e):
            for k in range(NCH):
                early, late = [], []
                for idx, it in enumerate(host[k]):
                    if it[0] == "g" or idx == 0:
                        early += p4_pe_pieces(it)
                    else:
                        late.append(it)
                ei = 0
                for g in range(8):
                    if ei < len(early):
                        early[ei](e)
                        ei += 1
                    kk, gg = (k, g - 1) if g >= 1 else (k - 1, 7)
                    if kk >= 0:
                        e.wait_ge(s_dve.h, dve3(kk, 1 + 3 * gg))
                    e.wait_ge(eas.h, 32 * (k + 1))
                    for hh in range(8):
                        h = g * 8 + hh
                        r = 32 * (h // 32)
                        _mi = e.matmul(pB2[:, ts(hh)], ones33[r:r + 1, :],
                                       expabsk[r:r + 1, (h % 32) * 128:(h % 32) * 128 + 128],
                                       start=(hh % 4 == 0), stop=(hh % 4 == 3),
                                       skip_group_check=True)
                    s_pe.inc(_mi, 1)
                    if ei < len(early):
                        early[ei](e)
                        ei += 1
                    e.wait_ge(s_dve.h, dve3(k, 1 + 3 * g))
                    kk, gg = (k, g - 1) if g >= 1 else (k - 1, 7)
                    if kk >= 0:
                        e.wait_ge(s_dve.h, dve3(kk, 2 + 3 * gg))
                    for hh in range(8):
                        h = g * 8 + hh
                        _mi = e.matmul(pD[:, ts(hh)], seed[:, ts(h)], cdaB[g % 2][:, ts(hh)],
                                       start=(hh % 4 == 0), stop=(hh % 4 == 3),
                                       skip_group_check=True)
                    s_pe.inc(_mi, 1)
                while ei < len(early):
                    early[ei](e)
                    ei += 1
                for it in late:
                    for f in p4_pe_pieces(it):
                        f(e)
            for it in tail34:
                for f in p4_pe_pieces(it):
                    f(e)

        def p4_dve(e, it):
            if it[0] == "g":
                tq = it[1]
                e.wait_ge(s_pe4.h, pe4_g[tq])
                if tq >= 1:
                    e.wait_ge(s_act4.h, act4_g[tq - 1])
                e.memset(rsq[:, :], 0.0)
                e.drain()
                for tt in range(4):
                    e.tensor_tensor(ycB[0][:, 0:128], pQ[tt][:, 0:128], identf[:], ALU.mult)
                    e.drain()
                    e.tensor_scalar(ycB[0][:, 128:256], ycB[0][:, 0:128], 1.0, None,
                                    ALU.mult, ALU.add, accum_out=rsq[:, tt:tt + 1])
                    e.drain()
                s_dve4.inc(e.nop(), 1)
            else:
                tq, o8 = it[1], it[2]
                ui = 8 * tq + o8
                e.wait_ge(s_pe4.h, pe4_u[(ui, 7)])
                e.wait_ge(s_act4.h, act4_g[tq])
                for tt in range(4):
                    m = 4 * ui + tt
                    tg = tq * 4 + tt
                    e.wait_ge(p4x1[m % 4].h, 16 * (m // 4 + 1))
                    if m >= 4:
                        e.wait_ge(p4st[m % 4].h, 16 * (m // 4))
                    e.scalar_tensor_tensor(x2b[m % 4][:, :], pQ[tt][:, :],
                                           rstd3p[:, tg:tg + 1], x1b[m % 4][:, :],
                                           ALU.mult, ALU.add)
                    e.drain()
                    s_dve4.inc(e.nop(), 1)

        @blk.vector
        def _(e):
            for k in range(NCH):
                for g in range(8):
                    e.wait_ge(s_pe.h, pe3(k, 2 * g + 1))
                    kk, gg = prev2b(k, g)
                    if kk >= 0:
                        e.wait_ge(s_pe.h, pe3(kk, 2 * gg + 2))
                    ct_rep = Ct_r[:, ts(k, L)].unsqueeze(1).broadcast_to([128, 8, 128])
                    e.tensor_tensor(cdaB[g % 2][:, :].rearrange("p (a b) -> p a b", a=8),
                                    pB2[:, :].rearrange("p (a b) -> p a b", a=8),
                                    ct_rep, ALU.mult)
                    e.drain()
                    s_dve.inc(e.nop(), 1)
                    e.wait_ge(s_pe.h, pe3(k, 2 * g + 2))
                    e.wait_ge(yls.h, 16 * 8 * (k + 1))
                    e.tensor_tensor(ycB[g % 2][:, :], yslab[:, ts(g, 1024)], pD[:, :], ALU.add)
                    e.drain()
                    s_dve.inc(e.nop(), 1)
                    e.wait_ge(s_act.h, act3(k))
                    v = 16 * (4 * k + g // 2)
                    if v > 0:
                        e.wait_ge(ygst[g % 2].h, v)
                    e.tensor_tensor(yg2[g % 2][:, :], ycB[g % 2][:, :],
                                    silz[:, ts(g, 1024)], ALU.mult)
                    e.drain()
                    s_dve.inc(e.nop(), 1)
                for it in host[k]:
                    p4_dve(e, it)
            for it in tail34:
                p4_dve(e, it)

        def p4_act(e, it):
            if it[0] == "g":
                tq = it[1]
                e.wait_ge(s_dve4.h, dve4_g[tq])
                e.activation(rln[:, 0:4], rsq[:, 0:4], AF.Ln, scale=1.0 / DSSM,
                             bias=epscol[:, 0:1])
                e.drain()
                e.activation(rstd3p[:, 4 * tq:4 * tq + 4], rln[:, 0:4], AF.Exp, scale=-0.5)
                e.drain()
                s_act4.inc(e.nop(), 1)
            else:
                tq, o8 = it[1], it[2]
                ui = 8 * tq + o8
                for tt in range(4):
                    m = 4 * ui + tt
                    tg = tq * 4 + tt
                    if m >= 4:
                        e.wait_ge(s_dve4.h, dve4_e[((m - 4) // 4, (m - 4) % 4)])
                    p4x1[m % 4].inc(e.dma_start(
                        out=x1b[m % 4][:, :],
                        in_=x1_s[3 + tg * 128:3 + tg * 128 + 128, ts(o8, 512)]), 16)

        @blk.scalar
        def _(e):
            for k in range(NCH):
                e.wait_ge(zls.h, 16 * 8 * (k + 1))
                if k >= 1:
                    e.wait_ge(s_dve.h, dve3(k - 1, 24))
                e.activation(silz[:], zslab[:], AF.Silu)
                e.drain()
                s_act.inc(e.nop(), 1)
                for it in host[k]:
                    p4_act(e, it)
            for it in tail34:
                p4_act(e, it)
    phase_barrier()
    ssd_es.close()
    mid_es.close()


    emit_ln("l3", None, x2_s, T_MAIN, ln3_s)
    phase_barrier()
    emit_ffn("f2", ln3_s, T_MAIN, None, x2_s, out_t)
    phase_barrier()

    @blk.sync
    def _(e):
        for name, dap in dbg.items():
            s_ld.inc(e.dma_start(out=dap, in_=dbg_map[name]), 16)
        e.wait_ge(s_ld.h, s_ld.n)
        e.wait_ge(s_ldg.h, s_ldg.n)
        e.wait_ge(s_tr.h, s_tr.n)

    es.close()
    return nc, dbg_map


def _aux_inputs():
    triu_np = np.triu(np.ones((128, 128), np.float32)).astype(ml_dtypes.bfloat16)
    ident_np = np.eye(128, dtype=np.float32)
    return triu_np, ident_np.astype(ml_dtypes.bfloat16), ident_np


def make_in_maps(inputs, T_MAIN=2048):
    x = np.ascontiguousarray(np.asarray(inputs["L0_attn_hidden_states"], np.float32))
    W_in = np.ascontiguousarray(np.asarray(inputs["L0_attn_W_in"], np.float32))
    W_out = np.ascontiguousarray(np.asarray(inputs["L0_attn_W_out"], np.float32))
    conv_w = np.ascontiguousarray(np.asarray(inputs["L0_attn_conv_w"], np.float32))
    dt_bias = np.asarray(inputs["L0_attn_dt_bias"], np.float32)
    A_log = np.asarray(inputs["L0_attn_A_log"], np.float32)
    D_param = np.asarray(inputs["L0_attn_D_param"], np.float32)
    nw = np.asarray(inputs["L0_attn_norm_weight"], np.float32).reshape(DSSM, 1)
    wp = np.asarray(inputs["L0_ffn_weight_packed"], np.int32)
    sce = np.asarray(inputs["L0_ffn_scales"])
    pT = np.ascontiguousarray(wp.view(np.uint8)[:, 0::4].T)
    scT = np.ascontiguousarray(np.repeat(
        np.asarray(sce).view(np.uint16).T, 16, axis=0)).view(np.float16)
    hvecs = np.stack([dt_bias, A_log, D_param, np.zeros_like(D_param)], axis=1)
    hvecs = np.ascontiguousarray(hvecs.astype(np.float32))
    triu_np, ident_b, ident_f = _aux_inputs()

    Bsz, S, _ = x.shape
    nhalf = S // T_MAIN
    in_maps = []
    for core in range(8):
        b, half = core // nhalf if nhalf > 1 else core // 2, core % nhalf if nhalf > 1 else core % 2
        b, half = divmod(core, nhalf) if nhalf > 1 else (core // 2, core % 2)
        xm = np.ascontiguousarray(x[b, half * T_MAIN:(half + 1) * T_MAIN])
        if half == 0:
            xp = np.zeros((3, H), np.float32)
        else:
            xp = np.ascontiguousarray(x[b, half * T_MAIN - 3:half * T_MAIN])
        flags = np.array([[1.0 if half == 0 else 0.0,
                           0.0 if half == 0 else 1.0]], np.float32)
        in_maps.append({
            "x_main": xm, "x_pre": xp, "W_in": W_in, "W_out": W_out,
            "packedT": pT, "scalesE": scT, "conv_win": conv_w, "hvecs": hvecs,
            "nw_in": np.ascontiguousarray(nw), "flags_in": flags,
            "triu_in": triu_np, "ident_in": ident_b, "identf_in": ident_f,
        })
    return in_maps


_BUILT = {}


def kernel(**inputs):
    T_MAIN = 2048
    key = T_MAIN
    if key not in _BUILT:
        _BUILT[key] = build_program(T_MAIN)
    nc, _ = _BUILT[key]
    in_maps = make_in_maps(inputs, T_MAIN)
    res = run_bass_kernel_spmd(nc, in_maps, core_ids=list(range(8)))
    x = np.asarray(inputs["L0_attn_hidden_states"])
    Bsz, S, _ = x.shape
    out = np.empty((Bsz, S, H), np.float32)
    for core in range(8):
        b, half = divmod(core, S // T_MAIN)
        out[b, half * T_MAIN:(half + 1) * T_MAIN] = res.results[core]["out"]
    return out


# version 15 (final)
# speedup vs baseline: 1.0777x; 1.0263x over previous
import contextlib
import numpy as np
import ml_dtypes
import concourse.bass as bass
import concourse.mybir as mybir
from concourse.bass_utils import run_bass_kernel_spmd

F32 = mybir.dt.float32
BF16 = mybir.dt.bfloat16
F16 = mybir.dt.float16
U8 = mybir.dt.uint8
I32 = mybir.dt.int32
AF = mybir.ActivationFunctionType
ALU = mybir.AluOpType

H = 4096
DSSM = 8192
NH = 64
CONV = 8448
PROJ = 16704
KTAPS = 4
EPS = 1e-5
L = 128
NDT = H // 128
NZT = DSSM // 128
NCT = CONV // 128


class Sem:
    def __init__(self, h):
        self.h = h
        self.n = 0
    def inc(self, instr, k):
        instr.then_inc(self.h, k)
        self.n += k
        return self.n


def ts(i, n=128):
    return slice(i * n, (i + 1) * n)


def build_program(T_MAIN=2048, debug_outs=(), n_cc=8):
    NCH = T_MAIN // L
    NTT = T_MAIN // 128
    TP = T_MAIN + 3
    TPAD = ((TP + 15) // 16) * 16
    NGRP = 8
    GH = NH // NGRP

    CHUNKS = []
    off = 0
    while TP - off > 3:
        sz = min(512, TP - 3 - off)
        CHUNKS.append((off, sz))
        off += sz
    CHUNKS.append((off, 3))
    NCHK = len(CHUNKS)

    nc = bass.Bass("TRN2", target_bir_lowering=False, debug=False)

    x_main = nc.dram_tensor("x_main", [T_MAIN, H], F32, kind="ExternalInput").ap()
    x_pre = nc.dram_tensor("x_pre", [3, H], F32, kind="ExternalInput").ap()
    W_in = nc.dram_tensor("W_in", [H, PROJ], F32, kind="ExternalInput").ap()
    W_out = nc.dram_tensor("W_out", [DSSM, H], F32, kind="ExternalInput").ap()
    packedT = nc.dram_tensor("packedT", [2048, H], U8, kind="ExternalInput").ap()
    scalesE = nc.dram_tensor("scalesE", [2048, H], F16, kind="ExternalInput").ap()
    conv_win = nc.dram_tensor("conv_win", [CONV, KTAPS], F32, kind="ExternalInput").ap()
    hvecs = nc.dram_tensor("hvecs", [NH, 4], F32, kind="ExternalInput").ap()
    nw_in = nc.dram_tensor("nw_in", [DSSM, 1], F32, kind="ExternalInput").ap()
    flags_in = nc.dram_tensor("flags_in", [1, 2], F32, kind="ExternalInput").ap()
    triu_in = nc.dram_tensor("triu_in", [128, 128], BF16, kind="ExternalInput").ap()
    ident_in = nc.dram_tensor("ident_in", [128, 128], BF16, kind="ExternalInput").ap()
    identf_in = nc.dram_tensor("identf_in", [128, 128], F32, kind="ExternalInput").ap()
    out_t = nc.dram_tensor("out", [T_MAIN, H], F32, kind="ExternalOutput").ap()

    wdeq_s = nc.dram_tensor("wdeq_s", [H, H], BF16).ap()
    ln1_s = nc.dram_tensor("ln1_s", [TPAD, H], BF16).ap()
    x1_s = nc.dram_tensor("x1_s", [TP, H], F32).ap()
    ln2_s = nc.dram_tensor("ln2_s", [TPAD, H], BF16).ap()
    z_s = nc.dram_tensor("z_s", [DSSM, T_MAIN], BF16).ap()
    xf_s = nc.dram_tensor("xf_s", [DSSM, T_MAIN], BF16).ap()
    yraw_s = nc.dram_tensor("yraw_s", [DSSM, T_MAIN], BF16).ap()
    yg_s = nc.dram_tensor("yg_s", [DSSM, T_MAIN], BF16).ap()
    x2_s = nc.dram_tensor("x2_s", [T_MAIN, H], F32).ap()
    TPAD3 = ((T_MAIN + 15) // 16) * 16
    ln3_s = nc.dram_tensor("ln3_s", [TPAD3, H], BF16).ap()
    wout_s = nc.dram_tensor("wout_s", [DSSM, H], BF16).ap()
    cc_in = nc.dram_tensor("cc_in", [128, DSSM], F32).ap()
    cc_out = nc.dram_tensor("cc_out", [128, DSSM], F32).ap()
    bncA = nc.dram_tensor("bncA", [NH, T_MAIN], BF16).ap()
    bncB = nc.dram_tensor("bncB", [NH, T_MAIN], BF16).ap()
    bncC = nc.dram_tensor("bncC", [NH, NCH], F32).ap()

    dbg_map = {"wdeq": wdeq_s, "ln1": ln1_s, "x1": x1_s, "ln2": ln2_s, "z": z_s,
               "xf": xf_s, "yraw": yraw_s, "yg": yg_s, "x2": x2_s, "ln3": ln3_s}
    dbg = {}
    for name in debug_outs:
        src = dbg_map[name]
        dbg[name] = nc.dram_tensor("dbg_" + name, list(src.tensor.shape), src.dtype,
                                   kind="ExternalOutput").ap()

    es = contextlib.ExitStack()
    sbuf = lambda n, shp, dt: es.enter_context(nc.sbuf_tensor(n, shp, dt))
    SEM = lambda n: Sem(es.enter_context(nc.semaphore(n)))

    triu = sbuf("triu", [128, 128], BF16)
    ident = sbuf("ident", [128, 128], BF16)
    identf = sbuf("identf", [128, 128], F32)
    onescol = sbuf("onescol", [1, 128], BF16)
    ones33 = sbuf("ones33", [33, 128], BF16)
    onescolf = sbuf("onescolf", [1, 128], F32)
    onesfull = sbuf("onesfull", [128, 128], BF16)
    fl = sbuf("fl", [128, 2], F32)
    hv = sbuf("hv", [NH, 4], F32)
    cw2 = sbuf("cw2", [128, NCT * KTAPS], F32)
    nw2 = sbuf("nw2", [128, NZT], F32)
    epscol = sbuf("epscol", [128, 1], F32)
    onecolf2 = sbuf("onecolf2", [128, 1], F32)
    rstd3p = sbuf("rstd3p", [128, NCH], F32)

    s_ld = SEM("s_ld")
    s_ldg = SEM("s_ldg")
    s_tr = SEM("s_tr")
    s_pe = SEM("s_pe")
    s_dve = SEM("s_dve")
    s_act = SEM("s_act")
    s_gp = SEM("s_gp")
    s_cc = SEM("s_cc")

    blk = es.enter_context(nc.Block())

    def phase_barrier():
        c = dict(ld=s_ld.n, ldg=s_ldg.n, tr=s_tr.n, pe=s_pe.n, dve=s_dve.n,
                 act=s_act.n, gp=s_gp.n)
        for sec in (blk.sync, blk.gpsimd, blk.vector, blk.scalar, blk.tensor):
            @sec
            def _(e):
                e.wait_ge(s_ld.h, c["ld"]); e.wait_ge(s_ldg.h, c["ldg"])
                e.wait_ge(s_tr.h, c["tr"]); e.wait_ge(s_pe.h, c["pe"])
                e.wait_ge(s_dve.h, c["dve"]); e.wait_ge(s_act.h, c["act"])
                e.wait_ge(s_gp.h, c["gp"])

    @blk.sync
    def _(e):
        s_ld.inc(e.dma_start(out=triu[:], in_=triu_in), 16)
        s_ld.inc(e.dma_start(out=ident[:], in_=ident_in), 16)
        s_ld.inc(e.dma_start(out=identf[:], in_=identf_in), 16)
        s_ld.inc(e.dma_start(out=hv[:], in_=hvecs), 16)
        s_ld.inc(e.dma_start(out=fl[:], in_=flags_in.squeeze(0).partition_broadcast(128)), 16)
        s_ld.inc(e.dma_start(out=cw2[:].rearrange("p (a k) -> p a k", k=KTAPS), in_=conv_win.rearrange("(a p) k -> p a k", p=128)), 16)
        s_ld.inc(e.dma_start(out=nw2[:].unsqueeze(2), in_=nw_in.rearrange("(a p) b -> p a b", p=128), allow_slow_non_contiguous=True), 16)

    @blk.vector
    def _(e):
        e.wait_ge(s_ld.h, s_ld.n)
        e.memset(onescol[:], 1.0)
        e.memset(ones33[:], 1.0)
        e.memset(onescolf[:], 1.0)
        e.memset(onesfull[:], 1.0)
        e.memset(epscol[:], float(EPS))
        e.memset(onecolf2[:], 1.0)
        e.drain()
        s_dve.inc(e.nop(), 1)
    phase_barrier()

    with contextlib.ExitStack() as ph:
        psb = lambda n, shp, dt: ph.enter_context(nc.sbuf_tensor(n, shp, dt))
        pk = [psb(f"pk{i}", [128, H], U8) for i in range(2)]
        sc = [psb(f"sc{i}", [128, H], F16) for i in range(2)]
        lohi = psb("lohi", [128, 2 * H], U8)
        tg0 = psb("tg0", [128, 2 * H], BF16)
        wdq = [psb(f"wdq{i}", [128, 2 * H], BF16) for i in range(2)]
        wofp = [psb(f"wofp{i}", [128, 2 * H], F32) for i in range(2)]
        wosp = [psb(f"wosp{i}", [128, 2 * H], BF16) for i in range(2)]
        p0ld = [SEM("p0lda"), SEM("p0ldb")]
        p0st = [SEM("p0sta"), SEM("p0stb")]
        wls = [SEM("p0wla"), SEM("p0wlb")]
        wss = [SEM("p0wsa"), SEM("p0wsb")]
        gp2 = SEM("p0gp2")
        dv2 = SEM("p0dv2")
        dve0, gp0 = s_dve.n, s_gp.n

        @blk.sync
        def _(e):
            for i in range(16):
                if i >= 2:
                    e.wait_ge(s_gp.h, gp0 + (i - 1))
                p0ld[i % 2].inc(e.dma_start(out=pk[i % 2][:], in_=packedT[ts(i), :]), 16)
                p0ld[i % 2].inc(e.dma_start(out=sc[i % 2][:], in_=scalesE[ts(i), :]), 16)

        @blk.vector
        def _(e):
            for i in range(16):
                e.wait_ge(p0ld[i % 2].h, 32 * (i // 2 + 1))
                if i >= 2:
                    e.wait_ge(p0st[i % 2].h, 32 * (i // 2))
                p, w, sci = pk[i % 2], wdq[i % 2], sc[i % 2]
                e.tensor_scalar(lohi[:, 0:H], p[:], 15, None, ALU.bitwise_and, ALU.bypass)
                e.tensor_scalar(lohi[:, H:2 * H].bitcast(I32), p[:].bitcast(I32), 4, 0x0F0F0F0F,
                                ALU.logical_shift_right, ALU.bitwise_and)
                e.drain()
                e.tensor_scalar(tg0[:, :], lohi[:, :], -8.0, None, ALU.add, ALU.bypass)
                e.drain()
                e.tensor_tensor(w[:, 0:H], tg0[:, 0:H], sci[:], ALU.mult)
                e.tensor_tensor(w[:, H:2 * H], tg0[:, H:2 * H], sci[:], ALU.mult)
                e.drain()
                s_gp.inc(e.nop(), 1)

        own_gp = [i % 4 == 3 for i in range(32)]
        dcnt = [sum(1 for j in range(i + 1) if not own_gp[j]) for i in range(32)]
        gcnt = [sum(1 for j in range(i + 1) if own_gp[j]) for i in range(32)]

        def presc(e, i, sem):
            e.wait_ge(wls[i % 2].h, 16 * (i // 2 + 1))
            if i >= 2:
                e.wait_ge(wss[i % 2].h, 16 * (i // 2))
            nwrep = nw2[:, i * 2:(i + 1) * 2].unsqueeze(2).broadcast_to([128, 2, H])
            e.tensor_tensor(wosp[i % 2][:, :].rearrange("p (a n) -> p a n", a=2),
                            wofp[i % 2][:, :].rearrange("p (a n) -> p a n", a=2),
                            nwrep, ALU.mult)
            e.drain()
            sem.inc(e.nop(), 1)

        @blk.vector
        def _(e):
            for i in range(32):
                if not own_gp[i]:
                    presc(e, i, dv2)

        @blk.gpsimd
        def _(e):
            def load(i):
                wls[i % 2].inc(e.dma_start(
                    out=wofp[i % 2][:, :].rearrange("p (a n) -> p a n", a=2),
                    in_=W_out[i * 256:(i + 1) * 256, :].rearrange("(a p) n -> p a n", p=128)), 16)
            load(0)
            load(1)
            for i in range(32):
                if own_gp[i]:
                    presc(e, i, gp2)
                else:
                    e.wait_ge(dv2.h, dcnt[i])
                if i + 2 < 32:
                    load(i + 2)

        @blk.scalar
        def _(e):
            for i in range(16):
                e.wait_ge(s_gp.h, gp0 + (i + 1))
                p0st[i % 2].inc(e.dma_start(out=wdeq_s[ts(i), :], in_=wdq[i % 2][:, 0:H]), 16)
                p0st[i % 2].inc(e.dma_start(out=wdeq_s[ts(16 + i), :], in_=wdq[i % 2][:, H:2 * H]), 16)

        @blk.sync
        def _(e):
            for i in range(32):
                if not own_gp[i]:
                    e.wait_ge(dv2.h, dcnt[i])
                else:
                    e.wait_ge(gp2.h, gcnt[i])
                wss[i % 2].inc(e.dma_start(
                    out=wout_s[i * 256:(i + 1) * 256, :].rearrange("(a p) n -> p a n", p=128),
                    in_=wosp[i % 2][:, :].rearrange("p (a n) -> p a n", a=2)), 16)

        @blk.scalar
        def _(e):
            e.wait_ge(p0st[0].h, p0st[0].n)
            e.wait_ge(p0st[1].h, p0st[1].n)
            s_act.inc(e.nop(), 1)

        @blk.sync
        def _(e):
            e.wait_ge(wss[0].h, wss[0].n)
            e.wait_ge(wss[1].h, wss[1].n)
            s_ld.inc(e.nop(), 16)
    phase_barrier()

    def resid_load2(e, sem, buf, r0, r1, cols, pre_ap, main_ap):
        w = cols.stop - cols.start
        if pre_ap is not None:
            if r0 < 3:
                sem.inc(e.dma_start(out=buf[0:3 - r0, 0:w], in_=pre_ap[r0:3, cols]), 16)
            if r1 > 3:
                a = max(r0, 3)
                sem.inc(e.dma_start(out=buf[a - r0:r1 - r0, 0:w], in_=main_ap[a - 3:r1 - 3, cols]), 16)
        else:
            sem.inc(e.dma_start(out=buf[0:r1 - r0, 0:w], in_=main_ap[r0:r1, cols]), 16)

    def resid_load(e, buf, r0, r1, cols, pre_ap, main_ap):
        w = cols.stop - cols.start
        if pre_ap is not None:
            if r0 < 3:
                s_ld.inc(e.dma_start(out=buf[0:3 - r0, 0:w], in_=pre_ap[r0:3, cols]), 16)
            if r1 > 3:
                a = max(r0, 3)
                s_ld.inc(e.dma_start(out=buf[a - r0:r1 - r0, 0:w], in_=main_ap[a - 3:r1 - 3, cols]), 16)
        else:
            s_ld.inc(e.dma_start(out=buf[0:r1 - r0, 0:w], in_=main_ap[r0:r1, cols]), 16)

    def resid_nld(r0, r1, has_pre):
        return ((1 if r0 < 3 else 0) + (1 if r1 > 3 else 0)) if has_pre else 1

    def emit_ln(tag, src_pre, src_main, nrows, dst):
        ntile = (nrows + 127) // 128
        hp = src_pre is not None
        with contextlib.ExitStack() as ph:
            psb = lambda n, shp, dt: ph.enter_context(nc.sbuf_tensor(n, shp, dt))
            xs = [psb(f"{tag}x{i}", [128, H], F32) for i in range(2)]
            xs2 = [psb(f"{tag}y{i}", [128, H], F32) for i in range(2)]
            dmov = psb(f"{tag}dm", [128, 512], BF16)
            dps = ph.enter_context(nc.psum_tensor(f"{tag}dp", [128, 512], F32))
            disc = psb(f"{tag}d", [128, H], BF16)
            disc2 = psb(f"{tag}d2", [128, H], BF16)
            zpad = psb(f"{tag}z", [16, H], BF16)
            lnt = [psb(f"{tag}l{i}", [128, H], BF16) for i in range(2)]
            cols = psb(f"{tag}c", [128, 8 * ntile], F32)
            lda = [SEM(tag + "la"), SEM(tag + "lb")]
            ldb = [SEM(tag + "ma"), SEM(tag + "mb")]
            stp = [SEM(tag + "sa"), SEM(tag + "sb")]
            dve0, act0 = s_dve.n, s_act.n

            npt = [resid_nld(i * 128, min((i + 1) * 128, nrows), hp) for i in range(ntile)]

            def cum_par(npt_list, i):
                return 16 * sum(npt_list[j] for j in range(i % 2, i + 1, 2))

            @blk.sync
            def _(e):
                for i in range(ntile):
                    if i >= 2:
                        e.wait_ge(s_act.h, act0 + (i - 1))
                    resid_load2(e, lda[i % 2], xs[i % 2], i * 128,
                                min((i + 1) * 128, nrows), slice(0, H), src_pre, src_main)

            @blk.vector
            def _(e):
                e.memset(cols[:], 0.0)
                e.memset(zpad[:], 0.0)
                e.memset(dmov[:], 1.0)
                e.drain()
                for i in range(ntile):
                    r = min((i + 1) * 128, nrows) - i * 128
                    e.wait_ge(lda[i % 2].h, cum_par(npt, i))
                    if i >= 2:
                        e.wait_ge(s_act.h, act0 + i - 1)
                    e.tensor_scalar(disc[0:r, :], xs[i % 2][0:r, :], 1.0, None, ALU.mult, ALU.add,
                                    accum_out=cols[0:r, i:i + 1])
                    e.drain()
                    s_dve.inc(e.nop(), 1)

            @blk.scalar
            def _(e):
                for i in range(ntile):
                    r = min((i + 1) * 128, nrows) - i * 128
                    e.wait_ge(s_dve.h, dve0 + i + 1)
                    e.activation(disc2[0:r, :], xs[i % 2][0:r, :], AF.Square,
                                 accum_out=cols[0:r, ntile + i:ntile + i + 1])
                    e.drain()
                    s_act.inc(e.nop(), 1)
                e.activation(cols[:, 2 * ntile:3 * ntile], cols[:, 0:ntile], AF.Copy, scale=1.0 / H)
                e.drain()
                e.activation(cols[:, 3 * ntile:4 * ntile], cols[:, 2 * ntile:3 * ntile], AF.Square)
                e.drain()
                s_act.inc(e.nop(), 1)

            @blk.vector
            def _(e):
                e.wait_ge(s_act.h, act0 + ntile + 1)
                e.tensor_scalar(cols[:, 4 * ntile:5 * ntile], cols[:, ntile:2 * ntile], 1.0 / H,
                                None, ALU.mult, ALU.bypass)
                e.drain()
                e.tensor_tensor(cols[:, 5 * ntile:6 * ntile], cols[:, 4 * ntile:5 * ntile],
                                cols[:, 3 * ntile:4 * ntile], ALU.subtract)
                e.drain()
                s_dve.inc(e.nop(), 1)

            @blk.scalar
            def _(e):
                e.wait_ge(s_dve.h, dve0 + ntile + 1)
                e.activation(cols[:, 6 * ntile:7 * ntile], cols[:, 5 * ntile:6 * ntile],
                             AF.Ln, bias=epscol[:, 0:1])
                e.drain()
                e.activation(cols[:, 7 * ntile:8 * ntile], cols[:, 6 * ntile:7 * ntile],
                             AF.Exp, scale=-0.5)
                e.drain()
                s_act.inc(e.nop(), 1)

            dve1 = s_dve.n
            @blk.sync
            def _(e):
                for i in range(ntile):
                    if i >= 2:
                        e.wait_ge(s_dve.h, dve1 + (i - 1))
                    resid_load2(e, ldb[i % 2], xs2[i % 2], i * 128,
                                min((i + 1) * 128, nrows), slice(0, H), src_pre, src_main)

            @blk.vector
            def _(e):
                e.wait_ge(s_act.h, act0 + ntile + 2)
                for i in range(ntile):
                    r = min((i + 1) * 128, nrows) - i * 128
                    e.wait_ge(ldb[i % 2].h, cum_par(npt, i))
                    if i >= 2:
                        e.wait_ge(stp[i % 2].h, 16 * (i // 2))
                    e.tensor_scalar(lnt[i % 2][0:r, :].rearrange("p (eo a) -> p eo a", eo=2),
                                    xs2[i % 2][0:r, :].rearrange("p (a eo) -> p eo a", eo=2),
                                    cols[0:r, 2 * ntile + i:2 * ntile + i + 1],
                                    cols[0:r, 7 * ntile + i:7 * ntile + i + 1],
                                    ALU.subtract, ALU.mult)
                    e.drain()
                    s_dve.inc(e.nop(), 1)

            @blk.scalar
            def _(e):
                for i in range(ntile):
                    r0, r1 = i * 128, min((i + 1) * 128, nrows)
                    e.wait_ge(s_dve.h, dve1 + i + 1)
                    stp[i % 2].inc(e.dma_start(out=dst[r0:r1, :], in_=lnt[i % 2][0:r1 - r0, :]), 16)
                dstpad = dst.tensor.shape[0]
                if dstpad > nrows:
                    stp[0].inc(e.dma_start(out=dst[nrows:dstpad, :],
                                           in_=zpad[0:dstpad - nrows, :]), 16)

            @blk.scalar
            def _(e):
                e.wait_ge(stp[0].h, stp[0].n)
                e.wait_ge(stp[1].h, stp[1].n)
                s_act.inc(e.nop(), 1)

            @blk.tensor
            def _(e):
                for base in (dve0, dve1):
                    for i in range(ntile):
                        e.wait_ge(s_dve.h, base + i + 1)
                        for _ in range(24):
                            _mi = e.matmul(dps[:, :], ident[:], dmov[:],
                                           start=True, stop=True, skip_group_check=True)
                s_pe.inc(_mi, 1)

    def emit_ffn(tag, ln_src, ln_rows, resid_pre, resid_main, dst):
        ntile = (ln_rows + 127) // 128
        hp = resid_pre is not None
        qgrp = [list(range(g, min(g + 2, ntile))) for g in range(0, ntile, 2)]
        NPAIR = 2
        with contextlib.ExitStack() as ph:
            psb = lambda n, shp, dt: ph.enter_context(nc.sbuf_tensor(n, shp, dt))
            wq = [psb(f"{tag}wq{i}", [128, NDT * 1024], BF16) for i in range(2)]
            lnq = [psb(f"{tag}lq{i}", [128, NDT * 272], BF16) for i in range(2)]
            xres = [psb(f"{tag}xr{i}", [128, 2048], F32) for i in range(2)]
            ot = [psb(f"{tag}ot{i}", [128, 2048], F32) for i in range(2)]
            pss = [ph.enter_context(nc.psum_tensor(f"{tag}p{i}", [128, 2048], F32)) for i in range(2)]
            ldx = [SEM(tag + "xa"), SEM(tag + "xb")]
            trq = [SEM(tag + "ta"), SEM(tag + "tb")]
            wqs = [SEM(tag + "wqa"), SEM(tag + "wqb")]
            stp = [SEM(tag + "sa"), SEM(tag + "sb")]
            pe0, dve0 = s_pe.n, s_dve.n

            tl = [(p, qi, t) for p in range(NPAIR) for qi, q in enumerate(qgrp) for t in q]
            npt = [resid_nld(t * 128, min((t + 1) * 128, ln_rows), hp) for (_, _, t) in tl]

            def cum_par(i):
                return 16 * sum(npt[j] for j in range(i % 2, i + 1, 2))

            grps = [(p, qi) for p in range(NPAIR) for qi in range(len(qgrp))]
            gidx = {g: i for i, g in enumerate(grps)}
            tiles_thru = {}
            c = 0
            for g in grps:
                c += len(qgrp[g[1]])
                tiles_thru[g] = c

            def tr_par(gi):
                return 16 * NDT * (gi // 2 + 1)

            @blk.gpsimd
            def _(e):
                kglob = 0
                for p in range(NPAIR):
                    if p >= 1:
                        e.wait_ge(s_pe.h, pe0 + p * ntile)
                    for q2 in range(2):
                        nq = 2 * p + q2
                        for piece in range(4):
                            rows = slice(piece * 1024, (piece + 1) * 1024)
                            wqs[q2].inc(e.dma_start(
                                out=wq[q2][:, piece * 8 * 1024:(piece + 1) * 8 * 1024].rearrange(
                                    "p (a n) -> p a n", a=8),
                                in_=wdeq_s[rows, ts(nq, 1024)].rearrange("(a p) n -> p a n", p=128)), 16)
                    for _i in range(ntile):
                        k = kglob
                        t = tl[k][2]
                        r0, r1 = t * 128, min((t + 1) * 128, ln_rows)
                        e.wait_ge(s_dve.h, dve0 + k + 1)
                        stp[k % 2].inc(e.dma_start(out=dst[r0:r1, p * 2048:(p + 1) * 2048],
                                                   in_=ot[k % 2][0:r1 - r0, :]), 16)
                        kglob += 1

            @blk.scalar
            def _(e):
                for gi, g in enumerate(grps):
                    if gi >= 2:
                        e.wait_ge(s_pe.h, pe0 + tiles_thru[grps[gi - 2]])
                    q = qgrp[g[1]]
                    r0 = q[0] * 128
                    rq = ((min((q[-1] + 1) * 128, ln_rows) - r0 + 15) // 16) * 16
                    for d in range(NDT):
                        trq[gi % 2].inc(e.dma_start_transpose(
                            lnq[gi % 2][:, d * 272:d * 272 + rq], ln_src[r0:r0 + rq, ts(d)]), 16)

            @blk.sync
            def _(e):
                for k, (p, qi, t) in enumerate(tl):
                    if k >= 2:
                        e.wait_ge(s_dve.h, dve0 + k - 1)
                    resid_load2(e, ldx[k % 2], xres[k % 2], t * 128,
                                min((t + 1) * 128, ln_rows), slice(p * 2048, (p + 1) * 2048),
                                resid_pre, resid_main)

            @blk.tensor
            def _(e):
                for k, (p, qi, t) in enumerate(tl):
                    gi = gidx[(p, qi)]
                    e.wait_ge(trq[gi % 2].h, tr_par(gi))
                    e.wait_ge(wqs[0].h, 16 * 4 * (p + 1))
                    e.wait_ge(wqs[1].h, 16 * 4 * (p + 1))
                    if k >= 2:
                        e.wait_ge(s_dve.h, dve0 + k - 1)
                    r0 = t * 128
                    m = min(128, ln_rows - r0)
                    toff = r0 - qgrp[qi][0] * 128
                    for d in range(NDT):
                        for q2 in range(2):
                            for c2 in range(2):
                                _mi = e.matmul(pss[k % 2][0:m, q2 * 1024 + c2 * 512:q2 * 1024 + (c2 + 1) * 512],
                                               lnq[gi % 2][:, d * 272 + toff:d * 272 + toff + m],
                                               wq[q2][:, d * 1024 + c2 * 512:d * 1024 + (c2 + 1) * 512],
                                               start=(d == 0), stop=(d == NDT - 1))
                    s_pe.inc(_mi, 1)

            @blk.vector
            def _(e):
                for k, (p, qi, t) in enumerate(tl):
                    m = min(128, ln_rows - t * 128)
                    e.wait_ge(s_pe.h, pe0 + k + 1)
                    e.wait_ge(ldx[k % 2].h, cum_par(k))
                    if k >= 2:
                        e.wait_ge(stp[k % 2].h, 16 * (k // 2))
                    e.scalar_tensor_tensor(ot[k % 2][0:m, :], pss[k % 2][0:m, :], 0.5,
                                           xres[k % 2][0:m, :], ALU.mult, ALU.add)
                    e.drain()
                    s_dve.inc(e.nop(), 1)

            @blk.gpsimd
            def _(e):
                e.wait_ge(stp[0].h, stp[0].n)
                e.wait_ge(stp[1].h, stp[1].n)
                s_gp.inc(e.nop(), 1)


    emit_ln("l1", x_pre, x_main, TP, ln1_s)
    phase_barrier()
    emit_ffn("f1", ln1_s, TP, x_pre, x_main, x1_s)
    phase_barrier()
    emit_ln("l2", x1_s[0:3, :], x1_s[3:TP, :], TP, ln2_s)
    phase_barrier()

    W_in_p = W_in.rearrange("(blk p eo) n -> eo p blk n", blk=16, p=128, eo=2)
    jorder = [130] + list(range(64)) + list(range(64, 130))
    conv_chunks = [(o, min(512, T_MAIN - o)) for o in range(0, T_MAIN, 512)]
    NCV = len(conv_chunks)

    def pe_before_main(k):
        return 5 * 0 + NCHK * k + NCV * min(max(k - 66, 0), 65) + (NCV if k > 130 else 0)

    def pe_before_conv(b):
        kb = 66 + b if b <= 64 else 130
        return NCHK * (kb + 1) + NCV * b

    def act_before_main(k):
        return NCHK * k + NCV * min(max(k - 66, 0), 65)

    def act_before_conv(b):
        kb = 66 + b if b <= 64 else 130
        return NCHK * (kb + 1) + NCV * b

    mid_es = contextlib.ExitStack()
    Bt_r = mid_es.enter_context(nc.sbuf_tensor("Bt_r", [128, T_MAIN], BF16))
    Ct_r = mid_es.enter_context(nc.sbuf_tensor("Ct_r", [128, T_MAIN], BF16))
    dtraw = mid_es.enter_context(nc.sbuf_tensor("dtraw", [NH, TP], F32))
    with contextlib.ExitStack() as ph:
        psb = lambda n, shp, dt: ph.enter_context(nc.sbuf_tensor(n, shp, dt))
        ln2T = psb("ln2T", [128, NDT * TPAD], BF16)
        wi = [psb(f"wi{i}", [128, NDT * 128], BF16) for i in range(2)]
        stripes = [psb(f"strp{i}", [128, TP], BF16) for i in range(2)]
        zstr = [psb(f"zs{i}", [128, TP], BF16) for i in range(2)]
        xfb = [psb(f"xfb{i}", [128, 512], BF16) for i in range(2)]
        diag4 = [psb(f"dg{i}", [128, KTAPS * 128], BF16) for i in range(2)]
        pMain = ph.enter_context(nc.psum_tensor("pMain", [128, 2048], F32))
        pRag = ph.enter_context(nc.psum_tensor("pRag", [128, 512], F32))
        pCv = [ph.enter_context(nc.psum_tensor(f"pCv{i}", [128, 512], F32)) for i in range(2)]
        wis = [SEM("p2wia"), SEM("p2wib")]
        zst = [SEM("p2za"), SEM("p2zb")]
        xfst = [SEM("p2xa"), SEM("p2xb")]
        ld0, ldg0, tr0, pe0, dve0, act0 = s_ld.n, s_ldg.n, s_tr.n, s_pe.n, s_dve.n, s_act.n

        @blk.scalar
        def _(e):
            for d in range(NDT):
                s_tr.inc(e.dma_start_transpose(ln2T[:, d * TPAD:d * TPAD + TPAD],
                                               ln2_s[:, ts(d)]), 16)
        TR_LN2T = s_tr.n

        @blk.gpsimd
        def _(e):
            for k, j in enumerate(jorder):
                if k >= 2:
                    e.wait_ge(s_pe.h, pe0 + pe_before_main(k - 2) + NCHK)
                jw = 64 if j == 130 else 128
                for half in range(2):
                    a0 = half * 16
                    wis[k % 2].inc(e.dma_start(
                        out=wi[k % 2][:, a0 * 128:(a0 + 16) * 128].rearrange(
                            "p (a n) -> p a n", n=128)[:, :, 0:jw],
                        in_=W_in_p[half, :, :, j * 128:j * 128 + jw]), 16)

        @blk.vector
        def _(e):
            for b in range(NCT):
                if b >= 2:
                    e.wait_ge(s_pe.h, pe0 + pe_before_conv(b - 2) + NCV)
                for tap in range(KTAPS):
                    e.tensor_scalar(diag4[b % 2][:, ts(tap)], ident[:],
                                    cw2[:, b * KTAPS + tap:b * KTAPS + tap + 1], None,
                                    ALU.mult, ALU.bypass)
                e.drain()
                s_dve.inc(e.nop(), 1)

        @blk.tensor
        def _(e):
            def emit_conv(b):
                e.wait_ge(s_act.h, act0 + act_before_conv(b))
                e.wait_ge(s_dve.h, dve0 + b + 1)
                st_buf = stripes[(65 + b) % 2]
                for oi, (o, osz) in enumerate(conv_chunks):
                    gci = b * NCV + oi
                    if gci >= 2:
                        bb, oo = divmod(gci - 2, NCV)
                        e.wait_ge(s_act.h, act0 + act_before_conv(bb) + oo + 1)
                    for tap in range(KTAPS):
                        _mi = e.matmul(pCv[gci % 2][:, 0:osz], diag4[b % 2][:, ts(tap)],
                                       st_buf[:, o + tap:o + tap + osz],
                                       start=(tap == 0), stop=(tap == KTAPS - 1))
                    s_pe.inc(_mi, 1)

            for k, j in enumerate(jorder):
                jw = 64 if j == 130 else 128
                e.wait_ge(s_tr.h, TR_LN2T)
                e.wait_ge(wis[k % 2].h, 32 * (k // 2 + 1))
                for ci, (off, sz) in enumerate(CHUNKS):
                    if k >= 1:
                        e.wait_ge(s_act.h, act0 + act_before_main(k - 1) + ci + 1)
                    tgt = pRag[0:jw, 0:3] if sz == 3 else pMain[0:jw, off:off + sz]
                    for d in range(NDT):
                        _mi = e.matmul(tgt, wi[k % 2][:, d * 128:d * 128 + jw],
                                       ln2T[:, d * TPAD + off:d * TPAD + off + sz],
                                       start=(d == 0), stop=(d == NDT - 1))
                    s_pe.inc(_mi, 1)
                if 66 <= k <= 130:
                    emit_conv(k - 66)
            emit_conv(65)

        def Z_ST(k):
            return 16 * ((k - 1) // 2 + 1)

        def XF_ST(gci):
            return 16 * (gci // 2 + 1)

        @blk.scalar
        def _(e):
            def silu_evacs(b):
                ct = b
                for oi, (o, osz) in enumerate(conv_chunks):
                    gci = b * NCV + oi
                    e.wait_ge(s_pe.h, pe0 + pe_before_conv(b) + oi + 1)
                    if ct == 64:
                        dst_ap = Bt_r[:, o:o + osz]
                    elif ct == 65:
                        dst_ap = Ct_r[:, o:o + osz]
                    else:
                        dst_ap = xfb[gci % 2][:, 0:osz]
                        if gci >= 2:
                            e.wait_ge(xfst[gci % 2].h, XF_ST(gci - 2))
                    e.activation(dst_ap, pCv[gci % 2][:, 0:osz], AF.Silu)
                    e.drain()
                    s_act.inc(e.nop(), 1)

            for k, j in enumerate(jorder):
                jw = 64 if j == 130 else 128
                for ci, (off, sz) in enumerate(CHUNKS):
                    e.wait_ge(s_pe.h, pe0 + pe_before_main(k) + ci + 1)
                    src = pRag[0:jw, 0:3] if sz == 3 else pMain[0:jw, off:off + sz]
                    if j == 130:
                        e.activation(dtraw[:, off:off + sz], src, AF.Copy)
                    elif j < 64:
                        if ci == 0 and k >= 3:
                            e.wait_ge(zst[k % 2].h, Z_ST(k - 2))
                        e.activation(zstr[k % 2][:, off:off + sz], src, AF.Copy)
                    else:
                        if ci == 0 and k >= 67:
                            e.wait_ge(s_pe.h, pe0 + pe_before_conv(j - 64 - 2) + NCV)
                        e.activation(stripes[k % 2][:, off:off + sz], src, AF.Copy)
                    e.drain()
                    s_act.inc(e.nop(), 1)
                if 66 <= k <= 130:
                    silu_evacs(k - 66)
            silu_evacs(65)

        @blk.sync
        def _(e):
            for k, j in enumerate(jorder):
                if j < 64:
                    e.wait_ge(s_act.h, act0 + act_before_main(k) + NCHK)
                    zst[k % 2].inc(e.dma_start(out=z_s[ts(j), :], in_=zstr[k % 2][:, 3:TP]), 16)
            for b in range(64):
                for oi, (o, osz) in enumerate(conv_chunks):
                    gci = b * NCV + oi
                    e.wait_ge(s_act.h, act0 + act_before_conv(b) + oi + 1)
                    xfst[gci % 2].inc(e.dma_start(out=xf_s[ts(b), o:o + osz],
                                                  in_=xfb[gci % 2][:, 0:osz]), 16)
        @blk.sync
        def _(e):
            for p in range(2):
                e.wait_ge(zst[p].h, zst[p].n)
                e.wait_ge(xfst[p].h, xfst[p].n)
            s_ld.inc(e.nop(), 16)
    phase_barrier()


    ssd_es = contextlib.ExitStack()
    ssb = lambda n, shp, dt: ssd_es.enter_context(nc.sbuf_tensor(n, shp, dt))
    seed = ssb("seed", [128, DSSM], BF16)
    ssd2_es = contextlib.ExitStack()
    ssb2 = lambda n, shp, dt: ssd2_es.enter_context(nc.sbuf_tensor(n, shp, dt))
    lamb = ssb2("lamb", [128, NH * NCH], F32)
    wTp = ssb2("wTp", [128, NCH * 64], F32)
    colterm = ssb2("colterm", [128, NCH * 64], F32)
    logdAT = ssb2("logdAT", [128, NCH * 64], BF16)
    hslab = ssb2("hslab", [128, DSSM], F32)
    hbf = ssb2("hbf", [128, DSSM], BF16)
    DIall = ssb2("DIall", [128, NH * 128], BF16)

    with contextlib.ExitStack() as ph:
        psb = lambda n, shp, dt: ph.enter_context(nc.sbuf_tensor(n, shp, dt))
        dtv = psb("dtv", [NH, T_MAIN], F32)
        logdt = psb("logdt", [NH, T_MAIN], F32)
        logdA_b = psb("logdA_b", [NH, T_MAIN], BF16)
        cumhs = psb("cumhs", [NH, T_MAIN], F32)
        cumabs = psb("cumabs", [NH, T_MAIN], F32)
        w_hs = psb("w_hs", [NH, T_MAIN], F32)
        ta = psb("ta", [NH, T_MAIN], F32)
        tb = psb("tb", [NH, T_MAIN], F32)
        tc = psb("tcx", [NH, T_MAIN], F32)
        td = psb("td", [NH, T_MAIN], F32)
        tbf = psb("tbf", [NH, T_MAIN], BF16)
        ones64 = psb("ones64", [NH, L], F32)
        negA = psb("negA", [NH, 1], F32)
        lam64 = psb("lam64", [NH, NCH], F32)
        lamflat = psb("lamflat", [1, NH * NCH], F32)
        ldtT = psb("ldtT", [128, 64], F32)
        Drow = psb("Drow", [1, 64], F32)
        Dbc = psb("Dbc", [128, 64], F32)
        pTb = ph.enter_context(nc.psum_tensor("pTb", [128, 64], BF16))
        pTf = ph.enter_context(nc.psum_tensor("pTf", [128, 64], F32))
        pTf2 = ph.enter_context(nc.psum_tensor("pTf2", [128, 64], F32))
        pTlam = ph.enter_context(nc.psum_tensor("pTlam", [128, 512], F32))
        dve0, act0, gp0, pe0, ldg0 = s_dve.n, s_act.n, s_gp.n, s_pe.n, s_ldg.n

        @blk.vector
        def _(e):
            e.memset(ones64[:], 1.0)
            e.memset(hslab[:], 0.0)
            e.tensor_scalar(ta[:], dtraw[:, 3:TP], hv[:, 0:1], None, ALU.add, ALU.bypass)
            e.drain()
            e.tensor_scalar(tb[:], ta[:], 0.0, None, ALU.max, ALU.bypass)
            e.tensor_scalar(tc[:], ta[:], -1.0, None, ALU.mult, ALU.bypass)
            e.drain()
            e.tensor_tensor(tc[:], ta[:], tc[:], ALU.max)
            e.drain()
            s_dve.inc(e.nop(), 1)

        @blk.scalar
        def _(e):
            e.wait_ge(s_dve.h, dve0 + 1)
            e.activation(td[:], tc[:], AF.Exp, scale=-1.0)
            e.drain()
            e.activation(tc[:], td[:], AF.Ln, bias=onecolf2[0:NH, 0:1])
            e.drain()
            e.activation(negA[:], hv[:, 1:2], AF.Exp)
            e.drain()
            s_act.inc(e.nop(), 1)

        @blk.vector
        def _(e):
            e.wait_ge(s_act.h, act0 + 1)
            e.tensor_tensor(dtv[:], tb[:], tc[:], ALU.add)
            e.drain()
            e.tensor_scalar(ta[:], dtv[:], 1e-19, None, ALU.max, ALU.bypass)
            e.tensor_scalar(negA[:], negA[:], -1.0, None, ALU.mult, ALU.bypass)
            e.drain()
            e.tensor_scalar(tb[:], dtv[:], negA[:, 0:1], None, ALU.mult, ALU.bypass)
            e.drain()
            e.tensor_copy(logdA_b[:], tb[:])
            for k in range(NCH):
                e.tensor_tensor_scan(cumhs[:, ts(k, L)], ones64[:], tb[:, ts(k, L)],
                                     0.0, ALU.mult, ALU.add)
            e.drain()
            for k in range(NCH):
                init = 0.0 if k == 0 else cumabs[:, k * L - 1:k * L]
                e.tensor_tensor_scan(cumabs[:, ts(k, L)], ones64[:], tb[:, ts(k, L)],
                                     init, ALU.mult, ALU.add)
                e.drain()
            s_dve.inc(e.nop(), 1)

        @blk.scalar
        def _(e):
            e.wait_ge(s_dve.h, dve0 + 2)
            e.activation(logdt[:], ta[:], AF.Ln)
            e.drain()
            e.activation(tbf[:], cumhs[:], AF.Exp)
            e.drain()
            s_act.inc(e.nop(), 1)
            for k in range(NCH):
                e.activation(ta[:, ts(k, L)], cumhs[:, ts(k, L)], AF.Exp, scale=-1.0,
                             bias=cumhs[:, k * L + L - 1:k * L + L])
            e.drain()
            e.activation(lam64[:], cumhs[:, L - 1::L], AF.Exp)
            e.drain()
            s_act.inc(e.nop(), 1)

        @blk.gpsimd
        def _(e):
            e.wait_ge(s_act.h, act0 + 2)
            s_ldg.inc(e.dma_start(out=bncA, in_=tbf[:]), 16)
            e.wait_ge(s_ldg.h, ldg0 + 16)

        @blk.vector
        def _(e):
            e.wait_ge(s_act.h, act0 + 3)
            e.wait_ge(s_ldg.h, ldg0 + 16)
            e.tensor_tensor(w_hs[:], ta[:], dtv[:], ALU.mult)
            e.drain()
            s_dve.inc(e.nop(), 1)

        @blk.scalar
        def _(e):
            e.wait_ge(s_dve.h, dve0 + 3)
            e.activation(tbf[:], cumabs[:], AF.Exp)
            e.drain()
            s_act.inc(e.nop(), 1)

        @blk.gpsimd
        def _(e):
            e.wait_ge(s_act.h, act0 + 4)
            s_ldg.inc(e.dma_start(out=bncB, in_=tbf[:]), 16)
            s_ldg.inc(e.dma_start(out=bncC, in_=lam64[:]), 16)
            e.wait_ge(s_ldg.h, ldg0 + 48)
            s_ldg.inc(e.dma_start(out=lamflat[:],
                                  in_=bncC.rearrange("a b -> (a b)").unsqueeze(0)), 16)
            e.wait_ge(s_ldg.h, ldg0 + 64)
            s_gp.inc(e.nop(), 1)

        NLAMMM = (NH * NCH + 511) // 512
        @blk.tensor
        def _(e):
            e.wait_ge(s_gp.h, gp0 + 1)
            for i in range(NLAMMM):
                if i >= 1:
                    e.wait_ge(s_act.h, act0 + 4 + i)
                w = min(512, NH * NCH - i * 512)
                _mi = e.matmul(pTlam[:, 0:w], onescolf[:], lamflat[0:1, i * 512:i * 512 + w],
                               start=True, stop=True)
                s_pe.inc(_mi, 1)

        @blk.scalar
        def _(e):
            for i in range(NLAMMM):
                e.wait_ge(s_pe.h, pe0 + i + 1)
                w = min(512, NH * NCH - i * 512)
                e.activation(lamb[:, i * 512:i * 512 + w], pTlam[:, 0:w], AF.Copy)
                e.drain()
                s_act.inc(e.nop(), 1)

        pe1, dve1, act1 = pe0 + NLAMMM, dve0 + 3, act0 + 4 + NLAMMM
        @blk.tensor
        def _(e):
            e.wait_ge(s_dve.h, dve1)
            e.wait_ge(s_act.h, act1)
            for k in range(NCH):
                if k > 0:
                    e.wait_ge(s_dve.h, dve1 + 2 * k)
                    e.wait_ge(s_act.h, act1 + 2 * k)
                e.transpose(pTb[:, :], logdA_b[:, ts(k, L)], ident[0:64, 0:64])
                _mi = e.transpose(pTf[:, :], logdt[:, ts(k, L)], identf[0:64, 0:64])
                s_pe.inc(_mi, 1)
                e.wait_ge(s_dve.h, dve1 + 2 * k + 1)
                e.wait_ge(s_act.h, act1 + 2 * k + 1)
                e.transpose(pTf2[:, :], w_hs[:, ts(k, L)], identf[0:64, 0:64])
                _mi = e.matmul(pTf[:, :], triu[:], logdAT[:, k * 64:(k + 1) * 64],
                               start=True, stop=True)
                s_pe.inc(_mi, 1)

        @blk.vector
        def _(e):
            for k in range(NCH):
                e.wait_ge(s_pe.h, pe1 + 2 * k + 1)
                e.tensor_copy(logdAT[:, k * 64:(k + 1) * 64], pTb[:, :])
                e.drain()
                s_dve.inc(e.nop(), 1)
                e.wait_ge(s_pe.h, pe1 + 2 * k + 2)
                e.wait_ge(s_act.h, act1 + 2 * k + 1)
                e.tensor_tensor(colterm[:, k * 64:(k + 1) * 64], ldtT[:],
                                pTf[:, :], ALU.subtract)
                e.drain()
                s_dve.inc(e.nop(), 1)

        @blk.scalar
        def _(e):
            for k in range(NCH):
                e.wait_ge(s_pe.h, pe1 + 2 * k + 1)
                e.activation(ldtT[:], pTf[:, :], AF.Copy)
                e.drain()
                s_act.inc(e.nop(), 1)
                e.wait_ge(s_pe.h, pe1 + 2 * k + 2)
                e.wait_ge(s_dve.h, dve1 + 2 * k + 2)
                e.activation(wTp[:, k * 64:(k + 1) * 64], pTf2[:, :], AF.Copy)
                e.drain()
                s_act.inc(e.nop(), 1)

        pe2, act2, dve2 = pe1 + 2 * NCH, act1 + 2 * NCH, dve1 + 2 * NCH
        @blk.tensor
        def _(e):
            e.wait_ge(s_act.h, act2)
            _mi = e.transpose(pTf[0:1, 0:64], hv[:, 2:3], identf[0:64, 0:64])
            s_pe.inc(_mi, 1)
            e.wait_ge(s_act.h, act2 + 1)
            _mi = e.matmul(pTf2[:, 0:64], onescolf[:], Drow[:], start=True, stop=True)
            s_pe.inc(_mi, 1)

        @blk.scalar
        def _(e):
            e.wait_ge(s_pe.h, pe2 + 1)
            e.activation(Drow[:], pTf[0:1, 0:64], AF.Copy)
            e.drain()
            s_act.inc(e.nop(), 1)
            e.wait_ge(s_pe.h, pe2 + 2)
            e.activation(Dbc[:], pTf2[:, 0:64], AF.Copy)
            e.drain()
            s_act.inc(e.nop(), 1)

        @blk.vector
        def _(e):
            e.wait_ge(s_act.h, act2 + 2)
            for h in range(NH):
                e.tensor_scalar(DIall[:, ts(h)], ident[:], Dbc[:, h:h + 1], None,
                                ALU.mult, ALU.bypass)
            e.drain()
            s_dve.inc(e.nop(), 1)
    phase_barrier()

    with contextlib.ExitStack() as ph:
        psb = lambda n, shp, dt: ph.enter_context(nc.sbuf_tensor(n, shp, dt))
        xq = [psb(f"xq{i}", [128, DSSM], BF16) for i in range(2)]
        smb = psb("smb", [128, L], BF16)
        btT = psb("btT", [128, 128], BF16)
        striB = [psb(f"stri{i}", [128, 1024], BF16) for i in range(2)]
        xwB = [psb(f"xw{i}", [128, 1024], BF16) for i in range(2)]
        EclB = [psb(f"ecl{i}", [128, 1024], BF16) for i in range(2)]
        EexpB = [psb(f"eexp{i}", [128, 1024], BF16) for i in range(2)]
        mppB = [psb(f"mpp{i}", [128, 1024], BF16) for i in range(2)]
        cdecB = [psb(f"cdec{i}", [128, 1024], BF16) for i in range(2)]
        ystB = [psb(f"yst{i}", [128, 1024], BF16) for i in range(2)]
        expfk = psb("expfk", [1, NH * L], BF16)
        pE = ph.enter_context(nc.psum_tensor("pE", [128, 1024], F32))
        pY = ph.enter_context(nc.psum_tensor("pY", [128, 1024], F32))
        pBS = ph.enter_context(nc.psum_tensor("pBS", [128, 1024], F32))
        pSC = ph.enter_context(nc.psum_tensor("pSC", [128, L], F32))
        pBT = ph.enter_context(nc.psum_tensor("pBT", [128, 128], BF16))
        xqs = [SEM("p3xa"), SEM("p3xb")]
        yst = [SEM("p3ya"), SEM("p3yb")]
        efs = SEM("p3ef")
        pe0, dve0, act0, gp0, tr0, ld0 = s_pe.n, s_dve.n, s_act.n, s_gp.n, s_tr.n, s_ld.n

        def pe_at(k, idx):
            return pe0 + 42 * k + idx

        def dve_at(k, idx):
            return dve0 + 42 * k + idx

        def act_at(k, idx):
            return act0 + 17 * k + idx

        def gp_at(k, idx):
            return gp0 + 8 * k + idx

        def prev2(k, g):
            gg = g - 2
            if gg < 0:
                return k - 1, gg + 8
            return k, gg

        @blk.tensor
        def _(e):
            for k in range(NCH):
                if k >= 1:
                    e.wait_ge(s_dve.h, dve_at(k - 1, 1))
                _mi = e.matmul(pSC[:, :], Bt_r[:, ts(k, L)], Ct_r[:, ts(k, L)], start=True, stop=True)
                s_pe.inc(_mi, 1)
                if k >= 1:
                    e.wait_ge(s_dve.h, dve_at(k - 1, 2))
                _mi = e.transpose(pBT[:, :], Bt_r[:, ts(k, L)], ident[:])
                s_pe.inc(_mi, 1)
                for g in range(8):
                    e.wait_ge(s_dve.h, dve_at(k, 3 + 5 * g))
                    kk, gg = (k, g - 1) if g >= 1 else (k - 1, 7)
                    if kk >= 0:
                        e.wait_ge(s_dve.h, dve_at(kk, 5 + 5 * gg))
                    for c2 in range(2):
                        _mi = e.matmul(pE[:, ts(c2, 512)], onesfull[:],
                                       striB[g % 2][:, ts(c2, 512)], start=True, stop=True)
                    s_pe.inc(_mi, 1)
                    kk, gg = (k, g - 1) if g >= 1 else (k - 1, 7)
                    if kk >= 0:
                        e.wait_ge(s_dve.h, dve_at(kk, 7 + 5 * gg))
                    e.wait_ge(efs.h, 16 * (k + 1))
                    for hh in range(8):
                        h = g * 8 + hh
                        _mi = e.matmul(pBS[:, ts(hh)], onescol[:],
                                       expfk[0:1, ts(h)],
                                       start=(hh % 4 == 0), stop=(hh % 4 == 3),
                                       skip_group_check=True)
                    s_pe.inc(_mi, 1)
                    e.wait_ge(s_gp.h, gp_at(k, 1 + g))
                    kk, gg = (k, g - 1) if g >= 1 else (k - 1, 7)
                    if kk >= 0:
                        e.wait_ge(s_act.h, act_at(kk, 3 + 2 * gg))
                    for hh in range(8):
                        h = g * 8 + hh
                        _mi = e.matmul(pY[:, ts(hh)], xq[k % 2][:, ts(h)], mppB[g % 2][:, ts(hh)],
                                       start=(hh % 4 == 0), stop=False, skip_group_check=True)
                    s_pe.inc(_mi, 1)
                    e.wait_ge(s_dve.h, dve_at(k, 6 + 5 * g))
                    e.wait_ge(s_act.h, act_at(k, 1))
                    for hh in range(8):
                        h = g * 8 + hh
                        _mi = e.matmul(pY[:, ts(hh)], hbf[:, ts(h)], cdecB[g % 2][:, ts(hh)],
                                       start=False, stop=(hh % 4 == 3), skip_group_check=True)
                    s_pe.inc(_mi, 1)
                    e.wait_ge(s_dve.h, dve_at(k, 4 + 5 * g))
                    for c2 in range(2):
                        _mi = e.matmul(pBS[:, ts(c2, 512)], btT[:], xwB[g % 2][:, ts(c2, 512)],
                                       start=True, stop=True, skip_group_check=True)
                    s_pe.inc(_mi, 1)

        @blk.vector
        def _(e):
            for k in range(NCH):
                e.wait_ge(s_pe.h, pe_at(k, 1))
                if k >= 1:
                    e.wait_ge(s_gp.h, gp_at(k - 1, 8))
                e.tensor_tensor(smb[:], pSC[:, :], triu[:], ALU.mult)
                e.drain()
                s_dve.inc(e.nop(), 1)
                e.wait_ge(s_pe.h, pe_at(k, 2))
                if k >= 1:
                    e.wait_ge(s_pe.h, pe_at(k - 1, 42))
                e.tensor_copy(btT[:], pBT[:, :])
                e.drain()
                s_dve.inc(e.nop(), 1)
                for g in range(8):
                    kk, gg = prev2(k, g)
                    if kk >= 0:
                        e.wait_ge(s_pe.h, pe_at(kk, 3 + 5 * gg))
                    tri_rep = triu[:, :].unsqueeze(1).broadcast_to([128, 8, 128])
                    ldA_rep = logdAT[:, k * 64 + g * 8:k * 64 + (g + 1) * 8].unsqueeze(2).broadcast_to([128, 8, 128])
                    e.tensor_tensor(striB[g % 2][:, :].rearrange("p (a b) -> p a b", a=8),
                                    tri_rep, ldA_rep, ALU.mult)
                    e.drain()
                    s_dve.inc(e.nop(), 1)
                    if g == 0:
                        e.wait_ge(xqs[k % 2].h, 16 * (k // 2 + 1))
                    kk, gg = prev2(k, g)
                    if kk >= 0:
                        e.wait_ge(s_pe.h, pe_at(kk, 7 + 5 * gg))
                    w_rep = wTp[:, k * 64 + g * 8:k * 64 + (g + 1) * 8].unsqueeze(2).broadcast_to([128, 8, 128])
                    e.tensor_tensor(xwB[g % 2][:, :].rearrange("p (a b) -> p a b", a=8),
                                    xq[k % 2][:, ts(g, 1024)].rearrange("p (a b) -> p a b", a=8),
                                    w_rep, ALU.mult)
                    e.drain()
                    s_dve.inc(e.nop(), 1)
                    e.wait_ge(s_pe.h, pe_at(k, 3 + 5 * g))
                    kk, gg = prev2(k, g)
                    if kk >= 0:
                        e.wait_ge(s_act.h, act_at(kk, 2 + 2 * gg))
                    ct_rep2 = colterm[:, k * 64 + g * 8:k * 64 + (g + 1) * 8].unsqueeze(2).broadcast_to([128, 8, 128])
                    e.tensor_tensor(EclB[g % 2][:, :].rearrange("p (a b) -> p a b", a=8),
                                    pE[:, :].rearrange("p (a b) -> p a b", a=8),
                                    ct_rep2, ALU.add)
                    e.drain()
                    e.tensor_scalar(EclB[g % 2][:, :], EclB[g % 2][:, :], 60.0, None,
                                    ALU.min, ALU.bypass)
                    e.drain()
                    s_dve.inc(e.nop(), 1)
                    e.wait_ge(s_pe.h, pe_at(k, 4 + 5 * g))
                    kk, gg = prev2(k, g)
                    if kk >= 0:
                        e.wait_ge(s_pe.h, pe_at(kk, 6 + 5 * gg))
                    ct_rep = Ct_r[:, ts(k, L)].unsqueeze(1).broadcast_to([128, 8, 128])
                    e.tensor_tensor(cdecB[g % 2][:, :].rearrange("p (a b) -> p a b", a=8),
                                    pBS[:, :].rearrange("p (a b) -> p a b", a=8),
                                    ct_rep, ALU.mult)
                    e.drain()
                    s_dve.inc(e.nop(), 1)
                    e.wait_ge(s_pe.h, pe_at(k, 7 + 5 * g))
                    e.wait_ge(s_act.h, act_at(k, 1))
                    lam_rep = lamb[:, k::NCH][:, g * 8:(g + 1) * 8].unsqueeze(2).broadcast_to([128, 8, 128])
                    e.tensor_tensor(hslab[:, ts(g, 1024)].rearrange("p (a b) -> p a b", a=8),
                                    hslab[:, ts(g, 1024)].rearrange("p (a b) -> p a b", a=8),
                                    lam_rep, ALU.mult)
                    e.drain()
                    e.tensor_tensor(hslab[:, ts(g, 1024)], hslab[:, ts(g, 1024)],
                                    pBS[:, :], ALU.add)
                    e.drain()
                    s_dve.inc(e.nop(), 1)

        @blk.scalar
        def _(e):
            for k in range(2):
                if k < NCH:
                    xqs[k % 2].inc(e.dma_start_transpose(xq[k % 2][:, :], xf_s[:, ts(k, L)]), 16)
            for k in range(NCH):
                if k >= 1:
                    e.wait_ge(s_dve.h, dve_at(k - 1, 42))
                e.activation(hbf[:], hslab[:], AF.Copy)
                e.drain()
                s_act.inc(e.nop(), 1)
                for g in range(8):
                    e.wait_ge(s_dve.h, dve_at(k, 5 + 5 * g))
                    kk, gg = prev2(k, g)
                    if kk >= 0:
                        e.wait_ge(s_gp.h, gp_at(kk, 1 + gg))
                    e.activation(EexpB[g % 2][:, :], EclB[g % 2][:, :], AF.Exp)
                    e.drain()
                    s_act.inc(e.nop(), 1)
                    e.wait_ge(s_pe.h, pe_at(k, 6 + 5 * g))
                    kk, gg = prev2(k, g)
                    if kk >= 0:
                        e.wait_ge(yst[gg % 2].h, 16 * ((8 * kk + gg) // 2 + 1))
                    e.activation(ystB[g % 2][:, :], pY[:, :], AF.Copy)
                    e.drain()
                    s_act.inc(e.nop(), 1)
                if k + 2 < NCH:
                    e.wait_ge(s_dve.h, dve_at(k, 39))
                    xqs[k % 2].inc(e.dma_start_transpose(xq[k % 2][:, :], xf_s[:, ts(k + 2, L)]), 16)

        @blk.gpsimd
        def _(e):
            for k in range(NCH):
                for g in range(8):
                    e.wait_ge(s_act.h, act_at(k, 2 + 2 * g))
                    e.wait_ge(s_dve.h, dve_at(k, 1))
                    kk, gg = prev2(k, g)
                    if kk >= 0:
                        e.wait_ge(s_pe.h, pe_at(kk, 5 + 5 * gg))
                    sm_rep = smb[:, :].unsqueeze(1).broadcast_to([128, 8, 128])
                    e.tensor_tensor(mppB[g % 2][:, :].rearrange("p (a b) -> p a b", a=8),
                                    EexpB[g % 2][:, :].rearrange("p (a b) -> p a b", a=8),
                                    sm_rep, ALU.mult)
                    e.drain()
                    e.tensor_tensor(mppB[g % 2][:, :], mppB[g % 2][:, :],
                                    DIall[:, ts(g, 1024)], ALU.add)
                    e.drain()
                    s_gp.inc(e.nop(), 1)

        @blk.sync
        def _(e):
            for k in range(NCH):
                if k >= 1:
                    e.wait_ge(s_pe.h, pe_at(k - 1, 39))
                efs.inc(e.dma_start(
                    out=expfk[0:1, :].rearrange("p (h t) -> p h t", h=NH),
                    in_=bncA[:, ts(k, L)].unsqueeze(0)), 16)
                for g in range(8):
                    e.wait_ge(s_act.h, act_at(k, 3 + 2 * g))
                    yst[g % 2].inc(e.dma_start(
                        out=yraw_s[g * 1024:(g + 1) * 1024, ts(k, L)].rearrange(
                            "(h p) t -> p h t", p=128),
                        in_=ystB[g % 2][:, :].rearrange("p (h t) -> p h t", h=8)), 16)

        @blk.sync
        def _(e):
            e.wait_ge(yst[0].h, yst[0].n)
            e.wait_ge(yst[1].h, yst[1].n)
            e.wait_ge(efs.h, efs.n)
            s_ld.inc(e.nop(), 16)
    phase_barrier()

    with contextlib.ExitStack() as ph:
        htmp = ph.enter_context(nc.sbuf_tensor("htmp", [128, DSSM], F32))
        dve0, gp0, cc0, ldg0 = s_dve.n, s_gp.n, s_cc.n, s_ldg.n

        @blk.vector
        def _(e):
            e.tensor_scalar(htmp[:], hslab[:], fl[:, 0:1], None, ALU.mult, ALU.bypass)
            e.drain()
            s_dve.inc(e.nop(), 1)

        @blk.gpsimd
        def _(e):
            e.wait_ge(s_dve.h, dve0 + 1)
            s_ldg.inc(e.dma_start(out=cc_in, in_=htmp[:]), 16)
            e.wait_ge(s_ldg.h, ldg0 + 16)
            cci = e.collective_compute(
                "AllReduce", ALU.add,
                replica_groups=[[2 * i, 2 * i + 1] for i in range(n_cc // 2)],
                ins=[cc_in.opt()], outs=[cc_out.opt()],
            )
            cci.then_inc(s_cc.h, 1)
            s_cc.n += 1
            e.wait_ge(s_cc.h, cc0 + 1)
            s_ldg.inc(e.dma_start(out=htmp[:], in_=cc_out), 16)
            e.wait_ge(s_ldg.h, ldg0 + 32)
            s_gp.inc(e.nop(), 1)

        @blk.vector
        def _(e):
            e.wait_ge(s_gp.h, gp0 + 1)
            e.tensor_scalar(seed[:], htmp[:], fl[:, 1:2], None, ALU.mult, ALU.bypass)
            e.drain()
            s_dve.inc(e.nop(), 1)
    phase_barrier()
    ssd2_es.close()

    assert T_MAIN % 512 == 0
    NTQ = NCH // 4
    host = {k: [] for k in range(NCH)}
    tail34 = []
    for tq in range(NTQ):
        items = [("g", tq, 0)] + [("u", tq, o8) for o8 in range(8)]
        if 4 * tq + 7 < NCH:
            host[4 * tq + 4] += items[0:3]
            host[4 * tq + 5] += items[3:5]
            host[4 * tq + 6] += items[5:7]
            host[4 * tq + 7] += items[7:9]
        else:
            tail34 += items
    ordered34 = [it for k in range(NCH) for it in host[k]] + tail34

    pe4_g, pe4_u, dve4_g, dve4_e, act4_g = {}, {}, {}, {}, {}
    _pe4 = _dve4 = _act4 = 0
    for it in ordered34:
        if it[0] == "g":
            _pe4 += 1; pe4_g[it[1]] = _pe4
            _dve4 += 1; dve4_g[it[1]] = _dve4
            _act4 += 1; act4_g[it[1]] = _act4
        else:
            ui = 8 * it[1] + it[2]
            for p in range(8):
                _pe4 += 1; pe4_u[(ui, p)] = _pe4
            for tt in range(4):
                _dve4 += 1; dve4_e[(ui, tt)] = _dve4

    with contextlib.ExitStack() as ph:
        psb = lambda n, shp, dt: ph.enter_context(nc.sbuf_tensor(n, shp, dt))
        yslab = psb("yslab", [128, DSSM], BF16)
        zslab = psb("zslab", [128, DSSM], BF16)
        silz = psb("silz", [128, DSSM], BF16)
        yg2 = [psb(f"yg2{i}", [128, 1024], BF16) for i in range(2)]
        ycB = [psb(f"yc{i}", [128, 1024], F32) for i in range(2)]
        cdaB = [psb(f"cda{i}", [128, 1024], BF16) for i in range(2)]
        expabsk = psb("expabsk", [33, 4096], BF16)
        ygr = psb("ygr", [128, NZT * 512], BF16)
        wosB = [psb(f"wos{i}", [128, 8 * 512], BF16) for i in range(2)]
        x1b = [psb(f"x1b{i}", [128, 512], F32) for i in range(4)]
        x2b = [psb(f"x2b{i}", [128, 512], F32) for i in range(4)]
        rsq = psb("rsq", [128, 4], F32)
        rln = psb("rln", [128, 4], F32)
        pB2 = ph.enter_context(nc.psum_tensor("pB2", [128, 1024], F32))
        pD = ph.enter_context(nc.psum_tensor("pD", [128, 1024], F32))
        pQ = [ph.enter_context(nc.psum_tensor(f"pQ{i}", [128, 512], F32)) for i in range(4)]
        eas = SEM("pbea")
        yls = SEM("pbyl")
        zls = SEM("pbzl")
        ygst = [SEM("pbga"), SEM("pbgb")]
        s_pe4 = SEM("spe4")
        s_dve4 = SEM("sdv4")
        s_act4 = SEM("sac4")
        p4yg = SEM("p4yg")
        p4w = [SEM("p4wa"), SEM("p4wb")]
        p4x1 = [SEM(f"p4x{i}") for i in range(4)]
        p4st = [SEM(f"p4s{i}") for i in range(4)]
        pe0, dve0, act0 = s_pe.n, s_dve.n, s_act.n

        def pe3(k, i):
            return pe0 + 16 * k + i

        def dve3(k, i):
            return dve0 + 24 * k + i

        def act3(k):
            return act0 + k + 1

        def prev2b(k, g):
            gg = g - 2
            if gg < 0:
                return k - 1, gg + 8
            return k, gg

        def ygr_t(c, tt):
            return ygr[:, (c * 4 + tt) * 128:(c * 4 + tt) * 128 + 128]

        def x2_stores(e, it):
            tq, o8 = it[1], it[2]
            ui = 8 * tq + o8
            for tt in range(4):
                m = 4 * ui + tt
                tg = tq * 4 + tt
                e.wait_ge(s_dve4.h, dve4_e[(ui, tt)])
                p4st[m % 4].inc(e.dma_start(
                    out=x2_s[tg * 128:tg * 128 + 128, ts(o8, 512)],
                    in_=x2b[m % 4][:, :]), 16)

        @blk.sync
        def _(e):
            for k in range(NCH):
                if k >= 1:
                    e.wait_ge(s_pe.h, pe3(k - 1, 15))
                eas.inc(e.dma_start(
                    out=expabsk[0:1, :].rearrange("p (h t) -> p h t", h=32),
                    in_=bncB[0:32, ts(k, L)].unsqueeze(0)), 16)
                eas.inc(e.dma_start(
                    out=expabsk[32:33, :].rearrange("p (h t) -> p h t", h=32),
                    in_=bncB[32:64, ts(k, L)].unsqueeze(0)), 16)
                if k >= 1:
                    e.wait_ge(s_dve.h, dve3(k - 1, 24))
                for g in range(8):
                    yls.inc(e.dma_start(
                        out=yslab[:, ts(g, 1024)].rearrange("p (h t) -> p h t", h=8),
                        in_=yraw_s[g * 1024:(g + 1) * 1024, ts(k, L)].rearrange(
                            "(h p) t -> p h t", p=128)), 16)
                if k >= 1:
                    e.wait_ge(s_act.h, act3(k - 1))
                for g in range(8):
                    zls.inc(e.dma_start(
                        out=zslab[:, ts(g, 1024)].rearrange("p (h t) -> p h t", h=8),
                        in_=z_s[g * 1024:(g + 1) * 1024, ts(k, L)].rearrange(
                            "(h p) t -> p h t", p=128)), 16)
                for g in range(8):
                    e.wait_ge(s_dve.h, dve3(k, 3 + 3 * g))
                    ygst[g % 2].inc(e.dma_start(
                        out=yg_s[g * 1024:(g + 1) * 1024, ts(k, L)].rearrange(
                            "(h p) t -> p h t", p=128),
                        in_=yg2[g % 2][:, :].rearrange("p (h t) -> p h t", h=8)), 16)
                if k % 4 == 3 and (k // 4) < NTQ:
                    tq = k // 4
                    e.wait_ge(ygst[0].h, 16 * 4 * (k + 1))
                    e.wait_ge(ygst[1].h, 16 * 4 * (k + 1))
                    if tq >= 1:
                        e.wait_ge(s_pe4.h, pe4_u[(8 * tq - 1, 7)])
                    for g in range(8):
                        p4yg.inc(e.dma_start(
                            out=ygr[:, g * 8 * 4 * 128:(g + 1) * 8 * 4 * 128].rearrange(
                                "p (a t) -> p a t", a=8),
                            in_=yg_s[g * 1024:(g + 1) * 1024,
                                     tq * 512:(tq + 1) * 512].rearrange(
                                "(a p) t -> p a t", p=128)), 16)
                for it in host[k]:
                    if it[0] == "u":
                        x2_stores(e, it)
            for it in tail34:
                if it[0] == "u":
                    x2_stores(e, it)

        @blk.sync
        def _(e):
            e.wait_ge(ygst[0].h, ygst[0].n)
            e.wait_ge(ygst[1].h, ygst[1].n)
            e.wait_ge(eas.h, eas.n)
            for i in range(4):
                e.wait_ge(p4st[i].h, p4st[i].n)
            s_ld.inc(e.nop(), 16)

        @blk.gpsimd
        def _(e):
            for it in ordered34:
                if it[0] != "u":
                    continue
                tq, o8 = it[1], it[2]
                ui = 8 * tq + o8
                for p in range(8):
                    q = 8 * ui + p
                    if q >= 2:
                        qq = q - 2
                        e.wait_ge(s_pe4.h, pe4_u[(qq // 8, qq % 8)])
                    p4w[q % 2].inc(e.dma_start(
                        out=wosB[q % 2][:, :].rearrange("p (a n) -> p a n", a=8),
                        in_=wout_s[p * 1024:(p + 1) * 1024, ts(o8, 512)].rearrange(
                            "(a p) n -> p a n", p=128)), 16)

        def p4_pe_pieces(it):
            if it[0] == "g":
                tq = it[1]

                def mkg(tt):
                    def f(e):
                        if tt == 0:
                            e.wait_ge(p4yg.h, 16 * 8 * (tq + 1))
                            if tq >= 1:
                                e.wait_ge(s_dve4.h, dve4_e[(8 * tq - 1, 3)])
                        for c in range(64):
                            _mi = e.matmul(pQ[tt][:, 0:128], ygr_t(c, tt), ygr_t(c, tt),
                                           start=(c == 0), stop=(c == 63),
                                           skip_group_check=True)
                        if tt == 3:
                            s_pe4.inc(_mi, 1)
                    return f
                return [mkg(tt) for tt in range(4)]
            tq, o8 = it[1], it[2]
            ui = 8 * tq + o8

            def mkp(p):
                def f(e):
                    if p == 0:
                        if o8 == 0:
                            e.wait_ge(s_dve4.h, dve4_g[tq])
                        else:
                            e.wait_ge(s_dve4.h, dve4_e[(ui - 1, 3)])
                    q = 8 * ui + p
                    e.wait_ge(p4w[q % 2].h, 16 * (q // 2 + 1))
                    for cc in range(8):
                        c = 8 * p + cc
                        for tt in range(4):
                            _mi = e.matmul(pQ[tt][:, :], ygr_t(c, tt),
                                           wosB[q % 2][:, cc * 512:(cc + 1) * 512],
                                           start=(c == 0), stop=(c == 63),
                                           skip_group_check=True)
                    s_pe4.inc(_mi, 1)
                return f
            return [mkp(p) for p in range(8)]

        @blk.tensor
        def _(e):
            for k in range(NCH):
                early, late = [], []
                for idx, it in enumerate(host[k]):
                    if it[0] == "g" or idx == 0:
                        early += p4_pe_pieces(it)
                    else:
                        late.append(it)
                ei = 0
                for g in range(8):
                    if ei < len(early):
                        early[ei](e)
                        ei += 1
                    kk, gg = (k, g - 1) if g >= 1 else (k - 1, 7)
                    if kk >= 0:
                        e.wait_ge(s_dve.h, dve3(kk, 1 + 3 * gg))
                    e.wait_ge(eas.h, 32 * (k + 1))
                    for hh in range(8):
                        h = g * 8 + hh
                        r = 32 * (h // 32)
                        _mi = e.matmul(pB2[:, ts(hh)], ones33[r:r + 1, :],
                                       expabsk[r:r + 1, (h % 32) * 128:(h % 32) * 128 + 128],
                                       start=(hh % 4 == 0), stop=(hh % 4 == 3),
                                       skip_group_check=True)
                    s_pe.inc(_mi, 1)
                    if ei < len(early):
                        early[ei](e)
                        ei += 1
                    e.wait_ge(s_dve.h, dve3(k, 1 + 3 * g))
                    kk, gg = (k, g - 1) if g >= 1 else (k - 1, 7)
                    if kk >= 0:
                        e.wait_ge(s_dve.h, dve3(kk, 2 + 3 * gg))
                    for hh in range(8):
                        h = g * 8 + hh
                        _mi = e.matmul(pD[:, ts(hh)], seed[:, ts(h)], cdaB[g % 2][:, ts(hh)],
                                       start=(hh % 4 == 0), stop=(hh % 4 == 3),
                                       skip_group_check=True)
                    s_pe.inc(_mi, 1)
                while ei < len(early):
                    early[ei](e)
                    ei += 1
                for it in late:
                    for f in p4_pe_pieces(it):
                        f(e)
            for it in tail34:
                for f in p4_pe_pieces(it):
                    f(e)

        def p4_dve(e, it):
            if it[0] == "g":
                tq = it[1]
                e.wait_ge(s_pe4.h, pe4_g[tq])
                if tq >= 1:
                    e.wait_ge(s_act4.h, act4_g[tq - 1])
                e.memset(rsq[:, :], 0.0)
                e.drain()
                for tt in range(4):
                    e.tensor_tensor(ycB[0][:, 0:128], pQ[tt][:, 0:128], identf[:], ALU.mult)
                    e.drain()
                    e.tensor_scalar(ycB[0][:, 128:256], ycB[0][:, 0:128], 1.0, None,
                                    ALU.mult, ALU.add, accum_out=rsq[:, tt:tt + 1])
                    e.drain()
                s_dve4.inc(e.nop(), 1)
            else:
                tq, o8 = it[1], it[2]
                ui = 8 * tq + o8
                e.wait_ge(s_pe4.h, pe4_u[(ui, 7)])
                e.wait_ge(s_act4.h, act4_g[tq])
                for tt in range(4):
                    m = 4 * ui + tt
                    tg = tq * 4 + tt
                    e.wait_ge(p4x1[m % 4].h, 16 * (m // 4 + 1))
                    if m >= 4:
                        e.wait_ge(p4st[m % 4].h, 16 * (m // 4))
                    e.scalar_tensor_tensor(x2b[m % 4][:, :], pQ[tt][:, :],
                                           rstd3p[:, tg:tg + 1], x1b[m % 4][:, :],
                                           ALU.mult, ALU.add)
                    e.drain()
                    s_dve4.inc(e.nop(), 1)

        @blk.vector
        def _(e):
            for k in range(NCH):
                for g in range(8):
                    e.wait_ge(s_pe.h, pe3(k, 2 * g + 1))
                    kk, gg = prev2b(k, g)
                    if kk >= 0:
                        e.wait_ge(s_pe.h, pe3(kk, 2 * gg + 2))
                    ct_rep = Ct_r[:, ts(k, L)].unsqueeze(1).broadcast_to([128, 8, 128])
                    e.tensor_tensor(cdaB[g % 2][:, :].rearrange("p (a b) -> p a b", a=8),
                                    pB2[:, :].rearrange("p (a b) -> p a b", a=8),
                                    ct_rep, ALU.mult)
                    e.drain()
                    s_dve.inc(e.nop(), 1)
                    e.wait_ge(s_pe.h, pe3(k, 2 * g + 2))
                    e.wait_ge(yls.h, 16 * 8 * (k + 1))
                    e.tensor_tensor(ycB[g % 2][:, :], yslab[:, ts(g, 1024)], pD[:, :], ALU.add)
                    e.drain()
                    s_dve.inc(e.nop(), 1)
                    e.wait_ge(s_act.h, act3(k))
                    v = 16 * (4 * k + g // 2)
                    if v > 0:
                        e.wait_ge(ygst[g % 2].h, v)
                    e.tensor_tensor(yg2[g % 2][:, :], ycB[g % 2][:, :],
                                    silz[:, ts(g, 1024)], ALU.mult)
                    e.drain()
                    s_dve.inc(e.nop(), 1)
                for it in host[k]:
                    p4_dve(e, it)
            for it in tail34:
                p4_dve(e, it)

        def p4_act(e, it):
            if it[0] == "g":
                tq = it[1]
                e.wait_ge(s_dve4.h, dve4_g[tq])
                e.activation(rln[:, 0:4], rsq[:, 0:4], AF.Ln, scale=1.0 / DSSM,
                             bias=epscol[:, 0:1])
                e.drain()
                e.activation(rstd3p[:, 4 * tq:4 * tq + 4], rln[:, 0:4], AF.Exp, scale=-0.5)
                e.drain()
                s_act4.inc(e.nop(), 1)
            else:
                tq, o8 = it[1], it[2]
                ui = 8 * tq + o8
                for tt in range(4):
                    m = 4 * ui + tt
                    tg = tq * 4 + tt
                    if m >= 4:
                        e.wait_ge(s_dve4.h, dve4_e[((m - 4) // 4, (m - 4) % 4)])
                    p4x1[m % 4].inc(e.dma_start(
                        out=x1b[m % 4][:, :],
                        in_=x1_s[3 + tg * 128:3 + tg * 128 + 128, ts(o8, 512)]), 16)

        @blk.scalar
        def _(e):
            for k in range(NCH):
                e.wait_ge(zls.h, 16 * 8 * (k + 1))
                if k >= 1:
                    e.wait_ge(s_dve.h, dve3(k - 1, 24))
                e.activation(silz[:], zslab[:], AF.Silu)
                e.drain()
                s_act.inc(e.nop(), 1)
                for it in host[k]:
                    p4_act(e, it)
            for it in tail34:
                p4_act(e, it)
    phase_barrier()
    ssd_es.close()
    mid_es.close()


    emit_ln("l3", None, x2_s, T_MAIN, ln3_s)
    phase_barrier()
    emit_ffn("f2", ln3_s, T_MAIN, None, x2_s, out_t)
    phase_barrier()

    @blk.sync
    def _(e):
        for name, dap in dbg.items():
            s_ld.inc(e.dma_start(out=dap, in_=dbg_map[name]), 16)
        e.wait_ge(s_ld.h, s_ld.n)
        e.wait_ge(s_ldg.h, s_ldg.n)
        e.wait_ge(s_tr.h, s_tr.n)

    es.close()
    return nc, dbg_map


def _aux_inputs():
    triu_np = np.triu(np.ones((128, 128), np.float32)).astype(ml_dtypes.bfloat16)
    ident_np = np.eye(128, dtype=np.float32)
    return triu_np, ident_np.astype(ml_dtypes.bfloat16), ident_np


def make_in_maps(inputs, T_MAIN=2048):
    x = np.ascontiguousarray(np.asarray(inputs["L0_attn_hidden_states"], np.float32))
    W_in = np.ascontiguousarray(np.asarray(inputs["L0_attn_W_in"], np.float32))
    W_out = np.ascontiguousarray(np.asarray(inputs["L0_attn_W_out"], np.float32))
    conv_w = np.ascontiguousarray(np.asarray(inputs["L0_attn_conv_w"], np.float32))
    dt_bias = np.asarray(inputs["L0_attn_dt_bias"], np.float32)
    A_log = np.asarray(inputs["L0_attn_A_log"], np.float32)
    D_param = np.asarray(inputs["L0_attn_D_param"], np.float32)
    nw = np.asarray(inputs["L0_attn_norm_weight"], np.float32).reshape(DSSM, 1)
    wp = np.asarray(inputs["L0_ffn_weight_packed"], np.int32)
    sce = np.asarray(inputs["L0_ffn_scales"])
    pT = np.ascontiguousarray(wp.view(np.uint8)[:, 0::4].T)
    scT = np.ascontiguousarray(np.repeat(
        np.asarray(sce).view(np.uint16).T, 16, axis=0)).view(np.float16)
    hvecs = np.stack([dt_bias, A_log, D_param, np.zeros_like(D_param)], axis=1)
    hvecs = np.ascontiguousarray(hvecs.astype(np.float32))
    triu_np, ident_b, ident_f = _aux_inputs()

    Bsz, S, _ = x.shape
    nhalf = S // T_MAIN
    in_maps = []
    for core in range(8):
        b, half = core // nhalf if nhalf > 1 else core // 2, core % nhalf if nhalf > 1 else core % 2
        b, half = divmod(core, nhalf) if nhalf > 1 else (core // 2, core % 2)
        xm = np.ascontiguousarray(x[b, half * T_MAIN:(half + 1) * T_MAIN])
        if half == 0:
            xp = np.zeros((3, H), np.float32)
        else:
            xp = np.ascontiguousarray(x[b, half * T_MAIN - 3:half * T_MAIN])
        flags = np.array([[1.0 if half == 0 else 0.0,
                           0.0 if half == 0 else 1.0]], np.float32)
        in_maps.append({
            "x_main": xm, "x_pre": xp, "W_in": W_in, "W_out": W_out,
            "packedT": pT, "scalesE": scT, "conv_win": conv_w, "hvecs": hvecs,
            "nw_in": np.ascontiguousarray(nw), "flags_in": flags,
            "triu_in": triu_np, "ident_in": ident_b, "identf_in": ident_f,
        })
    return in_maps


_BUILT = {}


def kernel(**inputs):
    T_MAIN = 2048
    key = T_MAIN
    if key not in _BUILT:
        _BUILT[key] = build_program(T_MAIN)
    nc, _ = _BUILT[key]
    in_maps = make_in_maps(inputs, T_MAIN)
    res = run_bass_kernel_spmd(nc, in_maps, core_ids=list(range(8)))
    x = np.asarray(inputs["L0_attn_hidden_states"])
    Bsz, S, _ = x.shape
    out = np.empty((Bsz, S, H), np.float32)
    for core in range(8):
        b, half = divmod(core, S // T_MAIN)
        out[b, half * T_MAIN:(half + 1) * T_MAIN] = res.results[core]["out"]
    return out
